# version 07
# baseline (speedup 1.0000x reference)
import numpy as np
from contextlib import ExitStack

import concourse.bass as bass
import concourse.mybir as mybir
from concourse.bass_utils import run_bass_kernel_spmd
from concourse import library_config

F32 = mybir.dt.float32
BF16 = mybir.dt.bfloat16
AF = mybir.ActivationFunctionType
ALU = mybir.AluOpType

B = 8
L = 2048
DM = 1024
DI = 2048
DS = 16
RK = 64
TC = 256
NCH = L // TC
NDB = DI // 128
POS = L // 8
G3 = 3 * DM
EPS = 1e-5
NCORE = 8
MEGA = NDB * TC
XRM = NDB * 260


class Plan:
    def __init__(self):
        self.c = {}

    def inc(self, name, amt):
        self.c[name] = self.c.get(name, 0) + amt
        return self.c[name]

    def val(self, name):
        return self.c.get(name, 0)


DEBUG = False


def build_nc():
    nc = bass.Bass("TRN2", target_bir_lowering=False, debug=False)

    def din(name, shape):
        return nc.dram_tensor(name, shape, F32, kind="ExternalInput").ap()

    xT = din("xT", [DM, L])
    xn = din("xn", [L, DM])
    winT = din("winT", [DM, 2 * DI])
    wxT = din("wxT", [DI, RK + 2 * DS])
    wdtT = din("wdtT", [RK, DI])
    woutT = din("woutT", [DI, DM])
    convw = din("convw", [128, 64])
    convb = din("convb", [128, 16])
    bdt = din("bdt", [128, 16])
    alog = din("alog", [1, DS])
    dvec = din("dvec", [128, 16])
    wgT = din("wgT", [DM, G3])
    ugT = din("ugT", [DM, G3])
    woT = din("woT", [DM, DM])
    uoT = din("uoT", [DM, DM])
    bg = din("bg", [128, 24])
    bo = din("bo", [1, DM])
    incv = din("incv", [128, 8])
    eye = din("eye", [128, 128])

    out = nc.dram_tensor("out", [B, POS, DM], F32, kind="ExternalOutput").ap()
    dbg = {}
    if DEBUG:
        for nm, shp in [("d_xr", [128, XRM]), ("d_xs", [128, MEGA]),
                        ("d_mdt", [128, MEGA]), ("d_xdbl", [96, TC]),
                        ("d_w0", [128, MEGA]), ("d_a0", [128, MEGA]),
                        ("d_h0", [128, MEGA]), ("d_yacc", [128, MEGA]),
                        ("d_ln1", [128, 2 * DM]), ("d_ag0", [8 * TC, DM]),
                        ("d_c0", [128, 2048]), ("d_hf0", [128, 2048]),
                        ("d_acol", [128, 16]), ("d_arw", [1, 16])]:
            dbg[nm] = nc.dram_tensor(nm, shp, F32, kind="ExternalOutput").ap()
    agin = nc.dram_tensor("agin", [L, DM], F32)
    agall = nc.dram_tensor("agall", [NCH, 8 * TC, DM], F32)

    es = ExitStack()

    AR_COLS = 47552
    AR = es.enter_context(nc.sbuf_tensor("AR", [128, AR_COLS], F32))
    _ptr = [0]

    def areg(n):
        o = _ptr[0]
        _ptr[0] += n
        assert _ptr[0] <= AR_COLS, f"arena overflow {_ptr[0]}"
        return o

    o_XR = areg(XRM)
    o_U = areg(MEGA)
    o_A1 = areg(MEGA)
    o_A2 = areg(MEGA)
    o_A4 = areg(MEGA)
    o_A5 = areg(MEGA)
    o_A8 = areg(MEGA)
    o_SZ = areg(MEGA // 2)
    o_B1 = areg(MEGA // 2)
    o_XTB = areg(2 * 1024)
    o_XN = areg(2 * DM)
    o_R = areg(2 * DM)
    o_WSL = areg(2 * 2048)
    o_WOS = areg(2 * 512)
    o_WXT = areg(768)
    o_WDT = areg(1024)
    o_XD = areg(256)
    o_XDB = areg(256)
    o_RP = areg(2 * TC)
    o_ST = areg(256)
    o_CL = areg(160)
    o_EYE = areg(128)
    o_MISC = areg(64)
    mamba_top = _ptr[0]

    _fp = [0]

    def freg(n):
        o = _fp[0]
        _fp[0] += n
        assert _fp[0] <= mamba_top, f"ffn overflow {_fp[0]} > {mamba_top}"
        return o

    f_WG = freg(12288)
    f_UG = freg(12288)
    f_WO = freg(4096)
    f_XTT = freg(2 * 1024)
    f_C = freg(2048)
    f_CB = freg(1024)
    f_F = freg(2048)
    f_I = freg(2048)
    f_G = freg(2048)
    f_X32 = freg(2048)
    f_H = freg(2048)
    f_UO = f_I

    BGC = es.enter_context(nc.sbuf_tensor("BGC", [128, 24], F32))
    INC = es.enter_context(nc.sbuf_tensor("INC", [128, 8], F32))
    BOR = es.enter_context(nc.sbuf_tensor("BOR", [1, DM], F32))
    ONE = es.enter_context(nc.sbuf_tensor("ONE", [1, 128], F32))
    ALG = es.enter_context(nc.sbuf_tensor("ALG", [1, DS], F32))
    ARW = es.enter_context(nc.sbuf_tensor("ARW", [1, DS], F32))

    PB = [es.enter_context(nc.psum_tensor(f"PB{i}", [128, 512], F32))
          for i in range(8)]

    s_d = es.enter_context(nc.semaphore("s_d"))
    s_k = es.enter_context(nc.semaphore("s_k"))
    s_o = es.enter_context(nc.semaphore("s_o"))
    s_b = [es.enter_context(nc.semaphore("s_b0")),
           es.enter_context(nc.semaphore("s_b1"))]
    s_g = es.enter_context(nc.semaphore("s_g"))
    s_gx = [es.enter_context(nc.semaphore("s_gx0")),
            es.enter_context(nc.semaphore("s_gx1"))]
    s_gw = [es.enter_context(nc.semaphore("s_gw0")),
            es.enter_context(nc.semaphore("s_gw1"))]
    s_go = [es.enter_context(nc.semaphore("s_go0")),
            es.enter_context(nc.semaphore("s_go1"))]
    s_t = es.enter_context(nc.semaphore("s_t"))
    s_v = es.enter_context(nc.semaphore("s_v"))
    s_s = es.enter_context(nc.semaphore("s_s"))
    s_p = es.enter_context(nc.semaphore("s_p"))
    s_c = es.enter_context(nc.semaphore("s_c"))
    sem_map = {"s_v": s_v, "s_s": s_s, "s_d": s_d, "s_g": s_g,
               "s_t": s_t, "s_p": s_p, "s_c": s_c,
               "s_k": s_k, "s_o": s_o, "s_b0": s_b[0], "s_b1": s_b[1],
               "s_gx0": s_gx[0], "s_gx1": s_gx[1],
               "s_gw0": s_gw[0], "s_gw1": s_gw[1],
               "s_go0": s_go[0], "s_go1": s_go[1]}

    block = es.enter_context(nc.Block())
    P = Plan()
    prog = {e: [] for e in ("sync", "gpsimd", "vector", "scalar", "tensor")}

    def em(eng, fn):
        prog[eng].append(fn)

    def wait(eng, sem, v):
        if v > 0:
            em(eng, lambda e, sem=sem, v=v: e.wait_ge(sem, v))

    def A(o, n, p0=0, pn=128):
        return AR[p0:p0 + pn, o:o + n]

    def Ab(o, nbf, p0=0, pn=128):
        return AR[p0:p0 + pn, :].bitcast(BF16)[:, 2 * o:2 * o + nbf]

    def sdma(src_ap, dst_ap, semn="s_d"):
        sem = sem_map[semn]
        em("sync", lambda e, s=src_ap, d=dst_ap, m=sem:
           e.dma_start(out=d, in_=s).then_inc(m, 16))
        return P.inc(semn, 16)

    def gdma(src_ap, dst_ap, semn="s_g"):
        sem = sem_map[semn]
        em("gpsimd", lambda g, s=src_ap, d=dst_ap, m=sem:
           g.dma_start(out=d, in_=s).then_inc(m, 16))
        return P.inc(semn, 16)

    def emv(fn):
        wait("vector", s_v, P.val("s_v"))
        em("vector", fn)
        return P.inc("s_v", 1)

    def ems(fn):
        wait("scalar", s_s, P.val("s_s"))
        em("scalar", fn)
        return P.inc("s_s", 1)

    def emp(fn):
        wait("gpsimd", s_p, P.val("s_p"))
        em("gpsimd", fn)
        return P.inc("s_p", 1)

    def tmm_group(mms, bank_ap=None):
        for fn in mms[:-1]:
            em("tensor", fn)
        em("tensor", lambda t, f=mms[-1]: f(t).then_inc(s_t, 1))
        return P.inc("s_t", 1)

    def dump(nm, src_ap, waits=()):
        if not DEBUG or nm not in dbg:
            return
        for eng_sem, v in waits:
            wait("sync", sem_map[eng_sem], v)
        sdma(src_ap, dbg[nm], "s_o")

    bank_free = [("s_v", 0)] * 8

    def t_wait_bank(bi):
        sem, v = bank_free[bi]
        wait("tensor", sem_map[sem], v)


    o_cw = o_CL
    o_cb = o_CL + 64
    o_bd = o_CL + 80
    o_dv = o_CL + 96
    o_ac = o_CL + 112
    o_nb = o_CL + 128

    sdma(convw, A(o_cw, 64), "s_k")
    sdma(convb, A(o_cb, 16), "s_k")
    sdma(bdt, A(o_bd, 16), "s_k")
    sdma(dvec, A(o_dv, 16), "s_k")
    sdma(alog, ALG[:, :], "s_k")
    sdma(eye, A(o_EYE, 128), "s_k")
    sdma(bg, BGC[:, :], "s_k")
    sdma(incv, INC[:, :], "s_k")
    d_cst = sdma(bo, BOR[:, :], "s_k")

    wait("vector", s_k, d_cst)
    emv(lambda v: v.tensor_scalar_mul(A(o_nb, 16), A(o_bd, 16), -1.0).then_inc(s_v, 1))
    v_ones = emv(lambda v: v.memset(ONE[:, :], 1.0).then_inc(s_v, 1))
    emv(lambda v: v.memset(A(o_ST, 256), 0.0).then_inc(s_v, 1))
    emv(lambda v: v.memset(A(o_RP, 128), 1.0).then_inc(s_v, 1))
    emv(lambda v: v.memset(A(o_MISC + 48, 1), EPS).then_inc(s_v, 1))
    emv(lambda v: v.memset(
        A(o_XR, XRM).rearrange("p (blk w) -> p blk w", w=260)[:, :, 0:3], 0.0)
        .then_inc(s_v, 1))

    wait("scalar", s_k, d_cst)
    v_arow = ems(lambda s: s.activation(ARW[:, :], ALG[:, :], AF.Exp).then_inc(s_s, 1))
    wait("tensor", s_s, v_arow)
    wait("tensor", s_v, v_ones)
    t_ac = tmm_group([lambda t: t.matmul(PB[0][:, 0:DS], ONE[:, 0:128],
                                         ARW[:, :], start=True, stop=True)])
    wait("vector", s_t, t_ac)
    p_acol_v = emv(lambda v: v.tensor_copy(A(o_ac, 16), PB[0][:, 0:DS])
                   .then_inc(s_v, 1))
    bank_free[0] = ("s_v", p_acol_v)
    dump("d_acol", A(o_ac, 16), [("s_v", p_acol_v)])
    dump("d_arw", ARW[:, :], [("s_s", v_arow)])

    winT_r = winT.rearrange("(blk p) o -> p blk o", p=128)
    woutT_r = woutT.rearrange("(blk p) o -> p blk o", p=128)
    xT_r = xT.rearrange("(blk p) t -> p blk t", p=128)

    g_xtb = [0, 0]
    g_xtb[0] = gdma(xT_r[:, :, 0:TC],
                    Ab(o_XTB, 2048).rearrange("p (blk t) -> p blk t", t=TC),
                    "s_gx0")

    g_wx = gdma(wxT.rearrange("(blk p) o -> p blk o", p=128),
                Ab(o_WXT, 1536).rearrange("p (blk o) -> p blk o", o=96))
    g_wdt = gdma(wdtT, Ab(o_WDT, 2048, 0, 64))

    v_ln1_done = 0
    v_yz_prev = 0
    t_wsl_done = {}
    t_wos_done = {}
    t_ylast = 0
    v_acc_done = {}
    v_scan_done = {}
    v_scan_half = {}
    p_w_hist = {}
    v_chat_hist = {}
    s_a_done = {}
    d_agin_prev = 0
    cc_round = [0] * NCH

    for c in range(NCH):
        tc0 = c * TC

        wait("sync", s_v, v_ln1_done)
        if d_agin_prev:
            wait("sync", s_d, d_agin_prev)
        d_xn = sdma(xn[tc0:tc0 + TC, :].rearrange("(tb p) m -> p tb m", p=128),
                    A(o_XN, 2 * DM).rearrange("p (tb m) -> p tb m", tb=2))

        wait("scalar", s_v, v_yz_prev)
        s_sz_done = [0] * NDB
        for sl in range(8):
            buf = sl % 2
            tprev = t_wsl_done.get((c, sl - 2),
                                   t_wsl_done.get((c - 1, sl + 6), 0))
            wait("gpsimd", s_t, tprev)
            g_sl = gdma(winT_r[:, :, sl * 512:(sl + 1) * 512],
                        Ab(o_WSL + buf * 2048, 4096)
                        .rearrange("p (blk o) -> p blk o", o=512),
                        "s_gw" + str(buf))
            wait("tensor", s_gw[buf], g_sl)
            wait("tensor", s_gx[c % 2], g_xtb[c % 2])
            for db in range(4):
                dout = sl * 4 + db
                bi = dout % 4
                t_wait_bank(bi)
                mms = []
                for mb in range(8):
                    lhsT = Ab(o_WSL + buf * 2048, 4096)[
                        :, mb * 512 + db * 128: mb * 512 + (db + 1) * 128]
                    rhs = Ab(o_XTB + (c % 2) * 1024, 2048)[:, mb * TC:(mb + 1) * TC]
                    mms.append(lambda t, o=PB[bi][:, 0:TC], l=lhsT, r=rhs,
                               st=(mb == 0), sp=(mb == 7):
                               t.matmul(o, l, r, start=st, stop=sp))
                t_mm = tmm_group(mms)
                if dout < 16:
                    wait("vector", s_t, t_mm)
                    v_ev = emv(lambda v, ps=PB[bi][:, 0:TC],
                               d=A(o_XR + dout * 260 + 3, TC):
                               v.tensor_copy(d, ps).then_inc(s_v, 1))
                    bank_free[bi] = ("s_v", v_ev)
                else:
                    zb = dout - 16
                    wait("scalar", s_t, t_mm)
                    s_sz_done[zb] = ems(
                        lambda s, ps=PB[bi][:, 0:TC],
                        d=Ab(o_SZ, MEGA)[:, zb * TC:(zb + 1) * TC]:
                        s.activation(d, ps, AF.Silu).then_inc(s_s, 1))
                    bank_free[bi] = ("s_s", s_sz_done[zb])
            t_wsl_done[(c, sl)] = P.val("s_t")

        if c == 0:
            dump("d_xr", A(o_XR, XRM), [("s_v", P.val("s_v"))])
        for db in range(NDB):
            slot = o_XR + db * 260
            xcd = A(o_A4 + db * TC, TC)
            emv(lambda v, d=xcd, s=A(slot, TC), w=A(o_cw + db * 4, 1),
                b=A(o_cb + db, 1):
                v.tensor_scalar(d, s, w, b, ALU.mult, ALU.add).then_inc(s_v, 1))
            for j in range(1, 4):
                emv(lambda v, d=xcd, s=A(slot + j, TC), w=A(o_cw + db * 4 + j, 1):
                    v.scalar_tensor_tensor(d, s, w, d, ALU.mult, ALU.add)
                    .then_inc(s_v, 1))
        xr_r0 = A(o_XR, XRM).rearrange("p (blk w) -> p blk w", w=260)
        emv(lambda v, s=xr_r0[:, :, 256:259], d=A(o_RP + 128, 48)
            .rearrange("p (blk w) -> p blk w", w=3):
            v.tensor_copy(d, s).then_inc(s_v, 1))
        v_conv = P.val("s_v")

        wait("scalar", s_v, v_conv)
        if (c - 1, DS - 1) in v_acc_done:
            wait("scalar", s_p, v_acc_done[(c - 1, DS - 1)][1])
        s_xs = ems(lambda s: s.activation(A(o_A2, MEGA), A(o_A4, MEGA), AF.Silu)
                   .then_inc(s_s, 1))
        if c == 0:
            dump("d_xs", A(o_A2, MEGA), [("s_s", s_xs)])
        wait("gpsimd", s_s, s_xs)
        if t_ylast:
            wait("gpsimd", s_t, t_ylast)
        p_xsb = emp(lambda g: g.tensor_copy(Ab(o_B1, MEGA), A(o_A2, MEGA))
                    .then_inc(s_p, 1))

        wait("tensor", s_p, p_xsb)
        if c == 0:
            wait("tensor", s_g, g_wdt)
        t_wait_bank(4)
        mms = []
        for dib in range(NDB):
            lhsT = Ab(o_WXT, 1536)[:, dib * 96:(dib + 1) * 96]
            rhs = Ab(o_B1, MEGA)[:, dib * TC:(dib + 1) * TC]
            mms.append(lambda t, l=lhsT, r=rhs, st=(dib == 0), sp=(dib == NDB - 1):
                       t.matmul(PB[4][0:96, 0:TC], l, r, start=st, stop=sp))
        t_wx = tmm_group(mms)

        wait("vector", s_t, t_wx)
        wait("vector", s_b[0], P.val("s_b0"))
        wait("vector", s_b[1], P.val("s_b1"))
        v_xd = emv(lambda v: v.tensor_copy(A(o_XD, TC, 0, 96), PB[4][0:96, 0:TC])
                   .then_inc(s_v, 1))
        bank_free[4] = ("s_v", v_xd)
        wait("scalar", s_v, v_xd)
        s_xdb = ems(lambda s: s.copy(Ab(o_XDB, 256, 0, 64), A(o_XD, TC, 0, 64))
                    .then_inc(s_s, 1))

        wait("tensor", s_s, s_xdb)
        if c == 0:
            wait("tensor", s_g, g_wdt)
        for db in range(NDB):
            bi = 4 + (db % 2)
            t_wait_bank(bi)
            mms = [lambda t, b=bi,
                   l=Ab(o_WDT, 2048, 0, 64)[:, db * 128:(db + 1) * 128],
                   r=Ab(o_XDB, 256, 0, 64):
                   t.matmul(PB[b][:, 0:TC], l, r, start=True, stop=True)]
            t_dt = tmm_group(mms)
            wait("scalar", s_t, t_dt)
            if db == 0 and v_acc_done.get((c - 1, DS - 1)):
                wait("scalar", s_p, v_acc_done[(c - 1, DS - 1)][1])
            s_sig = ems(lambda s, b=bi, d=A(o_A1 + db * TC, TC),
                        nb=A(o_nb + db, 1):
                        s.activation(d, PB[b][:, 0:TC], AF.Sigmoid,
                                     bias=nb, scale=-1.0).then_inc(s_s, 1))
            bank_free[bi] = ("s_s", s_sig)

        s_mdt = ems(lambda s: s.activation(A(o_A4, MEGA), A(o_A1, MEGA), AF.Ln)
                    .then_inc(s_s, 1))

        if c == 0:
            dump("d_mdt", A(o_A4, MEGA), [("s_s", s_mdt)])
            dump("d_xdbl", A(o_XD, TC, 0, 96), [("s_v", v_xd)])
        wait("vector", s_s, s_mdt)
        if (c - 1, DS - 1) in v_acc_done:
            wait("vector", s_p, v_acc_done[(c - 1, DS - 1)][1])
        v_u = emv(lambda v: v.scalar_tensor_tensor(
            A(o_U, MEGA), A(o_A4, MEGA), -1.0, A(o_A2, MEGA),
            ALU.mult, ALU.mult).then_inc(s_v, 1))
        for db in range(NDB):
            emv(lambda v, d=A(o_A8 + db * TC, TC), s=A(o_A2 + db * TC, TC),
                w=A(o_dv + db, 1):
                v.tensor_scalar_mul(d, s, w).then_inc(s_v, 1))
        v_yacc0 = P.val("s_v")

        t_rep_hist = {}
        d_bc_hist = {}

        def stage_bc(k):
            bcb2 = o_WDT + (k % 2) * 512
            sdma(A(o_XD, TC, 64 + k, 1), A(bcb2, TC, 64, 1), "s_b" + str(k % 2))
            d_bc_hist[k] = sdma(A(o_XD, TC, 80 + k, 1), A(bcb2 + TC, TC, 64, 1),
                                "s_b" + str(k % 2))

        wait("sync", s_v, v_xd)
        stage_bc(0)
        abuf = [o_A1, o_XR]
        for k in range(DS):
            par = k % 2
            pbB, pbC = PB[2 * par], PB[2 * par + 1]
            if k + 1 < DS:
                wait("sync", s_t, t_rep_hist.get(k - 1, 0))
                stage_bc(k + 1)
            bcb = o_WDT + par * 512
            wait("tensor", s_b[par], d_bc_hist[k])
            if k < 2:
                t_wait_bank(2 * par)
                t_wait_bank(2 * par + 1)
            pwar = v_acc_done.get((c, k - 2), v_acc_done.get((c - 1, DS - 1), 0))
            if pwar:
                wait("tensor", s_p, pwar[1])
            t_rep = tmm_group([
                lambda t, r=A(bcb, TC, 64, 1), o1=A(o_RP, 128, 64, 1), pb=pbB:
                t.matmul(pb[:, 0:TC], o1, r, start=True, stop=True),
                lambda t, r=A(bcb + TC, TC, 64, 1), o1=A(o_RP, 128, 64, 1), pb=pbC:
                t.matmul(pb[:, 0:TC], o1, r, start=True, stop=True)])
            t_rep_hist[k] = t_rep

            wait("scalar", s_t, t_rep)
            pw2 = p_w_hist.get((c, k - 2), p_w_hist.get((c - 1, DS - 1), 0))
            if pw2:
                wait("scalar", s_p, pw2)
            ch2 = v_chat_hist.get((c, k - 2), v_chat_hist.get((c - 1, DS - 1), 0))
            if ch2:
                wait("scalar", s_v, ch2)
            if k < 2 and v_ln1_done:
                wait("scalar", s_v, v_ln1_done)
            o_br = o_R + par * 1024
            o_cr = o_R + par * 1024 + 512
            s_ev1 = ems(lambda s, d=A(o_br, TC), p=pbB[:, 0:TC]:
                        s.copy(d, p).then_inc(s_s, 1))
            s_rep = ems(lambda s, d=A(o_cr, TC), p=pbC[:, 0:TC]:
                        s.copy(d, p).then_inc(s_s, 1))
            bank_free[2 * par] = ("s_s", s_rep)
            bank_free[2 * par + 1] = ("s_s", s_rep)

            aw = v_scan_done.get((c, k - 2),
                                 v_scan_done.get((c - 1, DS - 1), 0))
            if aw:
                wait("scalar", s_v, aw)
            cw = v_acc_done.get((c, k - 2), v_acc_done.get((c - 1, DS - 1), 0))
            if cw:
                wait("scalar", s_p, cw[1])
            if k == 0:
                wait("scalar", s_v, v_u)
            wait("scalar", s_v, p_acol_v)
            s_a = ems(lambda s, sc=A(o_ac + k, 1), d=A(abuf[par], MEGA):
                      s.activation(d, A(o_A4, MEGA), AF.Exp,
                                   scale=sc).then_inc(s_s, 1))
            s_a_done[(c, k)] = s_a

            HM = MEGA // 2
            bb = A(o_br, TC).unsqueeze(1).broadcast_to([128, NDB // 2, TC])
            p_wh = []
            for hf2 in range(2):
                wait("gpsimd", s_s, s_rep)
                if k == 0:
                    wait("gpsimd", s_v, v_u)
                    wait("gpsimd", s_v, v_yacc0)
                    if c == 0:
                        wait("gpsimd", s_s, s_xs)
                sw = v_scan_half.get((c, k - 1, hf2),
                                     v_scan_half.get((c - 1, DS - 1, hf2), 0))
                if sw:
                    wait("gpsimd", s_v, sw)
                uu = A(o_U + hf2 * HM, HM).rearrange("p (blk t) -> p blk t", t=TC)
                ww = A(o_A2 + hf2 * HM, HM).rearrange("p (blk t) -> p blk t", t=TC)
                p_wh.append(emp(lambda g, w=ww, u=uu, b2=bb:
                                g.tensor_tensor(w, u, b2, ALU.mult).then_inc(s_p, 1)))
            p_w_hist[(c, k)] = p_wh[-1]

            wait("vector", s_s, s_a)
            for db in range(NDB):
                if db == 0:
                    wait("vector", s_p, p_wh[0])
                if db == NDB // 2:
                    wait("vector", s_p, p_wh[1])
                emv(lambda v, o=A(o_A5 + db * TC, TC),
                    a=A(abuf[par] + db * TC, TC),
                    w=A(o_A2 + db * TC, TC), i=A(o_ST + k * 16 + db, 1):
                    v.tensor_tensor_scan(o, a, w, i, ALU.mult, ALU.add)
                    .then_inc(s_v, 1))
                if db == NDB // 2 - 1:
                    v_scan_half[(c, k, 0)] = P.val("s_v")
            v_scan_half[(c, k, 1)] = P.val("s_v")
            lastc = A(o_A5, MEGA).rearrange("p (blk t) -> p blk t", t=TC)[:, :, TC - 1:TC]
            v_scan_done[(c, k)] = emv(
                lambda v, lc=lastc, d=A(o_ST + k * 16, 16).unsqueeze(2):
                v.tensor_copy(d, lc).then_inc(s_v, 1))

            cb2 = A(o_cr, TC).unsqueeze(1).broadcast_to([128, NDB, TC])
            hh = A(o_A5, MEGA).rearrange("p (blk t) -> p blk t", t=TC)
            tb2 = A(abuf[par], MEGA).rearrange("p (blk t) -> p blk t", t=TC)
            v_chat = emv(lambda v, o=tb2, h=hh, c2=cb2:
                         v.tensor_tensor(o, h, c2, ALU.mult).then_inc(s_v, 1))
            v_chat_hist[(c, k)] = v_chat
            wait("gpsimd", s_v, v_chat)
            p_acc = emp(lambda g, t2=A(abuf[par], MEGA):
                        g.tensor_tensor(A(o_A8, MEGA), A(o_A8, MEGA),
                                        t2, ALU.add).then_inc(s_p, 1))
            v_acc_done[(c, k)] = ("p", p_acc)

        for bi4 in range(4):
            bank_free[bi4] = ("s_p", v_acc_done[(c, DS - 1)][1])

        if c == 0:
            dump("d_yacc", A(o_A8, MEGA), [("s_v", P.val("s_v"))])
        wait("vector", s_p, v_acc_done[(c, DS - 1)][1])
        wait("vector", s_s, max(s_sz_done))
        wait("vector", s_t, t_wx)
        v_yz = emv(lambda v: v.tensor_tensor(Ab(o_B1, MEGA), A(o_A8, MEGA),
                                             Ab(o_SZ, MEGA), ALU.mult)
                   .then_inc(s_v, 1))
        v_yz_prev = v_yz

        wait("tensor", s_v, v_yz)
        for bi in (4, 5, 6, 7):
            t_wait_bank(bi)
        t_y = {}
        for dib in range(NDB):
            tprev = t_wos_done.get((c, dib - 2),
                                   t_wos_done.get((c - 1, dib + 14), 0))
            wait("gpsimd", s_t, tprev)
            g_wo = gdma(woutT_r[:, dib, :], Ab(o_WOS + (dib % 2) * 512, 1024),
                        "s_go" + str(dib % 2))
            wait("tensor", s_go[dib % 2], g_wo)
            mms = []
            for tb in range(2):
                for hf in range(2):
                    bi = 4 + tb * 2 + hf
                    lhsT = Ab(o_B1, MEGA)[:, dib * TC + tb * 128:
                                          dib * TC + tb * 128 + 128]
                    rhs = Ab(o_WOS + (dib % 2) * 512, 1024)[:, hf * 512:(hf + 1) * 512]
                    mms.append(lambda t, b=bi, l=lhsT, r=rhs, st=(dib == 0),
                               sp=(dib == NDB - 1):
                               t.matmul(PB[b][:, :], l, r, start=st, stop=sp))
            t_wos_done[(c, dib)] = tmm_group(mms)
        t_ylast = P.val("s_t")

        rs_cols = o_MISC
        v_ev_by_bank = {}
        for tb in range(2):
            for hf in range(2):
                bi = 4 + tb * 2 + hf
                wait("vector", s_t, t_ylast)
                wait("vector", s_d, d_xn)
                wait("vector", s_p, v_acc_done[(c, DS - 1)][1])
                v_ev = emv(lambda v, d=A(o_R + tb * DM + hf * 512, 512),
                           p=PB[bi][:, :], x=A(o_XN + tb * DM + hf * 512, 512),
                           rc=A(rs_cols + tb * 2 + hf, 1):
                           v.scalar_tensor_tensor(d, p, 1.0, x, ALU.mult,
                                                  ALU.add, accum_out=rc)
                           .then_inc(s_v, 1))
                bank_free[bi] = ("s_v", v_ev)

        o_stat = o_MISC + 8
        for tb in range(2):
            mcol = A(o_stat + tb * 4 + 0, 1)
            vcol = A(o_stat + tb * 4 + 1, 1)
            rcol = A(o_stat + tb * 4 + 2, 1)
            ncol = A(o_stat + tb * 4 + 3, 1)
            tcol = A(o_MISC + 16, 1)
            emv(lambda v, m=mcol, a=A(rs_cols + tb * 2, 1),
                b2=A(rs_cols + tb * 2 + 1, 1):
                v.tensor_tensor(m, a, b2, ALU.add).then_inc(s_v, 1))
            v_m = emv(lambda v, m=mcol: v.tensor_scalar_mul(m, m, 1.0 / DM)
                      .then_inc(s_v, 1))
            wait("scalar", s_v, v_m)
            s_sq = ems(lambda s, jk=A(o_A5 + tb * DM, DM),
                       r=A(o_R + tb * DM, DM), q=vcol:
                       s.activation(jk, r, AF.Square, scale=1.0 / (DM ** 0.5),
                                    accum_out=q).then_inc(s_s, 1))
            wait("vector", s_s, s_sq)
            emv(lambda v, t2=tcol, m=mcol:
                v.tensor_tensor(t2, m, m, ALU.mult).then_inc(s_v, 1))
            v_var = emv(lambda v, q=vcol, t2=tcol:
                        v.tensor_tensor(q, q, t2, ALU.subtract).then_inc(s_v, 1))
            wait("scalar", s_v, v_var)
            ems(lambda s, q=vcol: s.activation(q, q, AF.Ln, bias=A(o_MISC + 48, 1))
                .then_inc(s_s, 1))
            s_rsc = ems(lambda s, q=vcol, r=rcol:
                        s.activation(r, q, AF.Exp, scale=-0.5).then_inc(s_s, 1))
            wait("vector", s_s, s_rsc)
            emv(lambda v, n=ncol, m=mcol, r=rcol:
                v.tensor_scalar(n, m, r, -1.0, ALU.mult, ALU.mult).then_inc(s_v, 1))
            emv(lambda v, d=A(o_XN + tb * DM, DM), r=A(o_R + tb * DM, DM),
                rc=rcol, n=ncol:
                v.tensor_scalar(d, r, rc, n, ALU.mult, ALU.add).then_inc(s_v, 1))
        v_ln1_done = P.val("s_v")

        wait("sync", s_v, v_ln1_done)
        d_agin_prev = sdma(A(o_XN, 2 * DM).rearrange("p (tb m) -> p tb m", tb=2),
                           agin[tc0:tc0 + TC, :]
                           .rearrange("(tb p) m -> p tb m", p=128))
        wait("gpsimd", s_d, d_agin_prev)
        em("gpsimd", lambda g, c2=c: g.collective_compute(
            "AllGather", ALU.bypass,
            replica_groups=[list(range(NCORE))],
            ins=[agin[c2 * TC:(c2 + 1) * TC, :].opt()],
            outs=[agall[c2, :, :].opt()]).then_inc(s_c, 1))
        cc_round[c] = P.inc("s_c", 1)

        if c == 0:
            dump("d_ln1", A(o_XN, 2 * DM), [("s_v", v_ln1_done)])
            if DEBUG:
                wait("sync", s_c, cc_round[0])
                sdma(agall[0, :, :], dbg["d_ag0"], "s_o")
        if c + 1 < NCH:
            xr_r = A(o_XR, XRM).rearrange("p (blk w) -> p blk w", w=260)
            emv(lambda v, s=A(o_RP + 128, 48)
                .rearrange("p (blk w) -> p blk w", w=3), d=xr_r[:, :, 0:3]:
                v.tensor_copy(d, s).then_inc(s_v, 1))
            wait("gpsimd", s_t, t_wsl_done[(c, 7)])
            g_xtb[(c + 1) % 2] = gdma(
                xT_r[:, :, (c + 1) * TC:(c + 2) * TC],
                Ab(o_XTB + ((c + 1) % 2) * 1024, 2048)
                .rearrange("p (blk t) -> p blk t", t=TC),
                "s_gx" + str((c + 1) % 2))

    v_md = P.val("s_v")
    s_md = P.val("s_s")
    t_md = P.val("s_t")
    wait("gpsimd", s_v, v_md)
    wait("gpsimd", s_s, s_md)
    wait("gpsimd", s_t, t_md)

    g_wg = gdma(wgT.rearrange("(blk p) o -> p blk o", p=128),
                Ab(f_WG, 8 * G3).rearrange("p (blk o) -> p blk o", o=G3))
    g_ug = gdma(ugT.rearrange("(blk p) o -> p blk o", p=128),
                Ab(f_UG, 8 * G3).rearrange("p (blk o) -> p blk o", o=G3))
    g_wo2 = gdma(woT.rearrange("(blk p) o -> p blk o", p=128),
                 Ab(f_WO, 8 * DM).rearrange("p (blk o) -> p blk o", o=DM))

    wait("vector", s_v, v_md)
    for mb in range(8):
        emv(lambda v, d=A(f_C + mb * TC, TC), i=INC[:, mb:mb + 1]:
            v.tensor_scalar(d, A(o_ST, TC), 0.0, i, ALU.mult, ALU.add)
            .then_inc(s_v, 1))
    v_cup = emv(lambda v: v.tensor_copy(Ab(f_CB, 2048), A(f_C, 2048))
                .then_inc(s_v, 1))

    agall_r = agall.rearrange("r (j2 tb p) m -> r j2 p tb m", tb=2, p=128)
    pid_holder = {}

    def get_pid(e):
        if "v" not in pid_holder:
            pid_holder["v"] = e.partition_id()
        return pid_holder["v"]

    wait("sync", s_c, cc_round[NCH - 1])

    def load_xt(j):
        def dof(e, j=j):
            pid = get_pid(e)
            e.dma_start(out=A(f_X32, 2 * DM).rearrange("p (tb m) -> p tb m", tb=2),
                        in_=agall_r[pid, j, :, :, :]).then_inc(s_d, 16)
        em("sync", dof)
        return P.inc("s_d", 16)

    def transpose_xt(j, buf, d_x):
        wait("tensor", s_d, d_x)
        s_last = 0
        for tb in range(2):
            for mb in range(8):
                src = A(f_X32 + tb * DM + mb * 128, 128)
                t_tr = tmm_group([lambda t, s2=src:
                                  t.transpose(PB[2][:, 0:128], s2, A(o_EYE, 128))])
                wait("scalar", s_t, t_tr)
                dstb = Ab(f_XTT + buf * 1024, 2048)[
                    :, mb * TC + tb * 128: mb * TC + tb * 128 + 128]
                s_last = ems(lambda s, d=dstb:
                             s.copy(d, PB[2][:, 0:128]).then_inc(s_s, 1))
                wait("tensor", s_s, s_last)
        return P.val("s_t"), s_last

    t_xp = {}
    s_xp = {}
    d_x32 = {}
    v_cstep = {}
    for j in range(B):
        wait("sync", s_t, t_xp.get(j - 1, 0))
        d_x32[j] = load_xt(j)
        buf = j % 2
        t_xp[j], s_xp[j] = transpose_xt(j, buf, d_x32[j])

        wait("tensor", s_g, g_wo2)
        wait("tensor", s_s, s_xp[j])
        wait("tensor", s_v, v_cup if j == 0 else v_cstep[j - 1])
        s_gate_last = 0
        GB_BANKS = (0, 1, 3, 4)
        for gb in range(24):
            bi = GB_BANKS[gb % 4]
            t_wait_bank(bi)
            mms = []
            for mb in range(8):
                lhsT = Ab(f_WG, 8 * G3)[:, mb * G3 + gb * 128: mb * G3 + (gb + 1) * 128]
                rhs = Ab(f_XTT + buf * 1024, 2048)[:, mb * TC:(mb + 1) * TC]
                mms.append(lambda t, b=bi, l=lhsT, r=rhs, st=(mb == 0):
                           t.matmul(PB[b][:, 0:TC], l, r, start=st, stop=False))
            for mb in range(8):
                lhsT = Ab(f_UG, 8 * G3)[:, mb * G3 + gb * 128: mb * G3 + (gb + 1) * 128]
                rhs = Ab(f_CB, 2048)[:, mb * TC:(mb + 1) * TC]
                mms.append(lambda t, b=bi, l=lhsT, r=rhs, sp=(mb == 7):
                           t.matmul(PB[b][:, 0:TC], l, r, start=False, stop=sp))
            t_g = tmm_group(mms)
            kind = gb // 8
            mb8 = gb % 8
            dst = [f_F, f_I, f_G][kind] + mb8 * TC
            func = AF.Tanh if kind == 2 else AF.Sigmoid
            wait("scalar", s_t, t_g)
            s_gate_last = ems(lambda s, b=bi, d=A(dst, TC),
                              bb2=BGC[:, gb:gb + 1], fn=func:
                              s.activation(d, PB[b][:, 0:TC], fn, bias=bb2)
                              .then_inc(s_s, 1))
            bank_free[bi] = ("s_s", s_gate_last)

        wait("vector", s_s, s_gate_last)
        for mb in range(8):
            fsl = A(f_F + mb * TC, TC)
            isl = A(f_I + mb * TC, TC)
            gsl = A(f_G + mb * TC, TC)
            csl = A(f_C + mb * TC, TC)
            emv(lambda v, f=fsl, c2=csl: v.tensor_tensor(f, f, c2, ALU.mult)
                .then_inc(s_v, 1))
            emv(lambda v, i=isl, g2=gsl: v.tensor_tensor(g2, i, g2, ALU.mult)
                .then_inc(s_v, 1))
            emv(lambda v, c2=csl, f=fsl, g2=gsl:
                v.tensor_tensor(c2, f, g2, ALU.add).then_inc(s_v, 1))
        v_cstep[j] = emv(lambda v: v.tensor_copy(Ab(f_CB, 2048), A(f_C, 2048))
                         .then_inc(s_v, 1))
        if j == 0:
            dump("d_c0", A(f_C, 2048), [("s_v", v_cstep[j])])

    wait("gpsimd", s_s, P.val("s_s"))
    wait("gpsimd", s_v, P.val("s_v"))
    g_uo2 = gdma(uoT.rearrange("(blk p) o -> p blk o", p=128),
                 Ab(f_UO, 8 * DM).rearrange("p (blk o) -> p blk o", o=DM))

    v_cF = v_cstep[B - 1]
    v_ln2 = 0
    d_out_prev = 0
    t_xp_fin_prev = t_xp[B - 1]
    for j in range(B):
        wait("sync", s_t, t_xp_fin_prev)
        if v_ln2:
            wait("sync", s_v, v_ln2)
        d_x = load_xt(j)
        buf = j % 2
        t_xp_fin_prev, s_xpf = transpose_xt(j, buf, d_x)

        wait("tensor", s_g, g_uo2)
        wait("tensor", s_s, s_xpf)
        wait("tensor", s_v, v_cF)
        s_tanh_last = 0
        for pb2 in range(2):
            for hf in range(2):
                bi = 5 + hf
                t_wait_bank(bi)
                mms = []
                for mb in range(8):
                    lhsT = Ab(f_XTT + buf * 1024, 2048)[
                        :, mb * TC + pb2 * 128: mb * TC + pb2 * 128 + 128]
                    rhs = Ab(f_WO, 8 * DM)[:, mb * DM + hf * 512: mb * DM + (hf + 1) * 512]
                    mms.append(lambda t, b=bi, l=lhsT, r=rhs, st=(mb == 0):
                               t.matmul(PB[b][:, :], l, r, start=st, stop=False))
                for mb in range(8):
                    lhsT = Ab(f_CB, 2048)[:, mb * TC + pb2 * 128: mb * TC + pb2 * 128 + 128]
                    rhs = Ab(f_UO, 8 * DM)[:, mb * DM + hf * 512: mb * DM + (hf + 1) * 512]
                    mms.append(lambda t, b=bi, l=lhsT, r=rhs:
                               t.matmul(PB[b][:, :], l, r, start=False, stop=False))
                mms.append(lambda t, b=bi, h=hf:
                           t.matmul(PB[b][:, :], ONE[:, 0:128],
                                    BOR[:, h * 512:(h + 1) * 512],
                                    start=False, stop=True))
                t_h = tmm_group(mms)
                wait("scalar", s_t, t_h)
                if d_out_prev and pb2 == 0 and hf == 0:
                    wait("scalar", s_o, d_out_prev)
                s_tanh_last = ems(lambda s, b=bi,
                                  d=A(f_H + pb2 * DM + hf * 512, 512):
                                  s.activation(d, PB[b][:, :], AF.Tanh)
                                  .then_inc(s_s, 1))
                bank_free[bi] = ("s_s", s_tanh_last)

        if j == 0:
            dump("d_hf0", A(f_H, 2048), [("s_s", s_tanh_last)])
        wait("vector", s_s, s_tanh_last)
        wait("vector", s_d, d_x)
        rs2 = o_MISC + 24
        for tb in range(2):
            for hf in range(2):
                emv(lambda v, d=A(f_H + tb * DM + hf * 512, 512),
                    x=A(f_X32 + tb * DM + hf * 512, 512),
                    rc=A(rs2 + tb * 2 + hf, 1):
                    v.scalar_tensor_tensor(d, d, 1.0, x, ALU.mult, ALU.add,
                                           accum_out=rc).then_inc(s_v, 1))
        o_st2 = o_MISC + 32
        for tb in range(2):
            mcol = A(o_st2 + tb * 4 + 0, 1)
            vcol = A(o_st2 + tb * 4 + 1, 1)
            rcol = A(o_st2 + tb * 4 + 2, 1)
            ncol = A(o_st2 + tb * 4 + 3, 1)
            tcol = A(o_MISC + 40, 1)
            emv(lambda v, m=mcol, a=A(rs2 + tb * 2, 1), b2=A(rs2 + tb * 2 + 1, 1):
                v.tensor_tensor(m, a, b2, ALU.add).then_inc(s_v, 1))
            v_m = emv(lambda v, m=mcol: v.tensor_scalar_mul(m, m, 1.0 / DM)
                      .then_inc(s_v, 1))
            wait("scalar", s_v, v_m)
            s_sq = ems(lambda s, jk=A(f_F + tb * DM, DM), r=A(f_H + tb * DM, DM),
                       q=vcol:
                       s.activation(jk, r, AF.Square, scale=1.0 / (DM ** 0.5),
                                    accum_out=q).then_inc(s_s, 1))
            wait("vector", s_s, s_sq)
            emv(lambda v, t2=tcol, m=mcol: v.tensor_tensor(t2, m, m, ALU.mult)
                .then_inc(s_v, 1))
            v_var = emv(lambda v, q=vcol, t2=tcol:
                        v.tensor_tensor(q, q, t2, ALU.subtract).then_inc(s_v, 1))
            wait("scalar", s_v, v_var)
            ems(lambda s, q=vcol: s.activation(q, q, AF.Ln, bias=A(o_MISC + 48, 1))
                .then_inc(s_s, 1))
            s_rsc = ems(lambda s, q=vcol, r=rcol:
                        s.activation(r, q, AF.Exp, scale=-0.5).then_inc(s_s, 1))
            wait("vector", s_s, s_rsc)
            emv(lambda v, n=ncol, m=mcol, r=rcol:
                v.tensor_scalar(n, m, r, -1.0, ALU.mult, ALU.mult).then_inc(s_v, 1))
            emv(lambda v, d=A(f_F + tb * DM, DM), r=A(f_H + tb * DM, DM),
                rc=rcol, n=ncol:
                v.tensor_scalar(d, r, rc, n, ALU.mult, ALU.add).then_inc(s_v, 1))
        v_ln2 = P.val("s_v")

        wait("sync", s_v, v_ln2)
        d_out_prev = sdma(A(f_F, 2 * DM).rearrange("p (tb m) -> p tb m", tb=2),
                          out[j, :, :].rearrange("(tb p) m -> p tb m", p=128),
                          "s_o")

    wait("sync", s_d, P.val("s_d"))
    wait("sync", s_o, P.val("s_o"))

    @block.gpsimd
    def _(g):
        for fn in prog["gpsimd"]:
            fn(g)

    @block.sync
    def _(e):
        for fn in prog["sync"]:
            fn(e)

    @block.vector
    def _(v):
        for fn in prog["vector"]:
            fn(v)

    @block.scalar
    def _(s):
        for fn in prog["scalar"]:
            fn(s)

    @block.tensor
    def _(t):
        for fn in prog["tensor"]:
            fn(t)

    es.close()
    return nc


_NC_CACHE = {}


def kernel(L0_attn_x, L0_attn_W_in, L0_attn_conv_w, L0_attn_conv_b, L0_attn_A_log,
           L0_attn_D, L0_attn_W_x, L0_attn_W_dt, L0_attn_b_dt, L0_attn_W_out,
           L0_ffn_wf, L0_ffn_uf, L0_ffn_bf, L0_ffn_wi, L0_ffn_ui, L0_ffn_bi,
           L0_ffn_wc, L0_ffn_uc, L0_ffn_bc, L0_ffn_wo, L0_ffn_uo, L0_ffn_bo,
           L0_ffn_in_c):
    x = np.asarray(L0_attn_x, dtype=np.float32)

    def T(a):
        return np.ascontiguousarray(np.asarray(a, dtype=np.float32).T)

    def colv(a):
        v = np.asarray(a, dtype=np.float32).reshape(-1)
        return np.ascontiguousarray(v.reshape(-1, 128).T)

    shared = {
        "winT": T(L0_attn_W_in),
        "wxT": T(L0_attn_W_x),
        "wdtT": T(L0_attn_W_dt),
        "woutT": T(L0_attn_W_out),
        "convw": np.ascontiguousarray(
            np.asarray(L0_attn_conv_w, dtype=np.float32)
            .reshape(16, 128, 4).transpose(1, 0, 2).reshape(128, 64)),
        "convb": colv(L0_attn_conv_b),
        "bdt": colv(L0_attn_b_dt),
        "alog": np.ascontiguousarray(
            np.asarray(L0_attn_A_log, dtype=np.float32).reshape(1, -1)),
        "dvec": colv(L0_attn_D),
        "wgT": np.ascontiguousarray(np.concatenate(
            [np.asarray(L0_ffn_wf), np.asarray(L0_ffn_wi),
             np.asarray(L0_ffn_wc)], axis=0).T.astype(np.float32)),
        "ugT": np.ascontiguousarray(np.concatenate(
            [np.asarray(L0_ffn_uf), np.asarray(L0_ffn_ui),
             np.asarray(L0_ffn_uc)], axis=0).T.astype(np.float32)),
        "woT": T(L0_ffn_wo),
        "uoT": T(L0_ffn_uo),
        "bg": colv(np.concatenate(
            [np.asarray(L0_ffn_bf), np.asarray(L0_ffn_bi),
             np.asarray(L0_ffn_bc)], axis=0)),
        "bo": np.ascontiguousarray(
            np.asarray(L0_ffn_bo, dtype=np.float32).reshape(1, -1)),
        "incv": colv(L0_ffn_in_c),
        "eye": np.eye(128, dtype=np.float32),
    }

    in_maps = []
    for j in range(NCORE):
        m = dict(shared)
        m["xT"] = np.ascontiguousarray(x[j].T)
        m["xn"] = np.ascontiguousarray(x[j])
        in_maps.append(m)

    if "nc" not in _NC_CACHE:
        _NC_CACHE["nc"] = build_nc()
    nc = _NC_CACHE["nc"]

    res = run_bass_kernel_spmd(nc, in_maps, core_ids=list(range(NCORE)))

    full = np.empty((B, L, DM), dtype=np.float32)
    for i in range(NCORE):
        full[:, i * POS:(i + 1) * POS, :] = res.results[i]["out"]
    return full


# version 08
# speedup vs baseline: 1.1890x; 1.1890x over previous
import numpy as np
from contextlib import ExitStack

import concourse.bass as bass
import concourse.mybir as mybir
from concourse.bass_utils import run_bass_kernel_spmd
from concourse import library_config

F32 = mybir.dt.float32
BF16 = mybir.dt.bfloat16
AF = mybir.ActivationFunctionType
ALU = mybir.AluOpType

B = 8
L = 2048
DM = 1024
DI = 2048
DS = 16
RK = 64
TC = 256
NCH = L // TC
NDB = DI // 128
POS = L // 8
G3 = 3 * DM
EPS = 1e-5
NCORE = 8
MEGA = NDB * TC
XRM = NDB * 260


class Plan:
    def __init__(self):
        self.c = {}

    def inc(self, name, amt):
        self.c[name] = self.c.get(name, 0) + amt
        return self.c[name]

    def val(self, name):
        return self.c.get(name, 0)


DEBUG = False


def build_nc():
    nc = bass.Bass("TRN2", target_bir_lowering=False, debug=False)

    def din(name, shape):
        return nc.dram_tensor(name, shape, F32, kind="ExternalInput").ap()

    xT = din("xT", [DM, L])
    xn = din("xn", [L, DM])
    winT = din("winT", [DM, 2 * DI])
    wxT = din("wxT", [DI, RK + 2 * DS])
    wdtT = din("wdtT", [RK, DI])
    woutT = din("woutT", [DI, DM])
    convw = din("convw", [128, 64])
    convb = din("convb", [128, 16])
    bdt = din("bdt", [128, 16])
    alog = din("alog", [1, DS])
    dvec = din("dvec", [128, 16])
    wgT = din("wgT", [DM, G3])
    ugT = din("ugT", [DM, G3])
    woT = din("woT", [DM, DM])
    uoT = din("uoT", [DM, DM])
    bg = din("bg", [128, 24])
    bo = din("bo", [1, DM])
    incv = din("incv", [128, 8])
    eye = din("eye", [128, 128])

    out = nc.dram_tensor("out", [B, POS, DM], F32, kind="ExternalOutput").ap()
    dbg = {}
    if DEBUG:
        for nm, shp in [("d_xr", [128, XRM]), ("d_xs", [128, MEGA]),
                        ("d_mdt", [128, MEGA]), ("d_xdbl", [96, TC]),
                        ("d_w0", [128, MEGA]), ("d_a0", [128, MEGA]),
                        ("d_h0", [128, MEGA]), ("d_yacc", [128, MEGA]),
                        ("d_ln1", [128, 2 * DM]), ("d_ag0", [8 * TC, DM]),
                        ("d_c0", [128, 2048]), ("d_hf0", [128, 2048]),
                        ("d_acol", [128, 16]), ("d_arw", [1, 16])]:
            dbg[nm] = nc.dram_tensor(nm, shp, F32, kind="ExternalOutput").ap()
    agin = nc.dram_tensor("agin", [L, DM], F32)
    agall = nc.dram_tensor("agall", [NCH, 8 * TC, DM], F32)

    es = ExitStack()

    AR_COLS = 47552
    AR = es.enter_context(nc.sbuf_tensor("AR", [128, AR_COLS], F32))
    _ptr = [0]

    def areg(n):
        o = _ptr[0]
        _ptr[0] += n
        assert _ptr[0] <= AR_COLS, f"arena overflow {_ptr[0]}"
        return o

    o_XR = areg(XRM)
    o_U = areg(MEGA)
    o_A1 = areg(MEGA)
    o_A2 = areg(MEGA)
    o_A4 = areg(MEGA)
    o_A5 = areg(MEGA)
    o_A8 = areg(MEGA)
    o_SZ = areg(MEGA // 2)
    o_B1 = areg(MEGA // 2)
    o_XTB = areg(2 * 1024)
    o_XN = areg(2 * DM)
    o_R = areg(2 * DM)
    o_WSL = areg(2 * 2048)
    o_WOS = areg(2 * 512)
    o_WXT = areg(768)
    o_WDT = areg(1024)
    o_XD = areg(256)
    o_XDB = areg(256)
    o_RP = areg(2 * TC)
    o_ST = areg(256)
    o_CL = areg(160)
    o_EYE = areg(128)
    o_MISC = areg(64)
    mamba_top = _ptr[0]

    _fp = [0]

    def freg(n):
        o = _fp[0]
        _fp[0] += n
        assert _fp[0] <= mamba_top, f"ffn overflow {_fp[0]} > {mamba_top}"
        return o

    f_WG = freg(12288)
    f_UG = freg(12288)
    f_WO = freg(4096)
    f_XTT = freg(2 * 1024)
    f_C = freg(2048)
    f_CB = freg(1024)
    f_F = freg(2048)
    f_I = freg(2048)
    f_G = freg(2048)
    f_X32 = freg(2048)
    f_H = freg(2048)
    f_UO = f_I

    BGC = es.enter_context(nc.sbuf_tensor("BGC", [128, 24], F32))
    INC = es.enter_context(nc.sbuf_tensor("INC", [128, 8], F32))
    BOR = es.enter_context(nc.sbuf_tensor("BOR", [1, DM], F32))
    ONE = es.enter_context(nc.sbuf_tensor("ONE", [1, 128], F32))
    ALG = es.enter_context(nc.sbuf_tensor("ALG", [1, DS], F32))
    ARW = es.enter_context(nc.sbuf_tensor("ARW", [1, DS], F32))

    PB = [es.enter_context(nc.psum_tensor(f"PB{i}", [128, 512], F32))
          for i in range(8)]

    s_d = es.enter_context(nc.semaphore("s_d"))
    s_k = es.enter_context(nc.semaphore("s_k"))
    s_o = es.enter_context(nc.semaphore("s_o"))
    s_b = [es.enter_context(nc.semaphore("s_b0")),
           es.enter_context(nc.semaphore("s_b1"))]
    s_g = es.enter_context(nc.semaphore("s_g"))
    s_gx = [es.enter_context(nc.semaphore("s_gx0")),
            es.enter_context(nc.semaphore("s_gx1"))]
    s_gw = [es.enter_context(nc.semaphore("s_gw0")),
            es.enter_context(nc.semaphore("s_gw1"))]
    s_go = [es.enter_context(nc.semaphore("s_go0")),
            es.enter_context(nc.semaphore("s_go1"))]
    s_t = es.enter_context(nc.semaphore("s_t"))
    s_v = es.enter_context(nc.semaphore("s_v"))
    s_s = es.enter_context(nc.semaphore("s_s"))
    s_p = es.enter_context(nc.semaphore("s_p"))
    s_c = es.enter_context(nc.semaphore("s_c"))
    sem_map = {"s_v": s_v, "s_s": s_s, "s_d": s_d, "s_g": s_g,
               "s_t": s_t, "s_p": s_p, "s_c": s_c,
               "s_k": s_k, "s_o": s_o, "s_b0": s_b[0], "s_b1": s_b[1],
               "s_gx0": s_gx[0], "s_gx1": s_gx[1],
               "s_gw0": s_gw[0], "s_gw1": s_gw[1],
               "s_go0": s_go[0], "s_go1": s_go[1]}

    block = es.enter_context(nc.Block())
    P = Plan()
    prog = {e: [] for e in ("sync", "gpsimd", "vector", "scalar", "tensor")}

    def em(eng, fn):
        prog[eng].append(fn)

    def wait(eng, sem, v):
        if v > 0:
            em(eng, lambda e, sem=sem, v=v: e.wait_ge(sem, v))

    def A(o, n, p0=0, pn=128):
        return AR[p0:p0 + pn, o:o + n]

    def Ab(o, nbf, p0=0, pn=128):
        return AR[p0:p0 + pn, :].bitcast(BF16)[:, 2 * o:2 * o + nbf]

    def sdma(src_ap, dst_ap, semn="s_d"):
        sem = sem_map[semn]
        em("sync", lambda e, s=src_ap, d=dst_ap, m=sem:
           e.dma_start(out=d, in_=s).then_inc(m, 16))
        return P.inc(semn, 16)

    def gdma(src_ap, dst_ap, semn="s_g"):
        sem = sem_map[semn]
        em("gpsimd", lambda g, s=src_ap, d=dst_ap, m=sem:
           g.dma_start(out=d, in_=s).then_inc(m, 16))
        return P.inc(semn, 16)

    def emv(fn):
        wait("vector", s_v, P.val("s_v"))
        em("vector", fn)
        return P.inc("s_v", 1)

    def ems(fn):
        wait("scalar", s_s, P.val("s_s"))
        em("scalar", fn)
        return P.inc("s_s", 1)

    def emp(fn):
        wait("gpsimd", s_p, P.val("s_p"))
        em("gpsimd", fn)
        return P.inc("s_p", 1)

    def tmm_group(mms, bank_ap=None):
        for fn in mms[:-1]:
            em("tensor", fn)
        em("tensor", lambda t, f=mms[-1]: f(t).then_inc(s_t, 1))
        return P.inc("s_t", 1)

    def dump(nm, src_ap, waits=()):
        if not DEBUG or nm not in dbg:
            return
        for eng_sem, v in waits:
            wait("sync", sem_map[eng_sem], v)
        sdma(src_ap, dbg[nm], "s_o")

    bank_free = [("s_v", 0)] * 8

    def t_wait_bank(bi):
        sem, v = bank_free[bi]
        wait("tensor", sem_map[sem], v)


    o_cw = o_CL
    o_cb = o_CL + 64
    o_bd = o_CL + 80
    o_dv = o_CL + 96
    o_ac = o_CL + 112
    o_nb = o_CL + 128

    sdma(convw, A(o_cw, 64), "s_k")
    sdma(convb, A(o_cb, 16), "s_k")
    sdma(bdt, A(o_bd, 16), "s_k")
    sdma(dvec, A(o_dv, 16), "s_k")
    sdma(alog, ALG[:, :], "s_k")
    sdma(eye, A(o_EYE, 128), "s_k")
    sdma(bg, BGC[:, :], "s_k")
    sdma(incv, INC[:, :], "s_k")
    d_cst = sdma(bo, BOR[:, :], "s_k")

    wait("vector", s_k, d_cst)
    emv(lambda v: v.tensor_scalar_mul(A(o_nb, 16), A(o_bd, 16), -1.0).then_inc(s_v, 1))
    v_ones = emv(lambda v: v.memset(ONE[:, :], 1.0).then_inc(s_v, 1))
    emv(lambda v: v.memset(A(o_ST, 256), 0.0).then_inc(s_v, 1))
    emv(lambda v: v.memset(A(o_RP, 128), 1.0).then_inc(s_v, 1))
    emv(lambda v: v.memset(A(o_MISC + 48, 1), EPS).then_inc(s_v, 1))
    emv(lambda v: v.memset(
        A(o_XR, XRM).rearrange("p (blk w) -> p blk w", w=260)[:, :, 0:3], 0.0)
        .then_inc(s_v, 1))

    wait("scalar", s_k, d_cst)
    v_arow = ems(lambda s: s.activation(ARW[:, :], ALG[:, :], AF.Exp).then_inc(s_s, 1))
    wait("tensor", s_s, v_arow)
    wait("tensor", s_v, v_ones)
    t_ac = tmm_group([lambda t: t.matmul(PB[0][:, 0:DS], ONE[:, 0:128],
                                         ARW[:, :], start=True, stop=True)])
    wait("vector", s_t, t_ac)
    p_acol_v = emv(lambda v: v.tensor_copy(A(o_ac, 16), PB[0][:, 0:DS])
                   .then_inc(s_v, 1))
    bank_free[0] = ("s_v", p_acol_v)
    dump("d_acol", A(o_ac, 16), [("s_v", p_acol_v)])
    dump("d_arw", ARW[:, :], [("s_s", v_arow)])

    winT_r = winT.rearrange("(blk p) o -> p blk o", p=128)
    woutT_r = woutT.rearrange("(blk p) o -> p blk o", p=128)
    xT_r = xT.rearrange("(blk p) t -> p blk t", p=128)

    g_xtb = [0, 0]
    g_xtb[0] = gdma(xT_r[:, :, 0:TC],
                    Ab(o_XTB, 2048).rearrange("p (blk t) -> p blk t", t=TC),
                    "s_gx0")

    g_wx = gdma(wxT.rearrange("(blk p) o -> p blk o", p=128),
                Ab(o_WXT, 1536).rearrange("p (blk o) -> p blk o", o=96))
    g_wdt = gdma(wdtT, Ab(o_WDT, 2048, 0, 64))

    v_ln1_done = 0
    v_yz_prev = 0
    t_wsl_done = {}
    t_wos_done = {}
    t_ylast = 0
    v_acc_done = {}
    v_scan_done = {}
    v_scan_half = {}
    p_w_hist = {}
    v_chat_hist = {}
    s_a_done = {}
    d_agin_prev = 0
    cc_round = [0] * NCH

    for c in range(NCH):
        tc0 = c * TC

        wait("sync", s_v, v_ln1_done)
        if d_agin_prev:
            wait("sync", s_d, d_agin_prev)
        d_xn = sdma(xn[tc0:tc0 + TC, :].rearrange("(tb p) m -> p tb m", p=128),
                    A(o_XN, 2 * DM).rearrange("p (tb m) -> p tb m", tb=2))

        wait("scalar", s_v, v_yz_prev)
        s_sz_done = [0] * NDB
        for sl in range(8):
            buf = sl % 2
            tprev = t_wsl_done.get((c, sl - 2),
                                   t_wsl_done.get((c - 1, sl + 6), 0))
            wait("gpsimd", s_t, tprev)
            g_sl = gdma(winT_r[:, :, sl * 512:(sl + 1) * 512],
                        Ab(o_WSL + buf * 2048, 4096)
                        .rearrange("p (blk o) -> p blk o", o=512),
                        "s_gw" + str(buf))
            wait("tensor", s_gw[buf], g_sl)
            wait("tensor", s_gx[c % 2], g_xtb[c % 2])
            for db in range(4):
                dout = sl * 4 + db
                bi = dout % 4
                t_wait_bank(bi)
                mms = []
                for mb in range(8):
                    lhsT = Ab(o_WSL + buf * 2048, 4096)[
                        :, mb * 512 + db * 128: mb * 512 + (db + 1) * 128]
                    rhs = Ab(o_XTB + (c % 2) * 1024, 2048)[:, mb * TC:(mb + 1) * TC]
                    mms.append(lambda t, o=PB[bi][:, 0:TC], l=lhsT, r=rhs,
                               st=(mb == 0), sp=(mb == 7):
                               t.matmul(o, l, r, start=st, stop=sp))
                t_mm = tmm_group(mms)
                if dout < 16:
                    wait("vector", s_t, t_mm)
                    v_ev = emv(lambda v, ps=PB[bi][:, 0:TC],
                               d=A(o_XR + dout * 260 + 3, TC):
                               v.tensor_copy(d, ps).then_inc(s_v, 1))
                    bank_free[bi] = ("s_v", v_ev)
                else:
                    zb = dout - 16
                    wait("scalar", s_t, t_mm)
                    s_sz_done[zb] = ems(
                        lambda s, ps=PB[bi][:, 0:TC],
                        d=Ab(o_SZ, MEGA)[:, zb * TC:(zb + 1) * TC]:
                        s.activation(d, ps, AF.Silu).then_inc(s_s, 1))
                    bank_free[bi] = ("s_s", s_sz_done[zb])
            t_wsl_done[(c, sl)] = P.val("s_t")

        if c == 0:
            dump("d_xr", A(o_XR, XRM), [("s_v", P.val("s_v"))])
        for db in range(NDB):
            slot = o_XR + db * 260
            xcd = A(o_A4 + db * TC, TC)
            emv(lambda v, d=xcd, s=A(slot, TC), w=A(o_cw + db * 4, 1),
                b=A(o_cb + db, 1):
                v.tensor_scalar(d, s, w, b, ALU.mult, ALU.add).then_inc(s_v, 1))
            for j in range(1, 4):
                emv(lambda v, d=xcd, s=A(slot + j, TC), w=A(o_cw + db * 4 + j, 1):
                    v.scalar_tensor_tensor(d, s, w, d, ALU.mult, ALU.add)
                    .then_inc(s_v, 1))
        xr_r0 = A(o_XR, XRM).rearrange("p (blk w) -> p blk w", w=260)
        emv(lambda v, s=xr_r0[:, :, 256:259], d=A(o_RP + 128, 48)
            .rearrange("p (blk w) -> p blk w", w=3):
            v.tensor_copy(d, s).then_inc(s_v, 1))
        v_conv = P.val("s_v")

        wait("scalar", s_v, v_conv)
        if (c - 1, DS - 1) in v_acc_done:
            wait("scalar", s_p, v_acc_done[(c - 1, DS - 1)][1])
        s_xs = ems(lambda s: s.activation(A(o_A2, MEGA), A(o_A4, MEGA), AF.Silu)
                   .then_inc(s_s, 1))
        if c == 0:
            dump("d_xs", A(o_A2, MEGA), [("s_s", s_xs)])
        wait("gpsimd", s_s, s_xs)
        if t_ylast:
            wait("gpsimd", s_t, t_ylast)
        p_xsb = emp(lambda g: g.tensor_copy(Ab(o_B1, MEGA), A(o_A2, MEGA))
                    .then_inc(s_p, 1))

        wait("tensor", s_p, p_xsb)
        if c == 0:
            wait("tensor", s_g, g_wdt)
        t_wait_bank(4)
        mms = []
        for dib in range(NDB):
            lhsT = Ab(o_WXT, 1536)[:, dib * 96:(dib + 1) * 96]
            rhs = Ab(o_B1, MEGA)[:, dib * TC:(dib + 1) * TC]
            mms.append(lambda t, l=lhsT, r=rhs, st=(dib == 0), sp=(dib == NDB - 1):
                       t.matmul(PB[4][0:96, 0:TC], l, r, start=st, stop=sp))
        t_wx = tmm_group(mms)

        wait("vector", s_t, t_wx)
        wait("vector", s_b[0], P.val("s_b0"))
        wait("vector", s_b[1], P.val("s_b1"))
        v_xd = emv(lambda v: v.tensor_copy(A(o_XD, TC, 0, 96), PB[4][0:96, 0:TC])
                   .then_inc(s_v, 1))
        bank_free[4] = ("s_v", v_xd)
        wait("scalar", s_v, v_xd)
        s_xdb = ems(lambda s: s.copy(Ab(o_XDB, 256, 0, 64), A(o_XD, TC, 0, 64))
                    .then_inc(s_s, 1))

        wait("tensor", s_s, s_xdb)
        if c == 0:
            wait("tensor", s_g, g_wdt)
        for db in range(NDB):
            bi = 4 + (db % 2)
            t_wait_bank(bi)
            mms = [lambda t, b=bi,
                   l=Ab(o_WDT, 2048, 0, 64)[:, db * 128:(db + 1) * 128],
                   r=Ab(o_XDB, 256, 0, 64):
                   t.matmul(PB[b][:, 0:TC], l, r, start=True, stop=True)]
            t_dt = tmm_group(mms)
            wait("scalar", s_t, t_dt)
            if db == 0 and v_acc_done.get((c - 1, DS - 1)):
                wait("scalar", s_p, v_acc_done[(c - 1, DS - 1)][1])
            s_sig = ems(lambda s, b=bi, d=A(o_A1 + db * TC, TC),
                        nb=A(o_nb + db, 1):
                        s.activation(d, PB[b][:, 0:TC], AF.Sigmoid,
                                     bias=nb, scale=-1.0).then_inc(s_s, 1))
            bank_free[bi] = ("s_s", s_sig)

        s_mdt = ems(lambda s: s.activation(A(o_A4, MEGA), A(o_A1, MEGA), AF.Ln)
                    .then_inc(s_s, 1))

        if c == 0:
            dump("d_mdt", A(o_A4, MEGA), [("s_s", s_mdt)])
            dump("d_xdbl", A(o_XD, TC, 0, 96), [("s_v", v_xd)])
        wait("vector", s_s, s_mdt)
        if (c - 1, DS - 1) in v_acc_done:
            wait("vector", s_p, v_acc_done[(c - 1, DS - 1)][1])
        v_u = emv(lambda v: v.scalar_tensor_tensor(
            A(o_U, MEGA), A(o_A4, MEGA), -1.0, A(o_A2, MEGA),
            ALU.mult, ALU.mult).then_inc(s_v, 1))
        for db in range(NDB):
            emv(lambda v, d=A(o_A8 + db * TC, TC), s=A(o_A2 + db * TC, TC),
                w=A(o_dv + db, 1):
                v.tensor_scalar_mul(d, s, w).then_inc(s_v, 1))
        v_yacc0 = P.val("s_v")

        t_rep_hist = {}
        d_bc_hist = {}

        def stage_bc(k):
            bcb2 = o_WDT + (k % 2) * 512
            sdma(A(o_XD, TC, 64 + k, 1), A(bcb2, TC, 64, 1), "s_b" + str(k % 2))
            d_bc_hist[k] = sdma(A(o_XD, TC, 80 + k, 1), A(bcb2 + TC, TC, 64, 1),
                                "s_b" + str(k % 2))

        wait("sync", s_v, v_xd)
        stage_bc(0)
        abuf = [o_A1, o_XR]
        for k in range(DS):
            par = k % 2
            pbB, pbC = PB[2 * par], PB[2 * par + 1]
            if k + 1 < DS:
                wait("sync", s_t, t_rep_hist.get(k - 1, 0))
                stage_bc(k + 1)
            bcb = o_WDT + par * 512
            wait("tensor", s_b[par], d_bc_hist[k])
            if k < 2:
                t_wait_bank(2 * par)
                t_wait_bank(2 * par + 1)
            pwar = v_chat_hist.get((c, k - 2), v_chat_hist.get((c - 1, DS - 1), 0))
            if pwar:
                wait("tensor", s_v, pwar)
            t_rep = tmm_group([
                lambda t, r=A(bcb, TC, 64, 1), o1=A(o_RP, 128, 64, 1), pb=pbB:
                t.matmul(pb[:, 0:TC], o1, r, start=True, stop=True),
                lambda t, r=A(bcb + TC, TC, 64, 1), o1=A(o_RP, 128, 64, 1), pb=pbC:
                t.matmul(pb[:, 0:TC], o1, r, start=True, stop=True)])
            t_rep_hist[k] = t_rep

            aw = v_scan_done.get((c, k - 2),
                                 v_scan_done.get((c - 1, DS - 1), 0))
            if aw:
                wait("scalar", s_v, aw)
            cw = v_acc_done.get((c, k - 2), v_acc_done.get((c - 1, DS - 1), 0))
            if cw:
                wait("scalar", s_p, cw[1])
            if k == 0:
                wait("scalar", s_v, v_u)
            wait("scalar", s_v, p_acol_v)
            s_a = ems(lambda s, sc=A(o_ac + k, 1), d=A(abuf[par], MEGA):
                      s.activation(d, A(o_A4, MEGA), AF.Exp,
                                   scale=sc).then_inc(s_s, 1))
            s_a_done[(c, k)] = s_a

            bb = pbB[:, 0:TC].unsqueeze(1).broadcast_to([128, NDB, TC])
            uu = A(o_U, MEGA).rearrange("p (blk t) -> p blk t", t=TC)
            ww = A(o_A2, MEGA).rearrange("p (blk t) -> p blk t", t=TC)
            wait("vector", s_t, t_rep)
            v_w = emv(lambda v, w=ww, u=uu, b2=bb:
                      v.tensor_tensor(w, u, b2, ALU.mult).then_inc(s_v, 1))
            p_w_hist[(c, k)] = 0

            wait("vector", s_s, s_a)
            for db in range(NDB):
                emv(lambda v, o=A(o_A5 + db * TC, TC),
                    a=A(abuf[par] + db * TC, TC),
                    w=A(o_A2 + db * TC, TC), i=A(o_ST + k * 16 + db, 1):
                    v.tensor_tensor_scan(o, a, w, i, ALU.mult, ALU.add)
                    .then_inc(s_v, 1))
                if db == NDB // 2 - 1:
                    v_scan_half[(c, k, 0)] = P.val("s_v")
            v_scan_half[(c, k, 1)] = P.val("s_v")
            lastc = A(o_A5, MEGA).rearrange("p (blk t) -> p blk t", t=TC)[:, :, TC - 1:TC]
            v_scan_done[(c, k)] = emv(
                lambda v, lc=lastc, d=A(o_ST + k * 16, 16).unsqueeze(2):
                v.tensor_copy(d, lc).then_inc(s_v, 1))

            cb2 = pbC[:, 0:TC].unsqueeze(1).broadcast_to([128, NDB, TC])
            hh = A(o_A5, MEGA).rearrange("p (blk t) -> p blk t", t=TC)
            tb2 = A(abuf[par], MEGA).rearrange("p (blk t) -> p blk t", t=TC)
            v_chat = emv(lambda v, o=tb2, h=hh, c2=cb2:
                         v.tensor_tensor(o, h, c2, ALU.mult).then_inc(s_v, 1))
            v_chat_hist[(c, k)] = v_chat
            bank_free[2 * par] = ("s_v", v_chat)
            bank_free[2 * par + 1] = ("s_v", v_chat)
            wait("gpsimd", s_v, v_chat)
            p_acc = emp(lambda g, t2=A(abuf[par], MEGA):
                        g.tensor_tensor(A(o_A8, MEGA), A(o_A8, MEGA),
                                        t2, ALU.add).then_inc(s_p, 1))
            v_acc_done[(c, k)] = ("p", p_acc)

        for bi4 in range(4):
            bank_free[bi4] = ("s_p", v_acc_done[(c, DS - 1)][1])

        if c == 0:
            dump("d_yacc", A(o_A8, MEGA), [("s_v", P.val("s_v"))])
        wait("vector", s_p, v_acc_done[(c, DS - 1)][1])
        wait("vector", s_s, max(s_sz_done))
        wait("vector", s_t, t_wx)
        v_yz = emv(lambda v: v.tensor_tensor(Ab(o_B1, MEGA), A(o_A8, MEGA),
                                             Ab(o_SZ, MEGA), ALU.mult)
                   .then_inc(s_v, 1))
        v_yz_prev = v_yz

        wait("tensor", s_v, v_yz)
        for bi in (4, 5, 6, 7):
            t_wait_bank(bi)
        t_y = {}
        for dib in range(NDB):
            tprev = t_wos_done.get((c, dib - 2),
                                   t_wos_done.get((c - 1, dib + 14), 0))
            wait("gpsimd", s_t, tprev)
            g_wo = gdma(woutT_r[:, dib, :], Ab(o_WOS + (dib % 2) * 512, 1024),
                        "s_go" + str(dib % 2))
            wait("tensor", s_go[dib % 2], g_wo)
            mms = []
            for tb in range(2):
                for hf in range(2):
                    bi = 4 + tb * 2 + hf
                    lhsT = Ab(o_B1, MEGA)[:, dib * TC + tb * 128:
                                          dib * TC + tb * 128 + 128]
                    rhs = Ab(o_WOS + (dib % 2) * 512, 1024)[:, hf * 512:(hf + 1) * 512]
                    mms.append(lambda t, b=bi, l=lhsT, r=rhs, st=(dib == 0),
                               sp=(dib == NDB - 1):
                               t.matmul(PB[b][:, :], l, r, start=st, stop=sp))
            t_wos_done[(c, dib)] = tmm_group(mms)
        t_ylast = P.val("s_t")

        rs_cols = o_MISC
        v_ev_by_bank = {}
        for tb in range(2):
            for hf in range(2):
                bi = 4 + tb * 2 + hf
                wait("vector", s_t, t_ylast)
                wait("vector", s_d, d_xn)
                wait("vector", s_p, v_acc_done[(c, DS - 1)][1])
                v_ev = emv(lambda v, d=A(o_R + tb * DM + hf * 512, 512),
                           p=PB[bi][:, :], x=A(o_XN + tb * DM + hf * 512, 512),
                           rc=A(rs_cols + tb * 2 + hf, 1):
                           v.scalar_tensor_tensor(d, p, 1.0, x, ALU.mult,
                                                  ALU.add, accum_out=rc)
                           .then_inc(s_v, 1))
                bank_free[bi] = ("s_v", v_ev)

        o_stat = o_MISC + 8
        for tb in range(2):
            mcol = A(o_stat + tb * 4 + 0, 1)
            vcol = A(o_stat + tb * 4 + 1, 1)
            rcol = A(o_stat + tb * 4 + 2, 1)
            ncol = A(o_stat + tb * 4 + 3, 1)
            tcol = A(o_MISC + 16, 1)
            emv(lambda v, m=mcol, a=A(rs_cols + tb * 2, 1),
                b2=A(rs_cols + tb * 2 + 1, 1):
                v.tensor_tensor(m, a, b2, ALU.add).then_inc(s_v, 1))
            v_m = emv(lambda v, m=mcol: v.tensor_scalar_mul(m, m, 1.0 / DM)
                      .then_inc(s_v, 1))
            wait("scalar", s_v, v_m)
            s_sq = ems(lambda s, jk=A(o_A5 + tb * DM, DM),
                       r=A(o_R + tb * DM, DM), q=vcol:
                       s.activation(jk, r, AF.Square, scale=1.0 / (DM ** 0.5),
                                    accum_out=q).then_inc(s_s, 1))
            wait("vector", s_s, s_sq)
            emv(lambda v, t2=tcol, m=mcol:
                v.tensor_tensor(t2, m, m, ALU.mult).then_inc(s_v, 1))
            v_var = emv(lambda v, q=vcol, t2=tcol:
                        v.tensor_tensor(q, q, t2, ALU.subtract).then_inc(s_v, 1))
            wait("scalar", s_v, v_var)
            ems(lambda s, q=vcol: s.activation(q, q, AF.Ln, bias=A(o_MISC + 48, 1))
                .then_inc(s_s, 1))
            s_rsc = ems(lambda s, q=vcol, r=rcol:
                        s.activation(r, q, AF.Exp, scale=-0.5).then_inc(s_s, 1))
            wait("vector", s_s, s_rsc)
            emv(lambda v, n=ncol, m=mcol, r=rcol:
                v.tensor_scalar(n, m, r, -1.0, ALU.mult, ALU.mult).then_inc(s_v, 1))
            emv(lambda v, d=A(o_XN + tb * DM, DM), r=A(o_R + tb * DM, DM),
                rc=rcol, n=ncol:
                v.tensor_scalar(d, r, rc, n, ALU.mult, ALU.add).then_inc(s_v, 1))
        v_ln1_done = P.val("s_v")

        wait("sync", s_v, v_ln1_done)
        d_agin_prev = sdma(A(o_XN, 2 * DM).rearrange("p (tb m) -> p tb m", tb=2),
                           agin[tc0:tc0 + TC, :]
                           .rearrange("(tb p) m -> p tb m", p=128))
        wait("gpsimd", s_d, d_agin_prev)
        em("gpsimd", lambda g, c2=c: g.collective_compute(
            "AllGather", ALU.bypass,
            replica_groups=[list(range(NCORE))],
            ins=[agin[c2 * TC:(c2 + 1) * TC, :].opt()],
            outs=[agall[c2, :, :].opt()]).then_inc(s_c, 1))
        cc_round[c] = P.inc("s_c", 1)

        if c == 0:
            dump("d_ln1", A(o_XN, 2 * DM), [("s_v", v_ln1_done)])
            if DEBUG:
                wait("sync", s_c, cc_round[0])
                sdma(agall[0, :, :], dbg["d_ag0"], "s_o")
        if c + 1 < NCH:
            xr_r = A(o_XR, XRM).rearrange("p (blk w) -> p blk w", w=260)
            emv(lambda v, s=A(o_RP + 128, 48)
                .rearrange("p (blk w) -> p blk w", w=3), d=xr_r[:, :, 0:3]:
                v.tensor_copy(d, s).then_inc(s_v, 1))
            wait("gpsimd", s_t, t_wsl_done[(c, 7)])
            g_xtb[(c + 1) % 2] = gdma(
                xT_r[:, :, (c + 1) * TC:(c + 2) * TC],
                Ab(o_XTB + ((c + 1) % 2) * 1024, 2048)
                .rearrange("p (blk t) -> p blk t", t=TC),
                "s_gx" + str((c + 1) % 2))

    v_md = P.val("s_v")
    s_md = P.val("s_s")
    t_md = P.val("s_t")
    wait("gpsimd", s_v, v_md)
    wait("gpsimd", s_s, s_md)
    wait("gpsimd", s_t, t_md)

    g_wg = gdma(wgT.rearrange("(blk p) o -> p blk o", p=128),
                Ab(f_WG, 8 * G3).rearrange("p (blk o) -> p blk o", o=G3))
    g_ug = gdma(ugT.rearrange("(blk p) o -> p blk o", p=128),
                Ab(f_UG, 8 * G3).rearrange("p (blk o) -> p blk o", o=G3))
    g_wo2 = gdma(woT.rearrange("(blk p) o -> p blk o", p=128),
                 Ab(f_WO, 8 * DM).rearrange("p (blk o) -> p blk o", o=DM))

    wait("vector", s_v, v_md)
    for mb in range(8):
        emv(lambda v, d=A(f_C + mb * TC, TC), i=INC[:, mb:mb + 1]:
            v.tensor_scalar(d, A(o_ST, TC), 0.0, i, ALU.mult, ALU.add)
            .then_inc(s_v, 1))
    v_cup = emv(lambda v: v.tensor_copy(Ab(f_CB, 2048), A(f_C, 2048))
                .then_inc(s_v, 1))

    agall_r = agall.rearrange("r (j2 tb p) m -> r j2 p tb m", tb=2, p=128)
    pid_holder = {}

    def get_pid(e):
        if "v" not in pid_holder:
            pid_holder["v"] = e.partition_id()
        return pid_holder["v"]

    wait("sync", s_c, cc_round[NCH - 1])

    def load_xt(j):
        def dof(e, j=j):
            pid = get_pid(e)
            e.dma_start(out=A(f_X32, 2 * DM).rearrange("p (tb m) -> p tb m", tb=2),
                        in_=agall_r[pid, j, :, :, :]).then_inc(s_d, 16)
        em("sync", dof)
        return P.inc("s_d", 16)

    def transpose_xt(j, buf, d_x):
        wait("tensor", s_d, d_x)
        s_last = 0
        for tb in range(2):
            for mb in range(8):
                src = A(f_X32 + tb * DM + mb * 128, 128)
                t_tr = tmm_group([lambda t, s2=src:
                                  t.transpose(PB[2][:, 0:128], s2, A(o_EYE, 128))])
                wait("scalar", s_t, t_tr)
                dstb = Ab(f_XTT + buf * 1024, 2048)[
                    :, mb * TC + tb * 128: mb * TC + tb * 128 + 128]
                s_last = ems(lambda s, d=dstb:
                             s.copy(d, PB[2][:, 0:128]).then_inc(s_s, 1))
                wait("tensor", s_s, s_last)
        return P.val("s_t"), s_last

    t_xp = {}
    s_xp = {}
    d_x32 = {}
    v_cstep = {}
    for j in range(B):
        wait("sync", s_t, t_xp.get(j - 1, 0))
        d_x32[j] = load_xt(j)
        buf = j % 2
        t_xp[j], s_xp[j] = transpose_xt(j, buf, d_x32[j])

        wait("tensor", s_g, g_wo2)
        wait("tensor", s_s, s_xp[j])
        wait("tensor", s_v, v_cup if j == 0 else v_cstep[j - 1])
        s_gate_last = 0
        GB_BANKS = (0, 1, 3, 4)
        for gb in range(24):
            bi = GB_BANKS[gb % 4]
            t_wait_bank(bi)
            mms = []
            for mb in range(8):
                lhsT = Ab(f_WG, 8 * G3)[:, mb * G3 + gb * 128: mb * G3 + (gb + 1) * 128]
                rhs = Ab(f_XTT + buf * 1024, 2048)[:, mb * TC:(mb + 1) * TC]
                mms.append(lambda t, b=bi, l=lhsT, r=rhs, st=(mb == 0):
                           t.matmul(PB[b][:, 0:TC], l, r, start=st, stop=False))
            for mb in range(8):
                lhsT = Ab(f_UG, 8 * G3)[:, mb * G3 + gb * 128: mb * G3 + (gb + 1) * 128]
                rhs = Ab(f_CB, 2048)[:, mb * TC:(mb + 1) * TC]
                mms.append(lambda t, b=bi, l=lhsT, r=rhs, sp=(mb == 7):
                           t.matmul(PB[b][:, 0:TC], l, r, start=False, stop=sp))
            t_g = tmm_group(mms)
            kind = gb // 8
            mb8 = gb % 8
            dst = [f_F, f_I, f_G][kind] + mb8 * TC
            func = AF.Tanh if kind == 2 else AF.Sigmoid
            wait("scalar", s_t, t_g)
            s_gate_last = ems(lambda s, b=bi, d=A(dst, TC),
                              bb2=BGC[:, gb:gb + 1], fn=func:
                              s.activation(d, PB[b][:, 0:TC], fn, bias=bb2)
                              .then_inc(s_s, 1))
            bank_free[bi] = ("s_s", s_gate_last)

        wait("vector", s_s, s_gate_last)
        for mb in range(8):
            fsl = A(f_F + mb * TC, TC)
            isl = A(f_I + mb * TC, TC)
            gsl = A(f_G + mb * TC, TC)
            csl = A(f_C + mb * TC, TC)
            emv(lambda v, f=fsl, c2=csl: v.tensor_tensor(f, f, c2, ALU.mult)
                .then_inc(s_v, 1))
            emv(lambda v, i=isl, g2=gsl: v.tensor_tensor(g2, i, g2, ALU.mult)
                .then_inc(s_v, 1))
            emv(lambda v, c2=csl, f=fsl, g2=gsl:
                v.tensor_tensor(c2, f, g2, ALU.add).then_inc(s_v, 1))
        v_cstep[j] = emv(lambda v: v.tensor_copy(Ab(f_CB, 2048), A(f_C, 2048))
                         .then_inc(s_v, 1))
        if j == 0:
            dump("d_c0", A(f_C, 2048), [("s_v", v_cstep[j])])

    wait("gpsimd", s_s, P.val("s_s"))
    wait("gpsimd", s_v, P.val("s_v"))
    g_uo2 = gdma(uoT.rearrange("(blk p) o -> p blk o", p=128),
                 Ab(f_UO, 8 * DM).rearrange("p (blk o) -> p blk o", o=DM))

    v_cF = v_cstep[B - 1]
    v_ln2 = 0
    d_out_prev = 0
    t_xp_fin_prev = t_xp[B - 1]
    for j in range(B):
        wait("sync", s_t, t_xp_fin_prev)
        if v_ln2:
            wait("sync", s_v, v_ln2)
        d_x = load_xt(j)
        buf = j % 2
        t_xp_fin_prev, s_xpf = transpose_xt(j, buf, d_x)

        wait("tensor", s_g, g_uo2)
        wait("tensor", s_s, s_xpf)
        wait("tensor", s_v, v_cF)
        s_tanh_last = 0
        for pb2 in range(2):
            for hf in range(2):
                bi = 5 + hf
                t_wait_bank(bi)
                mms = []
                for mb in range(8):
                    lhsT = Ab(f_XTT + buf * 1024, 2048)[
                        :, mb * TC + pb2 * 128: mb * TC + pb2 * 128 + 128]
                    rhs = Ab(f_WO, 8 * DM)[:, mb * DM + hf * 512: mb * DM + (hf + 1) * 512]
                    mms.append(lambda t, b=bi, l=lhsT, r=rhs, st=(mb == 0):
                               t.matmul(PB[b][:, :], l, r, start=st, stop=False))
                for mb in range(8):
                    lhsT = Ab(f_CB, 2048)[:, mb * TC + pb2 * 128: mb * TC + pb2 * 128 + 128]
                    rhs = Ab(f_UO, 8 * DM)[:, mb * DM + hf * 512: mb * DM + (hf + 1) * 512]
                    mms.append(lambda t, b=bi, l=lhsT, r=rhs:
                               t.matmul(PB[b][:, :], l, r, start=False, stop=False))
                mms.append(lambda t, b=bi, h=hf:
                           t.matmul(PB[b][:, :], ONE[:, 0:128],
                                    BOR[:, h * 512:(h + 1) * 512],
                                    start=False, stop=True))
                t_h = tmm_group(mms)
                wait("scalar", s_t, t_h)
                if d_out_prev and pb2 == 0 and hf == 0:
                    wait("scalar", s_o, d_out_prev)
                s_tanh_last = ems(lambda s, b=bi,
                                  d=A(f_H + pb2 * DM + hf * 512, 512):
                                  s.activation(d, PB[b][:, :], AF.Tanh)
                                  .then_inc(s_s, 1))
                bank_free[bi] = ("s_s", s_tanh_last)

        if j == 0:
            dump("d_hf0", A(f_H, 2048), [("s_s", s_tanh_last)])
        wait("vector", s_s, s_tanh_last)
        wait("vector", s_d, d_x)
        rs2 = o_MISC + 24
        for tb in range(2):
            for hf in range(2):
                emv(lambda v, d=A(f_H + tb * DM + hf * 512, 512),
                    x=A(f_X32 + tb * DM + hf * 512, 512),
                    rc=A(rs2 + tb * 2 + hf, 1):
                    v.scalar_tensor_tensor(d, d, 1.0, x, ALU.mult, ALU.add,
                                           accum_out=rc).then_inc(s_v, 1))
        o_st2 = o_MISC + 32
        for tb in range(2):
            mcol = A(o_st2 + tb * 4 + 0, 1)
            vcol = A(o_st2 + tb * 4 + 1, 1)
            rcol = A(o_st2 + tb * 4 + 2, 1)
            ncol = A(o_st2 + tb * 4 + 3, 1)
            tcol = A(o_MISC + 40, 1)
            emv(lambda v, m=mcol, a=A(rs2 + tb * 2, 1), b2=A(rs2 + tb * 2 + 1, 1):
                v.tensor_tensor(m, a, b2, ALU.add).then_inc(s_v, 1))
            v_m = emv(lambda v, m=mcol: v.tensor_scalar_mul(m, m, 1.0 / DM)
                      .then_inc(s_v, 1))
            wait("scalar", s_v, v_m)
            s_sq = ems(lambda s, jk=A(f_F + tb * DM, DM), r=A(f_H + tb * DM, DM),
                       q=vcol:
                       s.activation(jk, r, AF.Square, scale=1.0 / (DM ** 0.5),
                                    accum_out=q).then_inc(s_s, 1))
            wait("vector", s_s, s_sq)
            emv(lambda v, t2=tcol, m=mcol: v.tensor_tensor(t2, m, m, ALU.mult)
                .then_inc(s_v, 1))
            v_var = emv(lambda v, q=vcol, t2=tcol:
                        v.tensor_tensor(q, q, t2, ALU.subtract).then_inc(s_v, 1))
            wait("scalar", s_v, v_var)
            ems(lambda s, q=vcol: s.activation(q, q, AF.Ln, bias=A(o_MISC + 48, 1))
                .then_inc(s_s, 1))
            s_rsc = ems(lambda s, q=vcol, r=rcol:
                        s.activation(r, q, AF.Exp, scale=-0.5).then_inc(s_s, 1))
            wait("vector", s_s, s_rsc)
            emv(lambda v, n=ncol, m=mcol, r=rcol:
                v.tensor_scalar(n, m, r, -1.0, ALU.mult, ALU.mult).then_inc(s_v, 1))
            emv(lambda v, d=A(f_F + tb * DM, DM), r=A(f_H + tb * DM, DM),
                rc=rcol, n=ncol:
                v.tensor_scalar(d, r, rc, n, ALU.mult, ALU.add).then_inc(s_v, 1))
        v_ln2 = P.val("s_v")

        wait("sync", s_v, v_ln2)
        d_out_prev = sdma(A(f_F, 2 * DM).rearrange("p (tb m) -> p tb m", tb=2),
                          out[j, :, :].rearrange("(tb p) m -> p tb m", p=128),
                          "s_o")

    wait("sync", s_d, P.val("s_d"))
    wait("sync", s_o, P.val("s_o"))

    @block.gpsimd
    def _(g):
        for fn in prog["gpsimd"]:
            fn(g)

    @block.sync
    def _(e):
        for fn in prog["sync"]:
            fn(e)

    @block.vector
    def _(v):
        for fn in prog["vector"]:
            fn(v)

    @block.scalar
    def _(s):
        for fn in prog["scalar"]:
            fn(s)

    @block.tensor
    def _(t):
        for fn in prog["tensor"]:
            fn(t)

    es.close()
    return nc


_NC_CACHE = {}


def kernel(L0_attn_x, L0_attn_W_in, L0_attn_conv_w, L0_attn_conv_b, L0_attn_A_log,
           L0_attn_D, L0_attn_W_x, L0_attn_W_dt, L0_attn_b_dt, L0_attn_W_out,
           L0_ffn_wf, L0_ffn_uf, L0_ffn_bf, L0_ffn_wi, L0_ffn_ui, L0_ffn_bi,
           L0_ffn_wc, L0_ffn_uc, L0_ffn_bc, L0_ffn_wo, L0_ffn_uo, L0_ffn_bo,
           L0_ffn_in_c):
    x = np.asarray(L0_attn_x, dtype=np.float32)

    def T(a):
        return np.ascontiguousarray(np.asarray(a, dtype=np.float32).T)

    def colv(a):
        v = np.asarray(a, dtype=np.float32).reshape(-1)
        return np.ascontiguousarray(v.reshape(-1, 128).T)

    shared = {
        "winT": T(L0_attn_W_in),
        "wxT": T(L0_attn_W_x),
        "wdtT": T(L0_attn_W_dt),
        "woutT": T(L0_attn_W_out),
        "convw": np.ascontiguousarray(
            np.asarray(L0_attn_conv_w, dtype=np.float32)
            .reshape(16, 128, 4).transpose(1, 0, 2).reshape(128, 64)),
        "convb": colv(L0_attn_conv_b),
        "bdt": colv(L0_attn_b_dt),
        "alog": np.ascontiguousarray(
            np.asarray(L0_attn_A_log, dtype=np.float32).reshape(1, -1)),
        "dvec": colv(L0_attn_D),
        "wgT": np.ascontiguousarray(np.concatenate(
            [np.asarray(L0_ffn_wf), np.asarray(L0_ffn_wi),
             np.asarray(L0_ffn_wc)], axis=0).T.astype(np.float32)),
        "ugT": np.ascontiguousarray(np.concatenate(
            [np.asarray(L0_ffn_uf), np.asarray(L0_ffn_ui),
             np.asarray(L0_ffn_uc)], axis=0).T.astype(np.float32)),
        "woT": T(L0_ffn_wo),
        "uoT": T(L0_ffn_uo),
        "bg": colv(np.concatenate(
            [np.asarray(L0_ffn_bf), np.asarray(L0_ffn_bi),
             np.asarray(L0_ffn_bc)], axis=0)),
        "bo": np.ascontiguousarray(
            np.asarray(L0_ffn_bo, dtype=np.float32).reshape(1, -1)),
        "incv": colv(L0_ffn_in_c),
        "eye": np.eye(128, dtype=np.float32),
    }

    in_maps = []
    for j in range(NCORE):
        m = dict(shared)
        m["xT"] = np.ascontiguousarray(x[j].T)
        m["xn"] = np.ascontiguousarray(x[j])
        in_maps.append(m)

    if "nc" not in _NC_CACHE:
        _NC_CACHE["nc"] = build_nc()
    nc = _NC_CACHE["nc"]

    res = run_bass_kernel_spmd(nc, in_maps, core_ids=list(range(NCORE)))

    full = np.empty((B, L, DM), dtype=np.float32)
    for i in range(NCORE):
        full[:, i * POS:(i + 1) * POS, :] = res.results[i]["out"]
    return full


# version 10
# speedup vs baseline: 1.2201x; 1.0262x over previous
import numpy as np
from contextlib import ExitStack

import concourse.bass as bass
import concourse.mybir as mybir
from concourse.bass_utils import run_bass_kernel_spmd
from concourse import library_config

F32 = mybir.dt.float32
BF16 = mybir.dt.bfloat16
AF = mybir.ActivationFunctionType
ALU = mybir.AluOpType

B = 8
L = 2048
DM = 1024
DI = 2048
DS = 16
RK = 64
TC = 256
NCH = L // TC
NDB = DI // 128
POS = L // 8
G3 = 3 * DM
EPS = 1e-5
NCORE = 8
MEGA = NDB * TC
XRM = NDB * 260


class Plan:
    def __init__(self):
        self.c = {}

    def inc(self, name, amt):
        self.c[name] = self.c.get(name, 0) + amt
        return self.c[name]

    def val(self, name):
        return self.c.get(name, 0)


DEBUG = False


def build_nc():
    nc = bass.Bass("TRN2", target_bir_lowering=False, debug=False)

    def din(name, shape):
        return nc.dram_tensor(name, shape, F32, kind="ExternalInput").ap()

    xT = din("xT", [DM, L])
    xn = din("xn", [L, DM])
    winT = din("winT", [DM, 2 * DI])
    wxT = din("wxT", [DI, RK + 2 * DS])
    wdtT = din("wdtT", [RK, DI])
    woutT = din("woutT", [DI, DM])
    convw = din("convw", [128, 64])
    convb = din("convb", [128, 16])
    bdt = din("bdt", [128, 16])
    alog = din("alog", [1, DS])
    dvec = din("dvec", [128, 16])
    wgT = din("wgT", [DM, G3])
    ugT = din("ugT", [DM, G3])
    woT = din("woT", [DM, DM])
    uoT = din("uoT", [DM, DM])
    bg = din("bg", [128, 24])
    bo = din("bo", [1, DM])
    incv = din("incv", [128, 8])
    eye = din("eye", [128, 128])

    out = nc.dram_tensor("out", [B, POS, DM], F32, kind="ExternalOutput").ap()
    dbg = {}
    if DEBUG:
        for nm, shp in [("d_xr", [128, XRM]), ("d_xs", [128, MEGA]),
                        ("d_mdt", [128, MEGA]), ("d_xdbl", [96, TC]),
                        ("d_w0", [128, MEGA]), ("d_a0", [128, MEGA]),
                        ("d_h0", [128, MEGA]), ("d_yacc", [128, MEGA]),
                        ("d_ln1", [128, 2 * DM]), ("d_ag0", [8 * TC, DM]),
                        ("d_c0", [128, 2048]), ("d_hf0", [128, 2048]),
                        ("d_acol", [128, 16]), ("d_arw", [1, 16])]:
            dbg[nm] = nc.dram_tensor(nm, shp, F32, kind="ExternalOutput").ap()
    agin = nc.dram_tensor("agin", [L, DM], F32)
    agall = nc.dram_tensor("agall", [NCH, 8 * TC, DM], F32)

    es = ExitStack()

    AR_COLS = 47552
    AR = es.enter_context(nc.sbuf_tensor("AR", [128, AR_COLS], F32))
    _ptr = [0]

    def areg(n):
        o = _ptr[0]
        _ptr[0] += n
        assert _ptr[0] <= AR_COLS, f"arena overflow {_ptr[0]}"
        return o

    o_XR = areg(XRM)
    o_U = areg(MEGA)
    o_A1 = areg(MEGA)
    o_A2 = areg(MEGA)
    o_A4 = areg(MEGA)
    o_A5 = areg(MEGA)
    o_A8 = areg(MEGA)
    o_SZ = areg(MEGA // 2)
    o_B1 = areg(MEGA // 2)
    o_XTB = areg(2 * 1024)
    o_XN = areg(2 * DM)
    o_R = areg(2 * DM)
    o_WSL = areg(2 * 2048)
    o_WOS = areg(2 * 512)
    o_WXT = areg(768)
    o_WDT = areg(1024)
    o_XD = areg(256)
    o_XDB = areg(256)
    o_RP = areg(2 * TC)
    o_ST = areg(256)
    o_CL = areg(160)
    o_EYE = areg(128)
    o_MISC = areg(64)
    mamba_top = _ptr[0]

    _fp = [0]

    def freg(n):
        o = _fp[0]
        _fp[0] += n
        assert _fp[0] <= mamba_top, f"ffn overflow {_fp[0]} > {mamba_top}"
        return o

    f_WG = freg(12288)
    f_UG = freg(12288)
    f_WO = freg(4096)
    f_XTT = freg(2 * 1024)
    f_C = freg(2048)
    f_CB = freg(1024)
    f_F = freg(2048)
    f_I = freg(2048)
    f_G = freg(2048)
    f_X32 = freg(2048)
    f_H = freg(2048)
    f_UO = f_I

    BGC = es.enter_context(nc.sbuf_tensor("BGC", [128, 24], F32))
    INC = es.enter_context(nc.sbuf_tensor("INC", [128, 8], F32))
    BOR = es.enter_context(nc.sbuf_tensor("BOR", [1, DM], F32))
    ONE = es.enter_context(nc.sbuf_tensor("ONE", [1, 128], F32))
    ALG = es.enter_context(nc.sbuf_tensor("ALG", [1, DS], F32))
    ARW = es.enter_context(nc.sbuf_tensor("ARW", [1, DS], F32))

    PB = [es.enter_context(nc.psum_tensor(f"PB{i}", [128, 512], F32))
          for i in range(8)]

    s_d = es.enter_context(nc.semaphore("s_d"))
    s_k = es.enter_context(nc.semaphore("s_k"))
    s_o = es.enter_context(nc.semaphore("s_o"))
    s_b = [es.enter_context(nc.semaphore("s_b0")),
           es.enter_context(nc.semaphore("s_b1"))]
    s_g = es.enter_context(nc.semaphore("s_g"))
    s_gx = [es.enter_context(nc.semaphore("s_gx0")),
            es.enter_context(nc.semaphore("s_gx1"))]
    s_gw = [es.enter_context(nc.semaphore("s_gw0")),
            es.enter_context(nc.semaphore("s_gw1"))]
    s_go = [es.enter_context(nc.semaphore("s_go0")),
            es.enter_context(nc.semaphore("s_go1"))]
    s_t = es.enter_context(nc.semaphore("s_t"))
    s_v = es.enter_context(nc.semaphore("s_v"))
    s_s = es.enter_context(nc.semaphore("s_s"))
    s_p = es.enter_context(nc.semaphore("s_p"))
    s_c = es.enter_context(nc.semaphore("s_c"))
    sem_map = {"s_v": s_v, "s_s": s_s, "s_d": s_d, "s_g": s_g,
               "s_t": s_t, "s_p": s_p, "s_c": s_c,
               "s_k": s_k, "s_o": s_o, "s_b0": s_b[0], "s_b1": s_b[1],
               "s_gx0": s_gx[0], "s_gx1": s_gx[1],
               "s_gw0": s_gw[0], "s_gw1": s_gw[1],
               "s_go0": s_go[0], "s_go1": s_go[1]}

    block = es.enter_context(nc.Block())
    P = Plan()
    prog = {e: [] for e in ("sync", "gpsimd", "vector", "scalar", "tensor")}

    def em(eng, fn):
        prog[eng].append(fn)

    def wait(eng, sem, v):
        if v > 0:
            em(eng, lambda e, sem=sem, v=v: e.wait_ge(sem, v))

    def A(o, n, p0=0, pn=128):
        return AR[p0:p0 + pn, o:o + n]

    def Ab(o, nbf, p0=0, pn=128):
        return AR[p0:p0 + pn, :].bitcast(BF16)[:, 2 * o:2 * o + nbf]

    def sdma(src_ap, dst_ap, semn="s_d"):
        sem = sem_map[semn]
        em("sync", lambda e, s=src_ap, d=dst_ap, m=sem:
           e.dma_start(out=d, in_=s).then_inc(m, 16))
        return P.inc(semn, 16)

    def gdma(src_ap, dst_ap, semn="s_g"):
        sem = sem_map[semn]
        em("gpsimd", lambda g, s=src_ap, d=dst_ap, m=sem:
           g.dma_start(out=d, in_=s).then_inc(m, 16))
        return P.inc(semn, 16)

    def emv(fn):
        wait("vector", s_v, P.val("s_v"))
        em("vector", fn)
        return P.inc("s_v", 1)

    def ems(fn):
        wait("scalar", s_s, P.val("s_s"))
        em("scalar", fn)
        return P.inc("s_s", 1)

    def emp(fn):
        wait("gpsimd", s_p, P.val("s_p"))
        em("gpsimd", fn)
        return P.inc("s_p", 1)

    def tmm_group(mms, bank_ap=None):
        for fn in mms[:-1]:
            em("tensor", fn)
        em("tensor", lambda t, f=mms[-1]: f(t).then_inc(s_t, 1))
        return P.inc("s_t", 1)

    def dump(nm, src_ap, waits=()):
        if not DEBUG or nm not in dbg:
            return
        for eng_sem, v in waits:
            wait("sync", sem_map[eng_sem], v)
        sdma(src_ap, dbg[nm], "s_o")
        dv = P.val("s_o")
        wait("sync", s_o, dv)
        for e2 in ("vector", "scalar", "gpsimd", "tensor"):
            wait(e2, s_o, dv)

    bank_free = [("s_v", 0)] * 8

    def t_wait_bank(bi):
        sem, v = bank_free[bi]
        wait("tensor", sem_map[sem], v)


    o_cw = o_CL
    o_cb = o_CL + 64
    o_bd = o_CL + 80
    o_dv = o_CL + 96
    o_ac = o_CL + 112
    o_nb = o_CL + 128

    sdma(convw, A(o_cw, 64), "s_k")
    sdma(convb, A(o_cb, 16), "s_k")
    sdma(bdt, A(o_bd, 16), "s_k")
    sdma(dvec, A(o_dv, 16), "s_k")
    sdma(alog, ALG[:, :], "s_k")
    sdma(eye, A(o_EYE, 128), "s_k")
    sdma(bg, BGC[:, :], "s_k")
    sdma(incv, INC[:, :], "s_k")
    d_cst = sdma(bo, BOR[:, :], "s_k")

    wait("vector", s_k, d_cst)
    emv(lambda v: v.tensor_scalar_mul(A(o_nb, 16), A(o_bd, 16), -1.0).then_inc(s_v, 1))
    v_ones = emv(lambda v: v.memset(ONE[:, :], 1.0).then_inc(s_v, 1))
    emv(lambda v: v.memset(A(o_ST, 256), 0.0).then_inc(s_v, 1))
    emv(lambda v: v.memset(A(o_RP, 128), 1.0).then_inc(s_v, 1))
    emv(lambda v: v.memset(A(o_MISC + 48, 1), EPS).then_inc(s_v, 1))
    emv(lambda v: v.memset(A(o_XR, XRM), 0.0).then_inc(s_v, 1))

    wait("scalar", s_k, d_cst)
    v_arow = ems(lambda s: s.activation(ARW[:, :], ALG[:, :], AF.Exp).then_inc(s_s, 1))
    wait("tensor", s_s, v_arow)
    wait("tensor", s_v, v_ones)
    t_ac = tmm_group([lambda t: t.matmul(PB[0][:, 0:DS], ONE[:, 0:128],
                                         ARW[:, :], start=True, stop=True)])
    wait("vector", s_t, t_ac)
    p_acol_v = emv(lambda v: v.tensor_copy(A(o_ac, 16), PB[0][:, 0:DS])
                   .then_inc(s_v, 1))
    bank_free[0] = ("s_v", p_acol_v)
    dump("d_acol", A(o_ac, 16), [("s_v", p_acol_v)])
    dump("d_arw", ARW[:, :], [("s_s", v_arow)])

    winT_r = winT.rearrange("(blk p) o -> p blk o", p=128)
    woutT_r = woutT.rearrange("(blk p) o -> p blk o", p=128)
    xT_r = xT.rearrange("(blk p) t -> p blk t", p=128)

    g_xtb = [0, 0]
    g_xtb[0] = gdma(xT_r[:, :, 0:TC],
                    Ab(o_XTB, 2048).rearrange("p (blk t) -> p blk t", t=TC),
                    "s_gx0")

    g_wx = gdma(wxT.rearrange("(blk p) o -> p blk o", p=128),
                Ab(o_WXT, 1536).rearrange("p (blk o) -> p blk o", o=96))
    g_wdt = gdma(wdtT, Ab(o_WDT, 2048, 0, 64))

    sz_done_map = {cc2: [0] * NDB for cc2 in range(NCH)}
    v_yz_by_chunk = {}

    def emit_xz_slice_load(c2, sl):
        buf = sl % 2
        tprev = t_wsl_done.get((c2, sl - 2), t_wsl_done.get((c2 - 1, sl + 6), 0))
        wait("gpsimd", s_t, tprev)
        g_sl = gdma(winT_r[:, :, sl * 512:(sl + 1) * 512],
                    Ab(o_WSL + buf * 2048, 4096)
                    .rearrange("p (blk o) -> p blk o", o=512),
                    "s_gw" + str(buf))
        g_wsl_val[(c2, sl)] = g_sl

    def emit_xz_group_mm(c2, dout):
        sl = dout // 4
        buf = sl % 2
        bi = dout % 2
        wait("tensor", s_gw[buf], g_wsl_val[(c2, sl)])
        wait("tensor", s_gx[c2 % 2], g_xtb[c2 % 2])
        t_wait_bank(bi)
        db = dout % 4
        mms = []
        for mb in range(8):
            lhsT = Ab(o_WSL + buf * 2048, 4096)[
                :, mb * 512 + db * 128: mb * 512 + (db + 1) * 128]
            rhs = Ab(o_XTB + (c2 % 2) * 1024, 2048)[:, mb * TC:(mb + 1) * TC]
            mms.append(lambda t, o=PB[bi][:, 0:TC], l=lhsT, r=rhs,
                       st=(mb == 0), sp=(mb == 7):
                       t.matmul(o, l, r, start=st, stop=sp))
        t_mm = tmm_group(mms)
        if dout % 4 == 3:
            t_wsl_done[(c2, sl)] = P.val("s_t")
        return t_mm, bi

    def emit_xr_evac(c2, dout, t_mm, bi):
        wait("vector", s_t, t_mm)
        v_ev = emv(lambda v, ps=PB[bi][:, 0:TC],
                   d=A(o_XR + dout * 260 + 3, TC):
                   v.tensor_copy(d, ps).then_inc(s_v, 1))
        bank_free[bi] = ("s_v", v_ev)

    def emit_z_evac(c2, dout, t_mm, bi):
        zb = dout - 16
        wait("scalar", s_t, t_mm)
        if zb == 0 and v_yz_by_chunk.get(c2 - 1):
            wait("scalar", s_v, v_yz_by_chunk[c2 - 1])
        sz_done_map[c2][zb] = ems(
            lambda s, ps=PB[bi][:, 0:TC],
            d=Ab(o_SZ, MEGA)[:, zb * TC:(zb + 1) * TC]:
            s.activation(d, ps, AF.Silu).then_inc(s_s, 1))
        bank_free[bi] = ("s_s", sz_done_map[c2][zb])

    g_wsl_val = {}

    v_ln1_done = 0
    v_yz_prev = 0
    t_wsl_done = {}
    t_wos_done = {}
    t_ylast = 0
    v_acc_done = {}
    v_scan_done = {}
    v_scan_half = {}
    p_w_hist = {}
    v_chat_hist = {}
    s_a_done = {}
    d_agin_prev = 0
    cc_round = [0] * NCH

    for c in range(NCH):
        tc0 = c * TC

        wait("sync", s_v, v_ln1_done)
        if d_agin_prev:
            wait("sync", s_d, d_agin_prev)
        d_xn = sdma(xn[tc0:tc0 + TC, :].rearrange("(tb p) m -> p tb m", p=128),
                    A(o_XN, 2 * DM).rearrange("p (tb m) -> p tb m", tb=2))

        s_sz_done = sz_done_map[c]
        if c == 0:
            for sl in range(4):
                emit_xz_slice_load(0, sl)
                for db in range(4):
                    t_mm, bi = emit_xz_group_mm(0, sl * 4 + db)
                    emit_xr_evac(0, sl * 4 + db, t_mm, bi)
            for sl in range(4, 8):
                emit_xz_slice_load(0, sl)
                for db in range(4):
                    t_mm, bi = emit_xz_group_mm(0, sl * 4 + db)
                    emit_z_evac(0, sl * 4 + db, t_mm, bi)

        if c == 0:
            dump("d_xr", A(o_XR, XRM), [("s_v", P.val("s_v"))])
        for db in range(NDB):
            slot = o_XR + db * 260
            xcd = A(o_A4 + db * TC, TC)
            emv(lambda v, d=xcd, s=A(slot, TC), w=A(o_cw + db * 4, 1),
                b=A(o_cb + db, 1):
                v.tensor_scalar(d, s, w, b, ALU.mult, ALU.add).then_inc(s_v, 1))
            for j in range(1, 4):
                emv(lambda v, d=xcd, s=A(slot + j, TC), w=A(o_cw + db * 4 + j, 1):
                    v.scalar_tensor_tensor(d, s, w, d, ALU.mult, ALU.add)
                    .then_inc(s_v, 1))
        xr_r0 = A(o_XR, XRM).rearrange("p (blk w) -> p blk w", w=260)
        emv(lambda v, s=xr_r0[:, :, 256:259], d=xr_r0[:, :, 0:3]:
            v.tensor_copy(d, s).then_inc(s_v, 1))
        v_conv = P.val("s_v")

        wait("scalar", s_v, v_conv)

        s_xs = ems(lambda s: s.activation(A(o_A2, MEGA), A(o_A4, MEGA), AF.Silu)
                   .then_inc(s_s, 1))
        if c == 0:
            dump("d_xs", A(o_A2, MEGA), [("s_s", s_xs)])
        wait("gpsimd", s_s, s_xs)
        if t_ylast:
            wait("gpsimd", s_t, t_ylast)
        p_xsb = emp(lambda g: g.tensor_copy(Ab(o_B1, MEGA), A(o_A2, MEGA))
                    .then_inc(s_p, 1))

        wait("tensor", s_p, p_xsb)
        if c == 0:
            wait("tensor", s_g, g_wdt)
        t_wait_bank(4)
        mms = []
        for dib in range(NDB):
            lhsT = Ab(o_WXT, 1536)[:, dib * 96:(dib + 1) * 96]
            rhs = Ab(o_B1, MEGA)[:, dib * TC:(dib + 1) * TC]
            mms.append(lambda t, l=lhsT, r=rhs, st=(dib == 0), sp=(dib == NDB - 1):
                       t.matmul(PB[4][0:96, 0:TC], l, r, start=st, stop=sp))
        t_wx = tmm_group(mms)

        wait("vector", s_t, t_wx)
        wait("vector", s_b[0], P.val("s_b0"))
        wait("vector", s_b[1], P.val("s_b1"))
        v_xd = emv(lambda v: v.tensor_copy(A(o_XD, TC, 0, 96), PB[4][0:96, 0:TC])
                   .then_inc(s_v, 1))
        bank_free[4] = ("s_v", v_xd)
        wait("scalar", s_v, v_xd)
        s_xdb = ems(lambda s: s.copy(Ab(o_XDB, 256, 0, 64), A(o_XD, TC, 0, 64))
                    .then_inc(s_s, 1))

        wait("tensor", s_s, s_xdb)
        if c == 0:
            wait("tensor", s_g, g_wdt)
        for db in range(NDB):
            bi = 4 + (db % 2)
            t_wait_bank(bi)
            mms = [lambda t, b=bi,
                   l=Ab(o_WDT, 2048, 0, 64)[:, db * 128:(db + 1) * 128],
                   r=Ab(o_XDB, 256, 0, 64):
                   t.matmul(PB[b][:, 0:TC], l, r, start=True, stop=True)]
            t_dt = tmm_group(mms)
            wait("scalar", s_t, t_dt)
            if db == 0 and v_acc_done.get((c - 1, DS - 1)):
                wait("scalar", s_v, v_acc_done[(c - 1, DS - 1)][1])
            s_sig = ems(lambda s, b=bi, d=A(o_A1 + db * TC, TC),
                        nb=A(o_nb + db, 1):
                        s.activation(d, PB[b][:, 0:TC], AF.Sigmoid,
                                     bias=nb, scale=-1.0).then_inc(s_s, 1))
            bank_free[bi] = ("s_s", s_sig)

        s_mdt = ems(lambda s: s.activation(A(o_A4, MEGA), A(o_A1, MEGA), AF.Ln)
                    .then_inc(s_s, 1))

        if c == 0:
            dump("d_mdt", A(o_A4, MEGA), [("s_s", s_mdt)])
            dump("d_xdbl", A(o_XD, TC, 0, 96), [("s_v", v_xd)])
        wait("vector", s_s, s_mdt)
        v_u = emv(lambda v: v.scalar_tensor_tensor(
            A(o_U, MEGA), A(o_A4, MEGA), -1.0, A(o_A2, MEGA),
            ALU.mult, ALU.mult).then_inc(s_v, 1))
        for db in range(NDB):
            emv(lambda v, d=A(o_A8 + db * TC, TC), s=A(o_A2 + db * TC, TC),
                w=A(o_dv + db, 1):
                v.tensor_scalar_mul(d, s, w).then_inc(s_v, 1))
        v_yacc0 = P.val("s_v")

        t_rep_hist = {}
        d_bc_hist = {}

        def stage_bc(k):
            bcb2 = o_WDT + (k % 2) * 512
            sdma(A(o_XD, TC, 64 + k, 1), A(bcb2, TC, 64, 1), "s_b" + str(k % 2))
            d_bc_hist[k] = sdma(A(o_XD, TC, 80 + k, 1), A(bcb2 + TC, TC, 64, 1),
                                "s_b" + str(k % 2))

        wait("sync", s_v, v_xd)
        stage_bc(0)
        HM = MEGA // 2
        if c + 1 < NCH:
            wait("gpsimd", s_t, t_wsl_done[(c, 7)])
            g_xtb[(c + 1) % 2] = gdma(
                xT_r[:, :, (c + 1) * TC:(c + 2) * TC],
                Ab(o_XTB + ((c + 1) % 2) * 1024, 2048)
                .rearrange("p (blk t) -> p blk t", t=TC),
                "s_gx" + str((c + 1) % 2))
        for k in range(DS):
            par = k % 2
            pbB, pbC = PB[2 * par + 2], PB[2 * par + 3] if False else PB[2 * par + 3]
            pbB = PB[2 * par + 2]
            pbC = PB[2 * par + 3]
            if k + 1 < DS:
                wait("sync", s_t, t_rep_hist.get(k - 1, 0))
                stage_bc(k + 1)
            bcb = o_WDT + par * 512
            wait("tensor", s_b[par], d_bc_hist[k])
            if k < 2:
                t_wait_bank(2 * par + 2)
                t_wait_bank(2 * par + 3)
            pwar = v_chat_hist.get((c, k - 2), v_chat_hist.get((c - 1, DS - 1), 0))
            if pwar:
                wait("tensor", s_v, pwar)
            t_rep = tmm_group([
                lambda t, r=A(bcb, TC, 64, 1), o1=A(o_RP, 128, 64, 1), pb=pbB:
                t.matmul(pb[:, 0:TC], o1, r, start=True, stop=True),
                lambda t, r=A(bcb + TC, TC, 64, 1), o1=A(o_RP, 128, 64, 1), pb=pbC:
                t.matmul(pb[:, 0:TC], o1, r, start=True, stop=True)])
            t_rep_hist[k] = t_rep

            s_ah = [0, 0]
            for hf2 in range(2):
                aw = v_scan_half.get((c, k - 1, hf2),
                                     v_scan_half.get((c - 1, DS - 1, hf2), 0))
                if aw:
                    wait("scalar", s_v, aw)
                wait("scalar", s_v, p_acol_v)
                s_ah[hf2] = ems(lambda s, sc=A(o_ac + k, 1),
                                d=A(o_A1 + hf2 * HM, HM),
                                src=A(o_A4 + hf2 * HM, HM):
                                s.activation(d, src, AF.Exp,
                                             scale=sc).then_inc(s_s, 1))
            s_a_done[(c, k)] = s_ah[1]

            bb = pbB[:, 0:TC].unsqueeze(1).broadcast_to([128, NDB, TC])
            uu = A(o_U, MEGA).rearrange("p (blk t) -> p blk t", t=TC)
            ww = A(o_A2, MEGA).rearrange("p (blk t) -> p blk t", t=TC)
            wait("vector", s_t, t_rep)
            emv(lambda v, w=ww, u=uu, b2=bb:
                v.tensor_tensor(w, u, b2, ALU.mult).then_inc(s_v, 1))

            for db in range(NDB):
                if db == 0:
                    wait("vector", s_s, s_ah[0])
                if db == NDB // 2:
                    wait("vector", s_s, s_ah[1])
                emv(lambda v, o=A(o_A5 + db * TC, TC),
                    a=A(o_A1 + db * TC, TC),
                    w=A(o_A2 + db * TC, TC), i=A(o_ST + k * 16 + db, 1):
                    v.tensor_tensor_scan(o, a, w, i, ALU.mult, ALU.add)
                    .then_inc(s_v, 1))
                if db == NDB // 2 - 1:
                    v_scan_half[(c, k, 0)] = P.val("s_v")
            v_scan_half[(c, k, 1)] = P.val("s_v")
            if c == 0 and k == 0:
                dump("d_a0", A(o_A1, MEGA), [("s_s", s_ah[1])])
                dump("d_w0", A(o_A2, MEGA), [("s_v", P.val("s_v"))])
                dump("d_h0", A(o_A5, MEGA), [("s_v", P.val("s_v"))])
            lastc = A(o_A5, MEGA).rearrange("p (blk t) -> p blk t", t=TC)[:, :, TC - 1:TC]
            v_scan_done[(c, k)] = emv(
                lambda v, lc=lastc, d=A(o_ST + k * 16, 16).unsqueeze(2):
                v.tensor_copy(d, lc).then_inc(s_v, 1))

            cb2 = pbC[:, 0:TC].unsqueeze(1).broadcast_to([128, NDB, TC])
            hh = A(o_A5, MEGA).rearrange("p (blk t) -> p blk t", t=TC)
            v_chat = emv(lambda v, o=hh, h=hh, c2=cb2:
                         v.tensor_tensor(o, h, c2, ALU.mult).then_inc(s_v, 1))
            v_chat_hist[(c, k)] = v_chat
            bank_free[2 * par + 2] = ("s_v", v_chat)
            bank_free[2 * par + 3] = ("s_v", v_chat)
            v_acc = emv(lambda v:
                        v.tensor_tensor(A(o_A8, MEGA), A(o_A8, MEGA),
                                        A(o_A5, MEGA), ALU.add).then_inc(s_v, 1))
            v_acc_done[(c, k)] = ("v", v_acc)

            if c + 1 < NCH:
                if k % 4 == 0:
                    emit_xz_slice_load(c + 1, k // 4)
                t_mmp, bip = emit_xz_group_mm(c + 1, k)
                emit_xr_evac(c + 1, k, t_mmp, bip)



        if c == 0:
            dump("d_yacc", A(o_A8, MEGA), [("s_v", P.val("s_v"))])
        wait("vector", s_s, max(s_sz_done))
        wait("vector", s_t, t_wx)
        v_yz = emv(lambda v: v.tensor_tensor(Ab(o_B1, MEGA), A(o_A8, MEGA),
                                             Ab(o_SZ, MEGA), ALU.mult)
                   .then_inc(s_v, 1))
        v_yz_prev = v_yz
        v_yz_by_chunk[c] = v_yz

        wait("tensor", s_v, v_yz)
        for bi in (4, 5, 6, 7):
            t_wait_bank(bi)
        t_y = {}
        for dib in range(NDB):
            tprev = t_wos_done.get((c, dib - 2),
                                   t_wos_done.get((c - 1, dib + 14), 0))
            wait("gpsimd", s_t, tprev)
            g_wo = gdma(woutT_r[:, dib, :], Ab(o_WOS + (dib % 2) * 512, 1024),
                        "s_go" + str(dib % 2))
            wait("tensor", s_go[dib % 2], g_wo)
            mms = []
            for tb in range(2):
                for hf in range(2):
                    bi = 4 + tb * 2 + hf
                    lhsT = Ab(o_B1, MEGA)[:, dib * TC + tb * 128:
                                          dib * TC + tb * 128 + 128]
                    rhs = Ab(o_WOS + (dib % 2) * 512, 1024)[:, hf * 512:(hf + 1) * 512]
                    mms.append(lambda t, b=bi, l=lhsT, r=rhs, st=(dib == 0),
                               sp=(dib == NDB - 1):
                               t.matmul(PB[b][:, :], l, r, start=st, stop=sp))
            t_wos_done[(c, dib)] = tmm_group(mms)
        t_ylast = P.val("s_t")

        rs_cols = o_MISC
        v_ev_by_bank = {}
        for tb in range(2):
            for hf in range(2):
                bi = 4 + tb * 2 + hf
                wait("vector", s_t, t_ylast)
                wait("vector", s_d, d_xn)
                v_ev = emv(lambda v, d=A(o_R + tb * DM + hf * 512, 512),
                           p=PB[bi][:, :], x=A(o_XN + tb * DM + hf * 512, 512),
                           rc=A(rs_cols + tb * 2 + hf, 1):
                           v.scalar_tensor_tensor(d, p, 1.0, x, ALU.mult,
                                                  ALU.add, accum_out=rc)
                           .then_inc(s_v, 1))
                bank_free[bi] = ("s_v", v_ev)

        o_stat = o_MISC + 8
        for tb in range(2):
            mcol = A(o_stat + tb * 4 + 0, 1)
            vcol = A(o_stat + tb * 4 + 1, 1)
            rcol = A(o_stat + tb * 4 + 2, 1)
            ncol = A(o_stat + tb * 4 + 3, 1)
            tcol = A(o_MISC + 16, 1)
            emv(lambda v, m=mcol, a=A(rs_cols + tb * 2, 1),
                b2=A(rs_cols + tb * 2 + 1, 1):
                v.tensor_tensor(m, a, b2, ALU.add).then_inc(s_v, 1))
            v_m = emv(lambda v, m=mcol: v.tensor_scalar_mul(m, m, 1.0 / DM)
                      .then_inc(s_v, 1))
            wait("scalar", s_v, v_m)
            s_sq = ems(lambda s, jk=A(o_A5 + tb * DM, DM),
                       r=A(o_R + tb * DM, DM), q=vcol:
                       s.activation(jk, r, AF.Square, scale=1.0 / (DM ** 0.5),
                                    accum_out=q).then_inc(s_s, 1))
            wait("vector", s_s, s_sq)
            emv(lambda v, t2=tcol, m=mcol:
                v.tensor_tensor(t2, m, m, ALU.mult).then_inc(s_v, 1))
            v_var = emv(lambda v, q=vcol, t2=tcol:
                        v.tensor_tensor(q, q, t2, ALU.subtract).then_inc(s_v, 1))
            wait("scalar", s_v, v_var)
            ems(lambda s, q=vcol: s.activation(q, q, AF.Ln, bias=A(o_MISC + 48, 1))
                .then_inc(s_s, 1))
            s_rsc = ems(lambda s, q=vcol, r=rcol:
                        s.activation(r, q, AF.Exp, scale=-0.5).then_inc(s_s, 1))
            wait("vector", s_s, s_rsc)
            emv(lambda v, n=ncol, m=mcol, r=rcol:
                v.tensor_scalar(n, m, r, -1.0, ALU.mult, ALU.mult).then_inc(s_v, 1))
            emv(lambda v, d=A(o_XN + tb * DM, DM), r=A(o_R + tb * DM, DM),
                rc=rcol, n=ncol:
                v.tensor_scalar(d, r, rc, n, ALU.mult, ALU.add).then_inc(s_v, 1))
        v_ln1_done = P.val("s_v")

        wait("sync", s_v, v_ln1_done)
        d_agin_prev = sdma(A(o_XN, 2 * DM).rearrange("p (tb m) -> p tb m", tb=2),
                           agin[tc0:tc0 + TC, :]
                           .rearrange("(tb p) m -> p tb m", p=128))
        wait("gpsimd", s_d, d_agin_prev)
        em("gpsimd", lambda g, c2=c: g.collective_compute(
            "AllGather", ALU.bypass,
            replica_groups=[list(range(NCORE))],
            ins=[agin[c2 * TC:(c2 + 1) * TC, :].opt()],
            outs=[agall[c2, :, :].opt()]).then_inc(s_c, 1))
        cc_round[c] = P.inc("s_c", 1)

        if c + 1 < NCH:
            for sl in range(4, 8):
                emit_xz_slice_load(c + 1, sl)
                for db in range(4):
                    t_mmz, biz = emit_xz_group_mm(c + 1, sl * 4 + db)
                    emit_z_evac(c + 1, sl * 4 + db, t_mmz, biz)

        if c == 0:
            dump("d_ln1", A(o_XN, 2 * DM), [("s_v", v_ln1_done)])
            if DEBUG:
                wait("sync", s_c, cc_round[0])
                sdma(agall[0, :, :], dbg["d_ag0"], "s_o")

    v_md = P.val("s_v")
    s_md = P.val("s_s")
    t_md = P.val("s_t")
    wait("gpsimd", s_v, v_md)
    wait("gpsimd", s_s, s_md)
    wait("gpsimd", s_t, t_md)

    g_wg = gdma(wgT.rearrange("(blk p) o -> p blk o", p=128),
                Ab(f_WG, 8 * G3).rearrange("p (blk o) -> p blk o", o=G3))
    g_ug = gdma(ugT.rearrange("(blk p) o -> p blk o", p=128),
                Ab(f_UG, 8 * G3).rearrange("p (blk o) -> p blk o", o=G3))
    g_wo2 = gdma(woT.rearrange("(blk p) o -> p blk o", p=128),
                 Ab(f_WO, 8 * DM).rearrange("p (blk o) -> p blk o", o=DM))

    wait("vector", s_v, v_md)
    for mb in range(8):
        emv(lambda v, d=A(f_C + mb * TC, TC), i=INC[:, mb:mb + 1]:
            v.tensor_scalar(d, A(o_ST, TC), 0.0, i, ALU.mult, ALU.add)
            .then_inc(s_v, 1))
    v_cup = emv(lambda v: v.tensor_copy(Ab(f_CB, 2048), A(f_C, 2048))
                .then_inc(s_v, 1))

    agall_r = agall.rearrange("r (j2 tb p) m -> r j2 p tb m", tb=2, p=128)
    pid_holder = {}

    def get_pid(e):
        if "v" not in pid_holder:
            pid_holder["v"] = e.partition_id()
        return pid_holder["v"]

    wait("sync", s_c, cc_round[NCH - 1])

    def load_xt(j):
        def dof(e, j=j):
            pid = get_pid(e)
            e.dma_start(out=A(f_X32, 2 * DM).rearrange("p (tb m) -> p tb m", tb=2),
                        in_=agall_r[pid, j, :, :, :]).then_inc(s_d, 16)
        em("sync", dof)
        return P.inc("s_d", 16)

    def transpose_xt(j, buf, d_x):
        wait("tensor", s_d, d_x)
        s_last = 0
        for tb in range(2):
            for mb in range(8):
                src = A(f_X32 + tb * DM + mb * 128, 128)
                t_tr = tmm_group([lambda t, s2=src:
                                  t.transpose(PB[2][:, 0:128], s2, A(o_EYE, 128))])
                wait("scalar", s_t, t_tr)
                dstb = Ab(f_XTT + buf * 1024, 2048)[
                    :, mb * TC + tb * 128: mb * TC + tb * 128 + 128]
                s_last = ems(lambda s, d=dstb:
                             s.copy(d, PB[2][:, 0:128]).then_inc(s_s, 1))
                wait("tensor", s_s, s_last)
        return P.val("s_t"), s_last

    t_xp = {}
    s_xp = {}
    d_x32 = {}
    v_cstep = {}
    for j in range(B):
        wait("sync", s_t, t_xp.get(j - 1, 0))
        d_x32[j] = load_xt(j)
        buf = j % 2
        t_xp[j], s_xp[j] = transpose_xt(j, buf, d_x32[j])

        wait("tensor", s_g, g_wo2)
        wait("tensor", s_s, s_xp[j])
        wait("tensor", s_v, v_cup if j == 0 else v_cstep[j - 1])
        s_gate_last = 0
        GB_BANKS = (0, 1, 3, 4)
        for gb in range(24):
            bi = GB_BANKS[gb % 4]
            t_wait_bank(bi)
            mms = []
            for mb in range(8):
                lhsT = Ab(f_WG, 8 * G3)[:, mb * G3 + gb * 128: mb * G3 + (gb + 1) * 128]
                rhs = Ab(f_XTT + buf * 1024, 2048)[:, mb * TC:(mb + 1) * TC]
                mms.append(lambda t, b=bi, l=lhsT, r=rhs, st=(mb == 0):
                           t.matmul(PB[b][:, 0:TC], l, r, start=st, stop=False))
            for mb in range(8):
                lhsT = Ab(f_UG, 8 * G3)[:, mb * G3 + gb * 128: mb * G3 + (gb + 1) * 128]
                rhs = Ab(f_CB, 2048)[:, mb * TC:(mb + 1) * TC]
                mms.append(lambda t, b=bi, l=lhsT, r=rhs, sp=(mb == 7):
                           t.matmul(PB[b][:, 0:TC], l, r, start=False, stop=sp))
            t_g = tmm_group(mms)
            kind = gb // 8
            mb8 = gb % 8
            dst = [f_F, f_I, f_G][kind] + mb8 * TC
            func = AF.Tanh if kind == 2 else AF.Sigmoid
            wait("scalar", s_t, t_g)
            s_gate_last = ems(lambda s, b=bi, d=A(dst, TC),
                              bb2=BGC[:, gb:gb + 1], fn=func:
                              s.activation(d, PB[b][:, 0:TC], fn, bias=bb2)
                              .then_inc(s_s, 1))
            bank_free[bi] = ("s_s", s_gate_last)

        wait("vector", s_s, s_gate_last)
        for mb in range(8):
            fsl = A(f_F + mb * TC, TC)
            isl = A(f_I + mb * TC, TC)
            gsl = A(f_G + mb * TC, TC)
            csl = A(f_C + mb * TC, TC)
            emv(lambda v, f=fsl, c2=csl: v.tensor_tensor(f, f, c2, ALU.mult)
                .then_inc(s_v, 1))
            emv(lambda v, i=isl, g2=gsl: v.tensor_tensor(g2, i, g2, ALU.mult)
                .then_inc(s_v, 1))
            emv(lambda v, c2=csl, f=fsl, g2=gsl:
                v.tensor_tensor(c2, f, g2, ALU.add).then_inc(s_v, 1))
        v_cstep[j] = emv(lambda v: v.tensor_copy(Ab(f_CB, 2048), A(f_C, 2048))
                         .then_inc(s_v, 1))
        if j == 0:
            dump("d_c0", A(f_C, 2048), [("s_v", v_cstep[j])])

    wait("gpsimd", s_s, P.val("s_s"))
    wait("gpsimd", s_v, P.val("s_v"))
    g_uo2 = gdma(uoT.rearrange("(blk p) o -> p blk o", p=128),
                 Ab(f_UO, 8 * DM).rearrange("p (blk o) -> p blk o", o=DM))

    v_cF = v_cstep[B - 1]
    v_ln2 = 0
    d_out_prev = 0
    t_xp_fin_prev = t_xp[B - 1]
    for j in range(B):
        wait("sync", s_t, t_xp_fin_prev)
        if v_ln2:
            wait("sync", s_v, v_ln2)
        d_x = load_xt(j)
        buf = j % 2
        t_xp_fin_prev, s_xpf = transpose_xt(j, buf, d_x)

        wait("tensor", s_g, g_uo2)
        wait("tensor", s_s, s_xpf)
        wait("tensor", s_v, v_cF)
        s_tanh_last = 0
        for pb2 in range(2):
            for hf in range(2):
                bi = 5 + hf
                t_wait_bank(bi)
                mms = []
                for mb in range(8):
                    lhsT = Ab(f_XTT + buf * 1024, 2048)[
                        :, mb * TC + pb2 * 128: mb * TC + pb2 * 128 + 128]
                    rhs = Ab(f_WO, 8 * DM)[:, mb * DM + hf * 512: mb * DM + (hf + 1) * 512]
                    mms.append(lambda t, b=bi, l=lhsT, r=rhs, st=(mb == 0):
                               t.matmul(PB[b][:, :], l, r, start=st, stop=False))
                for mb in range(8):
                    lhsT = Ab(f_CB, 2048)[:, mb * TC + pb2 * 128: mb * TC + pb2 * 128 + 128]
                    rhs = Ab(f_UO, 8 * DM)[:, mb * DM + hf * 512: mb * DM + (hf + 1) * 512]
                    mms.append(lambda t, b=bi, l=lhsT, r=rhs:
                               t.matmul(PB[b][:, :], l, r, start=False, stop=False))
                mms.append(lambda t, b=bi, h=hf:
                           t.matmul(PB[b][:, :], ONE[:, 0:128],
                                    BOR[:, h * 512:(h + 1) * 512],
                                    start=False, stop=True))
                t_h = tmm_group(mms)
                wait("scalar", s_t, t_h)
                if d_out_prev and pb2 == 0 and hf == 0:
                    wait("scalar", s_o, d_out_prev)
                s_tanh_last = ems(lambda s, b=bi,
                                  d=A(f_H + pb2 * DM + hf * 512, 512):
                                  s.activation(d, PB[b][:, :], AF.Tanh)
                                  .then_inc(s_s, 1))
                bank_free[bi] = ("s_s", s_tanh_last)

        if j == 0:
            dump("d_hf0", A(f_H, 2048), [("s_s", s_tanh_last)])
        wait("vector", s_s, s_tanh_last)
        wait("vector", s_d, d_x)
        rs2 = o_MISC + 24
        for tb in range(2):
            for hf in range(2):
                emv(lambda v, d=A(f_H + tb * DM + hf * 512, 512),
                    x=A(f_X32 + tb * DM + hf * 512, 512),
                    rc=A(rs2 + tb * 2 + hf, 1):
                    v.scalar_tensor_tensor(d, d, 1.0, x, ALU.mult, ALU.add,
                                           accum_out=rc).then_inc(s_v, 1))
        o_st2 = o_MISC + 32
        for tb in range(2):
            mcol = A(o_st2 + tb * 4 + 0, 1)
            vcol = A(o_st2 + tb * 4 + 1, 1)
            rcol = A(o_st2 + tb * 4 + 2, 1)
            ncol = A(o_st2 + tb * 4 + 3, 1)
            tcol = A(o_MISC + 40, 1)
            emv(lambda v, m=mcol, a=A(rs2 + tb * 2, 1), b2=A(rs2 + tb * 2 + 1, 1):
                v.tensor_tensor(m, a, b2, ALU.add).then_inc(s_v, 1))
            v_m = emv(lambda v, m=mcol: v.tensor_scalar_mul(m, m, 1.0 / DM)
                      .then_inc(s_v, 1))
            wait("scalar", s_v, v_m)
            s_sq = ems(lambda s, jk=A(f_F + tb * DM, DM), r=A(f_H + tb * DM, DM),
                       q=vcol:
                       s.activation(jk, r, AF.Square, scale=1.0 / (DM ** 0.5),
                                    accum_out=q).then_inc(s_s, 1))
            wait("vector", s_s, s_sq)
            emv(lambda v, t2=tcol, m=mcol: v.tensor_tensor(t2, m, m, ALU.mult)
                .then_inc(s_v, 1))
            v_var = emv(lambda v, q=vcol, t2=tcol:
                        v.tensor_tensor(q, q, t2, ALU.subtract).then_inc(s_v, 1))
            wait("scalar", s_v, v_var)
            ems(lambda s, q=vcol: s.activation(q, q, AF.Ln, bias=A(o_MISC + 48, 1))
                .then_inc(s_s, 1))
            s_rsc = ems(lambda s, q=vcol, r=rcol:
                        s.activation(r, q, AF.Exp, scale=-0.5).then_inc(s_s, 1))
            wait("vector", s_s, s_rsc)
            emv(lambda v, n=ncol, m=mcol, r=rcol:
                v.tensor_scalar(n, m, r, -1.0, ALU.mult, ALU.mult).then_inc(s_v, 1))
            emv(lambda v, d=A(f_F + tb * DM, DM), r=A(f_H + tb * DM, DM),
                rc=rcol, n=ncol:
                v.tensor_scalar(d, r, rc, n, ALU.mult, ALU.add).then_inc(s_v, 1))
        v_ln2 = P.val("s_v")

        wait("sync", s_v, v_ln2)
        d_out_prev = sdma(A(f_F, 2 * DM).rearrange("p (tb m) -> p tb m", tb=2),
                          out[j, :, :].rearrange("(tb p) m -> p tb m", p=128),
                          "s_o")

    wait("sync", s_d, P.val("s_d"))
    wait("sync", s_o, P.val("s_o"))

    @block.gpsimd
    def _(g):
        for fn in prog["gpsimd"]:
            fn(g)

    @block.sync
    def _(e):
        for fn in prog["sync"]:
            fn(e)

    @block.vector
    def _(v):
        for fn in prog["vector"]:
            fn(v)

    @block.scalar
    def _(s):
        for fn in prog["scalar"]:
            fn(s)

    @block.tensor
    def _(t):
        for fn in prog["tensor"]:
            fn(t)

    es.close()
    return nc


_NC_CACHE = {}


def kernel(L0_attn_x, L0_attn_W_in, L0_attn_conv_w, L0_attn_conv_b, L0_attn_A_log,
           L0_attn_D, L0_attn_W_x, L0_attn_W_dt, L0_attn_b_dt, L0_attn_W_out,
           L0_ffn_wf, L0_ffn_uf, L0_ffn_bf, L0_ffn_wi, L0_ffn_ui, L0_ffn_bi,
           L0_ffn_wc, L0_ffn_uc, L0_ffn_bc, L0_ffn_wo, L0_ffn_uo, L0_ffn_bo,
           L0_ffn_in_c):
    x = np.asarray(L0_attn_x, dtype=np.float32)

    def T(a):
        return np.ascontiguousarray(np.asarray(a, dtype=np.float32).T)

    def colv(a):
        v = np.asarray(a, dtype=np.float32).reshape(-1)
        return np.ascontiguousarray(v.reshape(-1, 128).T)

    shared = {
        "winT": T(L0_attn_W_in),
        "wxT": T(L0_attn_W_x),
        "wdtT": T(L0_attn_W_dt),
        "woutT": T(L0_attn_W_out),
        "convw": np.ascontiguousarray(
            np.asarray(L0_attn_conv_w, dtype=np.float32)
            .reshape(16, 128, 4).transpose(1, 0, 2).reshape(128, 64)),
        "convb": colv(L0_attn_conv_b),
        "bdt": colv(L0_attn_b_dt),
        "alog": np.ascontiguousarray(
            np.asarray(L0_attn_A_log, dtype=np.float32).reshape(1, -1)),
        "dvec": colv(L0_attn_D),
        "wgT": np.ascontiguousarray(np.concatenate(
            [np.asarray(L0_ffn_wf), np.asarray(L0_ffn_wi),
             np.asarray(L0_ffn_wc)], axis=0).T.astype(np.float32)),
        "ugT": np.ascontiguousarray(np.concatenate(
            [np.asarray(L0_ffn_uf), np.asarray(L0_ffn_ui),
             np.asarray(L0_ffn_uc)], axis=0).T.astype(np.float32)),
        "woT": T(L0_ffn_wo),
        "uoT": T(L0_ffn_uo),
        "bg": colv(np.concatenate(
            [np.asarray(L0_ffn_bf), np.asarray(L0_ffn_bi),
             np.asarray(L0_ffn_bc)], axis=0)),
        "bo": np.ascontiguousarray(
            np.asarray(L0_ffn_bo, dtype=np.float32).reshape(1, -1)),
        "incv": colv(L0_ffn_in_c),
        "eye": np.eye(128, dtype=np.float32),
    }

    in_maps = []
    for j in range(NCORE):
        m = dict(shared)
        m["xT"] = np.ascontiguousarray(x[j].T)
        m["xn"] = np.ascontiguousarray(x[j])
        in_maps.append(m)

    if "nc" not in _NC_CACHE:
        _NC_CACHE["nc"] = build_nc()
    nc = _NC_CACHE["nc"]

    res = run_bass_kernel_spmd(nc, in_maps, core_ids=list(range(NCORE)))

    full = np.empty((B, L, DM), dtype=np.float32)
    for i in range(NCORE):
        full[:, i * POS:(i + 1) * POS, :] = res.results[i]["out"]
    return full


# version 11
# speedup vs baseline: 1.2537x; 1.0275x over previous
import numpy as np
from contextlib import ExitStack

import concourse.bass as bass
import concourse.mybir as mybir
from concourse.bass_utils import run_bass_kernel_spmd
from concourse import library_config

F32 = mybir.dt.float32
BF16 = mybir.dt.bfloat16
AF = mybir.ActivationFunctionType
ALU = mybir.AluOpType

B = 8
L = 2048
DM = 1024
DI = 2048
DS = 16
RK = 64
TC = 256
NCH = L // TC
NDB = DI // 128
POS = L // 8
G3 = 3 * DM
EPS = 1e-5
NCORE = 8
MEGA = NDB * TC
XRM = NDB * 260


class Plan:
    def __init__(self):
        self.c = {}

    def inc(self, name, amt):
        self.c[name] = self.c.get(name, 0) + amt
        return self.c[name]

    def val(self, name):
        return self.c.get(name, 0)


DEBUG = False


def build_nc():
    nc = bass.Bass("TRN2", target_bir_lowering=False, debug=False)

    def din(name, shape):
        return nc.dram_tensor(name, shape, F32, kind="ExternalInput").ap()

    xT = din("xT", [DM, L])
    xn = din("xn", [L, DM])
    winT = din("winT", [DM, 2 * DI])
    wxT = din("wxT", [DI, RK + 2 * DS])
    wdtT = din("wdtT", [RK, DI])
    woutT = din("woutT", [DI, DM])
    convw = din("convw", [128, 64])
    convb = din("convb", [128, 16])
    bdt = din("bdt", [128, 16])
    alog = din("alog", [1, DS])
    dvec = din("dvec", [128, 16])
    wgT = din("wgT", [DM, G3])
    ugT = din("ugT", [DM, G3])
    woT = din("woT", [DM, DM])
    uoT = din("uoT", [DM, DM])
    bg = din("bg", [128, 24])
    bo = din("bo", [1, DM])
    incv = din("incv", [128, 8])
    eye = din("eye", [128, 128])

    out = nc.dram_tensor("out", [B, POS, DM], F32, kind="ExternalOutput").ap()
    dbg = {}
    if DEBUG:
        for nm, shp in [("d_xr", [128, XRM]), ("d_xs", [128, MEGA]),
                        ("d_mdt", [128, MEGA]), ("d_xdbl", [96, TC]),
                        ("d_w0", [128, MEGA]), ("d_a0", [128, MEGA]),
                        ("d_h0", [128, MEGA]), ("d_yacc", [128, MEGA]),
                        ("d_ln1", [128, 2 * DM]), ("d_ag0", [8 * TC, DM]),
                        ("d_c0", [128, 2048]), ("d_hf0", [128, 2048]),
                        ("d_acol", [128, 16]), ("d_arw", [1, 16])]:
            dbg[nm] = nc.dram_tensor(nm, shp, F32, kind="ExternalOutput").ap()
    agin = nc.dram_tensor("agin", [L, DM], F32)
    agall = nc.dram_tensor("agall", [NCH, 8 * TC, DM], F32,
                           addr_space="Shared")

    es = ExitStack()

    AR_COLS = 47552
    AR = es.enter_context(nc.sbuf_tensor("AR", [128, AR_COLS], F32))
    _ptr = [0]

    def areg(n):
        o = _ptr[0]
        _ptr[0] += n
        assert _ptr[0] <= AR_COLS, f"arena overflow {_ptr[0]}"
        return o

    o_XR = areg(XRM)
    o_U = areg(MEGA)
    o_A1 = areg(MEGA)
    o_A2 = areg(MEGA)
    o_A4 = areg(MEGA)
    o_A5 = areg(MEGA)
    o_A8 = areg(MEGA)
    o_SZ = areg(MEGA // 2)
    o_B1 = areg(MEGA // 2)
    o_XTB = areg(2 * 1024)
    o_XN = areg(2 * DM)
    o_R = areg(2 * DM)
    o_WSL = areg(2 * 2048)
    o_WOS = areg(2 * 512)
    o_WXT = areg(768)
    o_WDT = areg(1024)
    o_XD = areg(256)
    o_XDB = areg(256)
    o_RP = areg(2 * TC)
    o_ST = areg(256)
    o_CL = areg(160)
    o_EYE = areg(128)
    o_MISC = areg(64)
    mamba_top = _ptr[0]

    _fp = [0]

    def freg(n):
        o = _fp[0]
        _fp[0] += n
        assert _fp[0] <= mamba_top, f"ffn overflow {_fp[0]} > {mamba_top}"
        return o

    f_WG = freg(12288)
    f_UG = freg(12288)
    f_WO = freg(4096)
    f_XTT = freg(2 * 1024)
    f_C = freg(2048)
    f_CB = freg(1024)
    f_F = freg(2048)
    f_I = freg(2048)
    f_G = freg(2048)
    f_X32 = freg(2048)
    f_H = freg(2048)
    f_UO = f_I

    BGC = es.enter_context(nc.sbuf_tensor("BGC", [128, 24], F32))
    INC = es.enter_context(nc.sbuf_tensor("INC", [128, 8], F32))
    BOR = es.enter_context(nc.sbuf_tensor("BOR", [1, DM], F32))
    ONE = es.enter_context(nc.sbuf_tensor("ONE", [1, 128], F32))
    ALG = es.enter_context(nc.sbuf_tensor("ALG", [1, DS], F32))
    ARW = es.enter_context(nc.sbuf_tensor("ARW", [1, DS], F32))

    PB = [es.enter_context(nc.psum_tensor(f"PB{i}", [128, 512], F32))
          for i in range(8)]

    s_d = es.enter_context(nc.semaphore("s_d"))
    s_k = es.enter_context(nc.semaphore("s_k"))
    s_o = es.enter_context(nc.semaphore("s_o"))
    s_b = [es.enter_context(nc.semaphore("s_b0")),
           es.enter_context(nc.semaphore("s_b1"))]
    s_g = es.enter_context(nc.semaphore("s_g"))
    s_gx = [es.enter_context(nc.semaphore("s_gx0")),
            es.enter_context(nc.semaphore("s_gx1"))]
    s_gw = [es.enter_context(nc.semaphore("s_gw0")),
            es.enter_context(nc.semaphore("s_gw1"))]
    s_go = [es.enter_context(nc.semaphore("s_go0")),
            es.enter_context(nc.semaphore("s_go1"))]
    s_t = es.enter_context(nc.semaphore("s_t"))
    s_v = es.enter_context(nc.semaphore("s_v"))
    s_s = es.enter_context(nc.semaphore("s_s"))
    s_p = es.enter_context(nc.semaphore("s_p"))
    s_c = es.enter_context(nc.semaphore("s_c"))
    sem_map = {"s_v": s_v, "s_s": s_s, "s_d": s_d, "s_g": s_g,
               "s_t": s_t, "s_p": s_p, "s_c": s_c,
               "s_k": s_k, "s_o": s_o, "s_b0": s_b[0], "s_b1": s_b[1],
               "s_gx0": s_gx[0], "s_gx1": s_gx[1],
               "s_gw0": s_gw[0], "s_gw1": s_gw[1],
               "s_go0": s_go[0], "s_go1": s_go[1]}

    block = es.enter_context(nc.Block())
    P = Plan()
    prog = {e: [] for e in ("sync", "gpsimd", "vector", "scalar", "tensor")}

    def em(eng, fn):
        prog[eng].append(fn)

    def wait(eng, sem, v):
        if v > 0:
            em(eng, lambda e, sem=sem, v=v: e.wait_ge(sem, v))

    def A(o, n, p0=0, pn=128):
        return AR[p0:p0 + pn, o:o + n]

    def Ab(o, nbf, p0=0, pn=128):
        return AR[p0:p0 + pn, :].bitcast(BF16)[:, 2 * o:2 * o + nbf]

    def sdma(src_ap, dst_ap, semn="s_d"):
        sem = sem_map[semn]
        em("sync", lambda e, s=src_ap, d=dst_ap, m=sem:
           e.dma_start(out=d, in_=s).then_inc(m, 16))
        return P.inc(semn, 16)

    def gdma(src_ap, dst_ap, semn="s_g"):
        sem = sem_map[semn]
        em("gpsimd", lambda g, s=src_ap, d=dst_ap, m=sem:
           g.dma_start(out=d, in_=s).then_inc(m, 16))
        return P.inc(semn, 16)

    def emv(fn):
        wait("vector", s_v, P.val("s_v"))
        em("vector", fn)
        return P.inc("s_v", 1)

    def ems(fn):
        wait("scalar", s_s, P.val("s_s"))
        em("scalar", fn)
        return P.inc("s_s", 1)

    def emp(fn):
        wait("gpsimd", s_p, P.val("s_p"))
        em("gpsimd", fn)
        return P.inc("s_p", 1)

    def tmm_group(mms, bank_ap=None):
        for fn in mms[:-1]:
            em("tensor", fn)
        em("tensor", lambda t, f=mms[-1]: f(t).then_inc(s_t, 1))
        return P.inc("s_t", 1)

    def dump(nm, src_ap, waits=()):
        if not DEBUG or nm not in dbg:
            return
        for eng_sem, v in waits:
            wait("sync", sem_map[eng_sem], v)
        sdma(src_ap, dbg[nm], "s_o")
        dv = P.val("s_o")
        wait("sync", s_o, dv)
        for e2 in ("vector", "scalar", "gpsimd", "tensor"):
            wait(e2, s_o, dv)

    bank_free = [("s_v", 0)] * 8

    def t_wait_bank(bi):
        sem, v = bank_free[bi]
        wait("tensor", sem_map[sem], v)


    o_cw = o_CL
    o_cb = o_CL + 64
    o_bd = o_CL + 80
    o_dv = o_CL + 96
    o_ac = o_CL + 112
    o_nb = o_CL + 128

    sdma(convw, A(o_cw, 64), "s_k")
    sdma(convb, A(o_cb, 16), "s_k")
    sdma(bdt, A(o_bd, 16), "s_k")
    sdma(dvec, A(o_dv, 16), "s_k")
    sdma(alog, ALG[:, :], "s_k")
    sdma(eye, A(o_EYE, 128), "s_k")
    sdma(bg, BGC[:, :], "s_k")
    sdma(incv, INC[:, :], "s_k")
    d_cst = sdma(bo, BOR[:, :], "s_k")

    wait("vector", s_k, d_cst)
    emv(lambda v: v.tensor_scalar_mul(A(o_nb, 16), A(o_bd, 16), -1.0).then_inc(s_v, 1))
    v_ones = emv(lambda v: v.memset(ONE[:, :], 1.0).then_inc(s_v, 1))
    emv(lambda v: v.memset(A(o_ST, 256), 0.0).then_inc(s_v, 1))
    emv(lambda v: v.memset(A(o_RP, 128), 1.0).then_inc(s_v, 1))
    emv(lambda v: v.memset(A(o_MISC + 48, 1), EPS).then_inc(s_v, 1))
    emv(lambda v: v.memset(A(o_XR, XRM), 0.0).then_inc(s_v, 1))

    wait("scalar", s_k, d_cst)
    v_arow = ems(lambda s: s.activation(ARW[:, :], ALG[:, :], AF.Exp).then_inc(s_s, 1))
    wait("tensor", s_s, v_arow)
    wait("tensor", s_v, v_ones)
    t_ac = tmm_group([lambda t: t.matmul(PB[0][:, 0:DS], ONE[:, 0:128],
                                         ARW[:, :], start=True, stop=True)])
    wait("vector", s_t, t_ac)
    p_acol_v = emv(lambda v: v.tensor_copy(A(o_ac, 16), PB[0][:, 0:DS])
                   .then_inc(s_v, 1))
    bank_free[0] = ("s_v", p_acol_v)
    dump("d_acol", A(o_ac, 16), [("s_v", p_acol_v)])
    dump("d_arw", ARW[:, :], [("s_s", v_arow)])

    winT_r = winT.rearrange("(blk p) o -> p blk o", p=128)
    woutT_r = woutT.rearrange("(blk p) o -> p blk o", p=128)
    xT_r = xT.rearrange("(blk p) t -> p blk t", p=128)

    g_xtb = [0, 0]
    g_xtb[0] = gdma(xT_r[:, :, 0:TC],
                    Ab(o_XTB, 2048).rearrange("p (blk t) -> p blk t", t=TC),
                    "s_gx0")

    g_wx = gdma(wxT.rearrange("(blk p) o -> p blk o", p=128),
                Ab(o_WXT, 1536).rearrange("p (blk o) -> p blk o", o=96))
    g_wdt = gdma(wdtT, Ab(o_WDT, 2048, 0, 64))

    sz_done_map = {cc2: [0] * NDB for cc2 in range(NCH)}
    v_yz_by_chunk = {}

    def emit_xz_slice_load(c2, sl):
        buf = sl % 2
        tprev = t_wsl_done.get((c2, sl - 2), t_wsl_done.get((c2 - 1, sl + 6), 0))
        wait("gpsimd", s_t, tprev)
        g_sl = gdma(winT_r[:, :, sl * 512:(sl + 1) * 512],
                    Ab(o_WSL + buf * 2048, 4096)
                    .rearrange("p (blk o) -> p blk o", o=512),
                    "s_gw" + str(buf))
        g_wsl_val[(c2, sl)] = g_sl

    def emit_xz_group_mm(c2, dout):
        sl = dout // 4
        buf = sl % 2
        bi = dout % 2
        wait("tensor", s_gw[buf], g_wsl_val[(c2, sl)])
        wait("tensor", s_gx[c2 % 2], g_xtb[c2 % 2])
        t_wait_bank(bi)
        db = dout % 4
        mms = []
        for mb in range(8):
            lhsT = Ab(o_WSL + buf * 2048, 4096)[
                :, mb * 512 + db * 128: mb * 512 + (db + 1) * 128]
            rhs = Ab(o_XTB + (c2 % 2) * 1024, 2048)[:, mb * TC:(mb + 1) * TC]
            mms.append(lambda t, o=PB[bi][:, 0:TC], l=lhsT, r=rhs,
                       st=(mb == 0), sp=(mb == 7):
                       t.matmul(o, l, r, start=st, stop=sp))
        t_mm = tmm_group(mms)
        if dout % 4 == 3:
            t_wsl_done[(c2, sl)] = P.val("s_t")
        return t_mm, bi

    def emit_xr_evac(c2, dout, t_mm, bi):
        wait("vector", s_t, t_mm)
        v_ev = emv(lambda v, ps=PB[bi][:, 0:TC],
                   d=A(o_XR + dout * 260 + 3, TC):
                   v.tensor_copy(d, ps).then_inc(s_v, 1))
        bank_free[bi] = ("s_v", v_ev)

    def emit_z_evac(c2, dout, t_mm, bi):
        zb = dout - 16
        wait("scalar", s_t, t_mm)
        if zb == 0 and v_yz_by_chunk.get(c2 - 1):
            wait("scalar", s_v, v_yz_by_chunk[c2 - 1])
        sz_done_map[c2][zb] = ems(
            lambda s, ps=PB[bi][:, 0:TC],
            d=Ab(o_SZ, MEGA)[:, zb * TC:(zb + 1) * TC]:
            s.activation(d, ps, AF.Silu).then_inc(s_s, 1))
        bank_free[bi] = ("s_s", sz_done_map[c2][zb])

    g_wsl_val = {}

    v_ln1_done = 0
    v_yz_prev = 0
    t_wsl_done = {}
    t_wos_done = {}
    t_ylast = 0
    v_acc_done = {}
    v_scan_done = {}
    v_scan_half = {}
    p_w_hist = {}
    v_chat_hist = {}
    s_a_done = {}
    d_agin_prev = 0
    cc_round = [0] * NCH

    for c in range(NCH):
        tc0 = c * TC

        wait("sync", s_v, v_ln1_done)
        if d_agin_prev:
            wait("sync", s_d, d_agin_prev)
        d_xn = sdma(xn[tc0:tc0 + TC, :].rearrange("(tb p) m -> p tb m", p=128),
                    A(o_XN, 2 * DM).rearrange("p (tb m) -> p tb m", tb=2))

        s_sz_done = sz_done_map[c]
        if c == 0:
            for sl in range(4):
                emit_xz_slice_load(0, sl)
                for db in range(4):
                    t_mm, bi = emit_xz_group_mm(0, sl * 4 + db)
                    emit_xr_evac(0, sl * 4 + db, t_mm, bi)
            for sl in range(4, 8):
                emit_xz_slice_load(0, sl)
                for db in range(4):
                    t_mm, bi = emit_xz_group_mm(0, sl * 4 + db)
                    emit_z_evac(0, sl * 4 + db, t_mm, bi)

        if c == 0:
            dump("d_xr", A(o_XR, XRM), [("s_v", P.val("s_v"))])
        for db in range(NDB):
            slot = o_XR + db * 260
            xcd = A(o_A4 + db * TC, TC)
            emv(lambda v, d=xcd, s=A(slot, TC), w=A(o_cw + db * 4, 1),
                b=A(o_cb + db, 1):
                v.tensor_scalar(d, s, w, b, ALU.mult, ALU.add).then_inc(s_v, 1))
            for j in range(1, 4):
                emv(lambda v, d=xcd, s=A(slot + j, TC), w=A(o_cw + db * 4 + j, 1):
                    v.scalar_tensor_tensor(d, s, w, d, ALU.mult, ALU.add)
                    .then_inc(s_v, 1))
        xr_r0 = A(o_XR, XRM).rearrange("p (blk w) -> p blk w", w=260)
        emv(lambda v, s=xr_r0[:, :, 256:259], d=xr_r0[:, :, 0:3]:
            v.tensor_copy(d, s).then_inc(s_v, 1))
        v_conv = P.val("s_v")

        wait("scalar", s_v, v_conv)

        s_xs = ems(lambda s: s.activation(A(o_A2, MEGA), A(o_A4, MEGA), AF.Silu)
                   .then_inc(s_s, 1))
        if c == 0:
            dump("d_xs", A(o_A2, MEGA), [("s_s", s_xs)])
        wait("gpsimd", s_s, s_xs)
        if t_ylast:
            wait("gpsimd", s_t, t_ylast)
        p_xsb = emp(lambda g: g.tensor_copy(Ab(o_B1, MEGA), A(o_A2, MEGA))
                    .then_inc(s_p, 1))

        wait("tensor", s_p, p_xsb)
        if c == 0:
            wait("tensor", s_g, g_wdt)
        t_wait_bank(4)
        mms = []
        for dib in range(NDB):
            lhsT = Ab(o_WXT, 1536)[:, dib * 96:(dib + 1) * 96]
            rhs = Ab(o_B1, MEGA)[:, dib * TC:(dib + 1) * TC]
            mms.append(lambda t, l=lhsT, r=rhs, st=(dib == 0), sp=(dib == NDB - 1):
                       t.matmul(PB[4][0:96, 0:TC], l, r, start=st, stop=sp))
        t_wx = tmm_group(mms)

        wait("vector", s_t, t_wx)
        wait("vector", s_b[0], P.val("s_b0"))
        wait("vector", s_b[1], P.val("s_b1"))
        v_xd = emv(lambda v: v.tensor_copy(A(o_XD, TC, 0, 96), PB[4][0:96, 0:TC])
                   .then_inc(s_v, 1))
        bank_free[4] = ("s_v", v_xd)
        wait("scalar", s_v, v_xd)
        s_xdb = ems(lambda s: s.copy(Ab(o_XDB, 256, 0, 64), A(o_XD, TC, 0, 64))
                    .then_inc(s_s, 1))

        wait("tensor", s_s, s_xdb)
        if c == 0:
            wait("tensor", s_g, g_wdt)
        for db in range(NDB):
            bi = 4 + (db % 2)
            t_wait_bank(bi)
            mms = [lambda t, b=bi,
                   l=Ab(o_WDT, 2048, 0, 64)[:, db * 128:(db + 1) * 128],
                   r=Ab(o_XDB, 256, 0, 64):
                   t.matmul(PB[b][:, 0:TC], l, r, start=True, stop=True)]
            t_dt = tmm_group(mms)
            wait("scalar", s_t, t_dt)
            if db == 0 and v_acc_done.get((c - 1, DS - 1)):
                wait("scalar", s_v, v_acc_done[(c - 1, DS - 1)][1])
            s_sig = ems(lambda s, b=bi, d=A(o_A1 + db * TC, TC),
                        nb=A(o_nb + db, 1):
                        s.activation(d, PB[b][:, 0:TC], AF.Sigmoid,
                                     bias=nb, scale=-1.0).then_inc(s_s, 1))
            bank_free[bi] = ("s_s", s_sig)

        s_mdt = ems(lambda s: s.activation(A(o_A4, MEGA), A(o_A1, MEGA), AF.Ln)
                    .then_inc(s_s, 1))

        if c == 0:
            dump("d_mdt", A(o_A4, MEGA), [("s_s", s_mdt)])
            dump("d_xdbl", A(o_XD, TC, 0, 96), [("s_v", v_xd)])
        wait("vector", s_s, s_mdt)
        v_u = emv(lambda v: v.scalar_tensor_tensor(
            A(o_U, MEGA), A(o_A4, MEGA), -1.0, A(o_A2, MEGA),
            ALU.mult, ALU.mult).then_inc(s_v, 1))
        for db in range(NDB):
            emv(lambda v, d=A(o_A8 + db * TC, TC), s=A(o_A2 + db * TC, TC),
                w=A(o_dv + db, 1):
                v.tensor_scalar_mul(d, s, w).then_inc(s_v, 1))
        v_yacc0 = P.val("s_v")

        t_rep_hist = {}
        d_bc_hist = {}

        def stage_bc(k):
            bcb2 = o_WDT + (k % 2) * 512
            sdma(A(o_XD, TC, 64 + k, 1), A(bcb2, TC, 64, 1), "s_b" + str(k % 2))
            d_bc_hist[k] = sdma(A(o_XD, TC, 80 + k, 1), A(bcb2 + TC, TC, 64, 1),
                                "s_b" + str(k % 2))

        wait("sync", s_v, v_xd)
        stage_bc(0)
        HM = MEGA // 2
        if c + 1 < NCH:
            wait("gpsimd", s_t, t_wsl_done[(c, 7)])
            g_xtb[(c + 1) % 2] = gdma(
                xT_r[:, :, (c + 1) * TC:(c + 2) * TC],
                Ab(o_XTB + ((c + 1) % 2) * 1024, 2048)
                .rearrange("p (blk t) -> p blk t", t=TC),
                "s_gx" + str((c + 1) % 2))
        for k in range(DS):
            par = k % 2
            pbB, pbC = PB[2 * par + 2], PB[2 * par + 3] if False else PB[2 * par + 3]
            pbB = PB[2 * par + 2]
            pbC = PB[2 * par + 3]
            if k + 1 < DS:
                wait("sync", s_t, t_rep_hist.get(k - 1, 0))
                stage_bc(k + 1)
            bcb = o_WDT + par * 512
            wait("tensor", s_b[par], d_bc_hist[k])
            if k < 2:
                t_wait_bank(2 * par + 2)
                t_wait_bank(2 * par + 3)
            pwar = v_chat_hist.get((c, k - 2), v_chat_hist.get((c - 1, DS - 1), 0))
            if pwar:
                wait("tensor", s_v, pwar)
            t_rep = tmm_group([
                lambda t, r=A(bcb, TC, 64, 1), o1=A(o_RP, 128, 64, 1), pb=pbB:
                t.matmul(pb[:, 0:TC], o1, r, start=True, stop=True),
                lambda t, r=A(bcb + TC, TC, 64, 1), o1=A(o_RP, 128, 64, 1), pb=pbC:
                t.matmul(pb[:, 0:TC], o1, r, start=True, stop=True)])
            t_rep_hist[k] = t_rep

            s_ah = [0, 0]
            for hf2 in range(2):
                aw = v_scan_half.get((c, k - 1, hf2),
                                     v_scan_half.get((c - 1, DS - 1, hf2), 0))
                if aw:
                    wait("scalar", s_v, aw)
                wait("scalar", s_v, p_acol_v)
                s_ah[hf2] = ems(lambda s, sc=A(o_ac + k, 1),
                                d=A(o_A1 + hf2 * HM, HM),
                                src=A(o_A4 + hf2 * HM, HM):
                                s.activation(d, src, AF.Exp,
                                             scale=sc).then_inc(s_s, 1))
            s_a_done[(c, k)] = s_ah[1]

            bb = pbB[:, 0:TC].unsqueeze(1).broadcast_to([128, NDB, TC])
            uu = A(o_U, MEGA).rearrange("p (blk t) -> p blk t", t=TC)
            ww = A(o_A2, MEGA).rearrange("p (blk t) -> p blk t", t=TC)
            wait("vector", s_t, t_rep)
            emv(lambda v, w=ww, u=uu, b2=bb:
                v.tensor_tensor(w, u, b2, ALU.mult).then_inc(s_v, 1))

            for db in range(NDB):
                if db == 0:
                    wait("vector", s_s, s_ah[0])
                if db == NDB // 2:
                    wait("vector", s_s, s_ah[1])
                emv(lambda v, o=A(o_A5 + db * TC, TC),
                    a=A(o_A1 + db * TC, TC),
                    w=A(o_A2 + db * TC, TC), i=A(o_ST + k * 16 + db, 1):
                    v.tensor_tensor_scan(o, a, w, i, ALU.mult, ALU.add)
                    .then_inc(s_v, 1))
                if db == NDB // 2 - 1:
                    v_scan_half[(c, k, 0)] = P.val("s_v")
            v_scan_half[(c, k, 1)] = P.val("s_v")
            if c == 0 and k == 0:
                dump("d_a0", A(o_A1, MEGA), [("s_s", s_ah[1])])
                dump("d_w0", A(o_A2, MEGA), [("s_v", P.val("s_v"))])
                dump("d_h0", A(o_A5, MEGA), [("s_v", P.val("s_v"))])
            lastc = A(o_A5, MEGA).rearrange("p (blk t) -> p blk t", t=TC)[:, :, TC - 1:TC]
            v_scan_done[(c, k)] = emv(
                lambda v, lc=lastc, d=A(o_ST + k * 16, 16).unsqueeze(2):
                v.tensor_copy(d, lc).then_inc(s_v, 1))

            cb2 = pbC[:, 0:TC].unsqueeze(1).broadcast_to([128, NDB, TC])
            hh = A(o_A5, MEGA).rearrange("p (blk t) -> p blk t", t=TC)
            v_chat = emv(lambda v, o=hh, h=hh, c2=cb2:
                         v.tensor_tensor(o, h, c2, ALU.mult).then_inc(s_v, 1))
            v_chat_hist[(c, k)] = v_chat
            bank_free[2 * par + 2] = ("s_v", v_chat)
            bank_free[2 * par + 3] = ("s_v", v_chat)
            v_acc = emv(lambda v:
                        v.tensor_tensor(A(o_A8, MEGA), A(o_A8, MEGA),
                                        A(o_A5, MEGA), ALU.add).then_inc(s_v, 1))
            v_acc_done[(c, k)] = ("v", v_acc)

            if c + 1 < NCH:
                if k % 4 == 0:
                    emit_xz_slice_load(c + 1, k // 4)
                t_mmp, bip = emit_xz_group_mm(c + 1, k)
                emit_xr_evac(c + 1, k, t_mmp, bip)



        if c == 0:
            dump("d_yacc", A(o_A8, MEGA), [("s_v", P.val("s_v"))])
        wait("vector", s_s, max(s_sz_done))
        wait("vector", s_t, t_wx)
        v_yz = emv(lambda v: v.tensor_tensor(Ab(o_B1, MEGA), A(o_A8, MEGA),
                                             Ab(o_SZ, MEGA), ALU.mult)
                   .then_inc(s_v, 1))
        v_yz_prev = v_yz
        v_yz_by_chunk[c] = v_yz

        wait("tensor", s_v, v_yz)
        for bi in (4, 5, 6, 7):
            t_wait_bank(bi)
        t_y = {}
        for dib in range(NDB):
            tprev = t_wos_done.get((c, dib - 2),
                                   t_wos_done.get((c - 1, dib + 14), 0))
            wait("gpsimd", s_t, tprev)
            g_wo = gdma(woutT_r[:, dib, :], Ab(o_WOS + (dib % 2) * 512, 1024),
                        "s_go" + str(dib % 2))
            wait("tensor", s_go[dib % 2], g_wo)
            mms = []
            for tb in range(2):
                for hf in range(2):
                    bi = 4 + tb * 2 + hf
                    lhsT = Ab(o_B1, MEGA)[:, dib * TC + tb * 128:
                                          dib * TC + tb * 128 + 128]
                    rhs = Ab(o_WOS + (dib % 2) * 512, 1024)[:, hf * 512:(hf + 1) * 512]
                    mms.append(lambda t, b=bi, l=lhsT, r=rhs, st=(dib == 0),
                               sp=(dib == NDB - 1):
                               t.matmul(PB[b][:, :], l, r, start=st, stop=sp))
            t_wos_done[(c, dib)] = tmm_group(mms)
        t_ylast = P.val("s_t")

        rs_cols = o_MISC
        v_ev_by_bank = {}
        for tb in range(2):
            for hf in range(2):
                bi = 4 + tb * 2 + hf
                wait("vector", s_t, t_ylast)
                wait("vector", s_d, d_xn)
                v_ev = emv(lambda v, d=A(o_R + tb * DM + hf * 512, 512),
                           p=PB[bi][:, :], x=A(o_XN + tb * DM + hf * 512, 512),
                           rc=A(rs_cols + tb * 2 + hf, 1):
                           v.scalar_tensor_tensor(d, p, 1.0, x, ALU.mult,
                                                  ALU.add, accum_out=rc)
                           .then_inc(s_v, 1))
                bank_free[bi] = ("s_v", v_ev)

        o_stat = o_MISC + 8
        for tb in range(2):
            mcol = A(o_stat + tb * 4 + 0, 1)
            vcol = A(o_stat + tb * 4 + 1, 1)
            rcol = A(o_stat + tb * 4 + 2, 1)
            ncol = A(o_stat + tb * 4 + 3, 1)
            tcol = A(o_MISC + 16, 1)
            emv(lambda v, m=mcol, a=A(rs_cols + tb * 2, 1),
                b2=A(rs_cols + tb * 2 + 1, 1):
                v.tensor_tensor(m, a, b2, ALU.add).then_inc(s_v, 1))
            v_m = emv(lambda v, m=mcol: v.tensor_scalar_mul(m, m, 1.0 / DM)
                      .then_inc(s_v, 1))
            wait("scalar", s_v, v_m)
            s_sq = ems(lambda s, jk=A(o_A5 + tb * DM, DM),
                       r=A(o_R + tb * DM, DM), q=vcol:
                       s.activation(jk, r, AF.Square, scale=1.0 / (DM ** 0.5),
                                    accum_out=q).then_inc(s_s, 1))
            wait("vector", s_s, s_sq)
            emv(lambda v, t2=tcol, m=mcol:
                v.tensor_tensor(t2, m, m, ALU.mult).then_inc(s_v, 1))
            v_var = emv(lambda v, q=vcol, t2=tcol:
                        v.tensor_tensor(q, q, t2, ALU.subtract).then_inc(s_v, 1))
            wait("scalar", s_v, v_var)
            ems(lambda s, q=vcol: s.activation(q, q, AF.Ln, bias=A(o_MISC + 48, 1))
                .then_inc(s_s, 1))
            s_rsc = ems(lambda s, q=vcol, r=rcol:
                        s.activation(r, q, AF.Exp, scale=-0.5).then_inc(s_s, 1))
            wait("vector", s_s, s_rsc)
            emv(lambda v, n=ncol, m=mcol, r=rcol:
                v.tensor_scalar(n, m, r, -1.0, ALU.mult, ALU.mult).then_inc(s_v, 1))
            emv(lambda v, d=A(o_XN + tb * DM, DM), r=A(o_R + tb * DM, DM),
                rc=rcol, n=ncol:
                v.tensor_scalar(d, r, rc, n, ALU.mult, ALU.add).then_inc(s_v, 1))
        v_ln1_done = P.val("s_v")

        wait("sync", s_v, v_ln1_done)
        d_agin_prev = sdma(A(o_XN, 2 * DM).rearrange("p (tb m) -> p tb m", tb=2),
                           agin[tc0:tc0 + TC, :]
                           .rearrange("(tb p) m -> p tb m", p=128))
        wait("gpsimd", s_d, d_agin_prev)
        em("gpsimd", lambda g, c2=c: g.collective_compute(
            "AllGather", ALU.bypass,
            replica_groups=[list(range(NCORE))],
            ins=[agin[c2 * TC:(c2 + 1) * TC, :].opt()],
            outs=[agall[c2, :, :].opt()]).then_inc(s_c, 1))
        cc_round[c] = P.inc("s_c", 1)

        if c + 1 < NCH:
            for sl in range(4, 8):
                emit_xz_slice_load(c + 1, sl)
                for db in range(4):
                    t_mmz, biz = emit_xz_group_mm(c + 1, sl * 4 + db)
                    emit_z_evac(c + 1, sl * 4 + db, t_mmz, biz)

        if c == 0:
            dump("d_ln1", A(o_XN, 2 * DM), [("s_v", v_ln1_done)])
            if DEBUG:
                wait("sync", s_c, cc_round[0])
                sdma(agall[0, :, :], dbg["d_ag0"], "s_o")

    v_md = P.val("s_v")
    s_md = P.val("s_s")
    t_md = P.val("s_t")
    wait("gpsimd", s_v, v_md)
    wait("gpsimd", s_s, s_md)
    wait("gpsimd", s_t, t_md)

    g_wg = gdma(wgT.rearrange("(blk p) o -> p blk o", p=128),
                Ab(f_WG, 8 * G3).rearrange("p (blk o) -> p blk o", o=G3))
    g_ug = gdma(ugT.rearrange("(blk p) o -> p blk o", p=128),
                Ab(f_UG, 8 * G3).rearrange("p (blk o) -> p blk o", o=G3))
    g_wo2 = gdma(woT.rearrange("(blk p) o -> p blk o", p=128),
                 Ab(f_WO, 8 * DM).rearrange("p (blk o) -> p blk o", o=DM))

    wait("vector", s_v, v_md)
    for mb in range(8):
        emv(lambda v, d=A(f_C + mb * TC, TC), i=INC[:, mb:mb + 1]:
            v.tensor_scalar(d, A(o_ST, TC), 0.0, i, ALU.mult, ALU.add)
            .then_inc(s_v, 1))
    v_cup = emv(lambda v: v.tensor_copy(Ab(f_CB, 2048), A(f_C, 2048))
                .then_inc(s_v, 1))

    agall_r = agall.rearrange("r (j2 tb p) m -> r j2 p tb m", tb=2, p=128)
    pid_holder = {}

    def get_pid(e):
        if "v" not in pid_holder:
            pid_holder["v"] = e.partition_id()
        return pid_holder["v"]

    wait("sync", s_c, cc_round[NCH - 1])

    def load_xt(j):
        def dof(e, j=j):
            pid = get_pid(e)
            e.dma_start(out=A(f_X32, 2 * DM).rearrange("p (tb m) -> p tb m", tb=2),
                        in_=agall_r[pid, j, :, :, :]).then_inc(s_d, 16)
        em("sync", dof)
        return P.inc("s_d", 16)

    def transpose_xt(j, buf, d_x):
        wait("tensor", s_d, d_x)
        s_last = 0
        for tb in range(2):
            for mb in range(8):
                src = A(f_X32 + tb * DM + mb * 128, 128)
                t_tr = tmm_group([lambda t, s2=src:
                                  t.transpose(PB[2][:, 0:128], s2, A(o_EYE, 128))])
                wait("scalar", s_t, t_tr)
                dstb = Ab(f_XTT + buf * 1024, 2048)[
                    :, mb * TC + tb * 128: mb * TC + tb * 128 + 128]
                s_last = ems(lambda s, d=dstb:
                             s.copy(d, PB[2][:, 0:128]).then_inc(s_s, 1))
                wait("tensor", s_s, s_last)
        return P.val("s_t"), s_last

    t_xp = {}
    s_xp = {}
    d_x32 = {}
    v_cstep = {}
    for j in range(B):
        wait("sync", s_t, t_xp.get(j - 1, 0))
        d_x32[j] = load_xt(j)
        buf = j % 2
        t_xp[j], s_xp[j] = transpose_xt(j, buf, d_x32[j])

        wait("tensor", s_g, g_wo2)
        wait("tensor", s_s, s_xp[j])
        wait("tensor", s_v, v_cup if j == 0 else v_cstep[j - 1])
        s_gate_last = 0
        GB_BANKS = (0, 1, 3, 4)
        for gb in range(24):
            bi = GB_BANKS[gb % 4]
            t_wait_bank(bi)
            mms = []
            for mb in range(8):
                lhsT = Ab(f_WG, 8 * G3)[:, mb * G3 + gb * 128: mb * G3 + (gb + 1) * 128]
                rhs = Ab(f_XTT + buf * 1024, 2048)[:, mb * TC:(mb + 1) * TC]
                mms.append(lambda t, b=bi, l=lhsT, r=rhs, st=(mb == 0):
                           t.matmul(PB[b][:, 0:TC], l, r, start=st, stop=False))
            for mb in range(8):
                lhsT = Ab(f_UG, 8 * G3)[:, mb * G3 + gb * 128: mb * G3 + (gb + 1) * 128]
                rhs = Ab(f_CB, 2048)[:, mb * TC:(mb + 1) * TC]
                mms.append(lambda t, b=bi, l=lhsT, r=rhs, sp=(mb == 7):
                           t.matmul(PB[b][:, 0:TC], l, r, start=False, stop=sp))
            t_g = tmm_group(mms)
            kind = gb // 8
            mb8 = gb % 8
            dst = [f_F, f_I, f_G][kind] + mb8 * TC
            func = AF.Tanh if kind == 2 else AF.Sigmoid
            wait("scalar", s_t, t_g)
            s_gate_last = ems(lambda s, b=bi, d=A(dst, TC),
                              bb2=BGC[:, gb:gb + 1], fn=func:
                              s.activation(d, PB[b][:, 0:TC], fn, bias=bb2)
                              .then_inc(s_s, 1))
            bank_free[bi] = ("s_s", s_gate_last)

        wait("vector", s_s, s_gate_last)
        for mb in range(8):
            fsl = A(f_F + mb * TC, TC)
            isl = A(f_I + mb * TC, TC)
            gsl = A(f_G + mb * TC, TC)
            csl = A(f_C + mb * TC, TC)
            emv(lambda v, f=fsl, c2=csl: v.tensor_tensor(f, f, c2, ALU.mult)
                .then_inc(s_v, 1))
            emv(lambda v, i=isl, g2=gsl: v.tensor_tensor(g2, i, g2, ALU.mult)
                .then_inc(s_v, 1))
            emv(lambda v, c2=csl, f=fsl, g2=gsl:
                v.tensor_tensor(c2, f, g2, ALU.add).then_inc(s_v, 1))
        v_cstep[j] = emv(lambda v: v.tensor_copy(Ab(f_CB, 2048), A(f_C, 2048))
                         .then_inc(s_v, 1))
        if j == 0:
            dump("d_c0", A(f_C, 2048), [("s_v", v_cstep[j])])

    wait("gpsimd", s_s, P.val("s_s"))
    wait("gpsimd", s_v, P.val("s_v"))
    g_uo2 = gdma(uoT.rearrange("(blk p) o -> p blk o", p=128),
                 Ab(f_UO, 8 * DM).rearrange("p (blk o) -> p blk o", o=DM))

    v_cF = v_cstep[B - 1]
    v_ln2 = 0
    d_out_prev = 0
    t_xp_fin_prev = t_xp[B - 1]
    for j in range(B):
        wait("sync", s_t, t_xp_fin_prev)
        if v_ln2:
            wait("sync", s_v, v_ln2)
        d_x = load_xt(j)
        buf = j % 2
        t_xp_fin_prev, s_xpf = transpose_xt(j, buf, d_x)

        wait("tensor", s_g, g_uo2)
        wait("tensor", s_s, s_xpf)
        wait("tensor", s_v, v_cF)
        s_tanh_last = 0
        for pb2 in range(2):
            for hf in range(2):
                bi = 5 + hf
                t_wait_bank(bi)
                mms = []
                for mb in range(8):
                    lhsT = Ab(f_XTT + buf * 1024, 2048)[
                        :, mb * TC + pb2 * 128: mb * TC + pb2 * 128 + 128]
                    rhs = Ab(f_WO, 8 * DM)[:, mb * DM + hf * 512: mb * DM + (hf + 1) * 512]
                    mms.append(lambda t, b=bi, l=lhsT, r=rhs, st=(mb == 0):
                               t.matmul(PB[b][:, :], l, r, start=st, stop=False))
                for mb in range(8):
                    lhsT = Ab(f_CB, 2048)[:, mb * TC + pb2 * 128: mb * TC + pb2 * 128 + 128]
                    rhs = Ab(f_UO, 8 * DM)[:, mb * DM + hf * 512: mb * DM + (hf + 1) * 512]
                    mms.append(lambda t, b=bi, l=lhsT, r=rhs:
                               t.matmul(PB[b][:, :], l, r, start=False, stop=False))
                mms.append(lambda t, b=bi, h=hf:
                           t.matmul(PB[b][:, :], ONE[:, 0:128],
                                    BOR[:, h * 512:(h + 1) * 512],
                                    start=False, stop=True))
                t_h = tmm_group(mms)
                wait("scalar", s_t, t_h)
                if d_out_prev and pb2 == 0 and hf == 0:
                    wait("scalar", s_o, d_out_prev)
                s_tanh_last = ems(lambda s, b=bi,
                                  d=A(f_H + pb2 * DM + hf * 512, 512):
                                  s.activation(d, PB[b][:, :], AF.Tanh)
                                  .then_inc(s_s, 1))
                bank_free[bi] = ("s_s", s_tanh_last)

        if j == 0:
            dump("d_hf0", A(f_H, 2048), [("s_s", s_tanh_last)])
        wait("vector", s_s, s_tanh_last)
        wait("vector", s_d, d_x)
        rs2 = o_MISC + 24
        for tb in range(2):
            for hf in range(2):
                emv(lambda v, d=A(f_H + tb * DM + hf * 512, 512),
                    x=A(f_X32 + tb * DM + hf * 512, 512),
                    rc=A(rs2 + tb * 2 + hf, 1):
                    v.scalar_tensor_tensor(d, d, 1.0, x, ALU.mult, ALU.add,
                                           accum_out=rc).then_inc(s_v, 1))
        o_st2 = o_MISC + 32
        for tb in range(2):
            mcol = A(o_st2 + tb * 4 + 0, 1)
            vcol = A(o_st2 + tb * 4 + 1, 1)
            rcol = A(o_st2 + tb * 4 + 2, 1)
            ncol = A(o_st2 + tb * 4 + 3, 1)
            tcol = A(o_MISC + 40, 1)
            emv(lambda v, m=mcol, a=A(rs2 + tb * 2, 1), b2=A(rs2 + tb * 2 + 1, 1):
                v.tensor_tensor(m, a, b2, ALU.add).then_inc(s_v, 1))
            v_m = emv(lambda v, m=mcol: v.tensor_scalar_mul(m, m, 1.0 / DM)
                      .then_inc(s_v, 1))
            wait("scalar", s_v, v_m)
            s_sq = ems(lambda s, jk=A(f_F + tb * DM, DM), r=A(f_H + tb * DM, DM),
                       q=vcol:
                       s.activation(jk, r, AF.Square, scale=1.0 / (DM ** 0.5),
                                    accum_out=q).then_inc(s_s, 1))
            wait("vector", s_s, s_sq)
            emv(lambda v, t2=tcol, m=mcol: v.tensor_tensor(t2, m, m, ALU.mult)
                .then_inc(s_v, 1))
            v_var = emv(lambda v, q=vcol, t2=tcol:
                        v.tensor_tensor(q, q, t2, ALU.subtract).then_inc(s_v, 1))
            wait("scalar", s_v, v_var)
            ems(lambda s, q=vcol: s.activation(q, q, AF.Ln, bias=A(o_MISC + 48, 1))
                .then_inc(s_s, 1))
            s_rsc = ems(lambda s, q=vcol, r=rcol:
                        s.activation(r, q, AF.Exp, scale=-0.5).then_inc(s_s, 1))
            wait("vector", s_s, s_rsc)
            emv(lambda v, n=ncol, m=mcol, r=rcol:
                v.tensor_scalar(n, m, r, -1.0, ALU.mult, ALU.mult).then_inc(s_v, 1))
            emv(lambda v, d=A(f_F + tb * DM, DM), r=A(f_H + tb * DM, DM),
                rc=rcol, n=ncol:
                v.tensor_scalar(d, r, rc, n, ALU.mult, ALU.add).then_inc(s_v, 1))
        v_ln2 = P.val("s_v")

        wait("sync", s_v, v_ln2)
        d_out_prev = sdma(A(f_F, 2 * DM).rearrange("p (tb m) -> p tb m", tb=2),
                          out[j, :, :].rearrange("(tb p) m -> p tb m", p=128),
                          "s_o")

    wait("sync", s_d, P.val("s_d"))
    wait("sync", s_o, P.val("s_o"))

    @block.gpsimd
    def _(g):
        for fn in prog["gpsimd"]:
            fn(g)

    @block.sync
    def _(e):
        for fn in prog["sync"]:
            fn(e)

    @block.vector
    def _(v):
        for fn in prog["vector"]:
            fn(v)

    @block.scalar
    def _(s):
        for fn in prog["scalar"]:
            fn(s)

    @block.tensor
    def _(t):
        for fn in prog["tensor"]:
            fn(t)

    es.close()
    return nc


_NC_CACHE = {}


def kernel(L0_attn_x, L0_attn_W_in, L0_attn_conv_w, L0_attn_conv_b, L0_attn_A_log,
           L0_attn_D, L0_attn_W_x, L0_attn_W_dt, L0_attn_b_dt, L0_attn_W_out,
           L0_ffn_wf, L0_ffn_uf, L0_ffn_bf, L0_ffn_wi, L0_ffn_ui, L0_ffn_bi,
           L0_ffn_wc, L0_ffn_uc, L0_ffn_bc, L0_ffn_wo, L0_ffn_uo, L0_ffn_bo,
           L0_ffn_in_c):
    x = np.asarray(L0_attn_x, dtype=np.float32)

    def T(a):
        return np.ascontiguousarray(np.asarray(a, dtype=np.float32).T)

    def colv(a):
        v = np.asarray(a, dtype=np.float32).reshape(-1)
        return np.ascontiguousarray(v.reshape(-1, 128).T)

    shared = {
        "winT": T(L0_attn_W_in),
        "wxT": T(L0_attn_W_x),
        "wdtT": T(L0_attn_W_dt),
        "woutT": T(L0_attn_W_out),
        "convw": np.ascontiguousarray(
            np.asarray(L0_attn_conv_w, dtype=np.float32)
            .reshape(16, 128, 4).transpose(1, 0, 2).reshape(128, 64)),
        "convb": colv(L0_attn_conv_b),
        "bdt": colv(L0_attn_b_dt),
        "alog": np.ascontiguousarray(
            np.asarray(L0_attn_A_log, dtype=np.float32).reshape(1, -1)),
        "dvec": colv(L0_attn_D),
        "wgT": np.ascontiguousarray(np.concatenate(
            [np.asarray(L0_ffn_wf), np.asarray(L0_ffn_wi),
             np.asarray(L0_ffn_wc)], axis=0).T.astype(np.float32)),
        "ugT": np.ascontiguousarray(np.concatenate(
            [np.asarray(L0_ffn_uf), np.asarray(L0_ffn_ui),
             np.asarray(L0_ffn_uc)], axis=0).T.astype(np.float32)),
        "woT": T(L0_ffn_wo),
        "uoT": T(L0_ffn_uo),
        "bg": colv(np.concatenate(
            [np.asarray(L0_ffn_bf), np.asarray(L0_ffn_bi),
             np.asarray(L0_ffn_bc)], axis=0)),
        "bo": np.ascontiguousarray(
            np.asarray(L0_ffn_bo, dtype=np.float32).reshape(1, -1)),
        "incv": colv(L0_ffn_in_c),
        "eye": np.eye(128, dtype=np.float32),
    }

    in_maps = []
    for j in range(NCORE):
        m = dict(shared)
        m["xT"] = np.ascontiguousarray(x[j].T)
        m["xn"] = np.ascontiguousarray(x[j])
        in_maps.append(m)

    if "nc" not in _NC_CACHE:
        _NC_CACHE["nc"] = build_nc()
    nc = _NC_CACHE["nc"]

    res = run_bass_kernel_spmd(nc, in_maps, core_ids=list(range(NCORE)))

    full = np.empty((B, L, DM), dtype=np.float32)
    for i in range(NCORE):
        full[:, i * POS:(i + 1) * POS, :] = res.results[i]["out"]
    return full


# version 12
# speedup vs baseline: 1.3135x; 1.0477x over previous
import numpy as np
from contextlib import ExitStack

import concourse.bass as bass
import concourse.mybir as mybir
from concourse.bass_utils import run_bass_kernel_spmd
from concourse import library_config

F32 = mybir.dt.float32
BF16 = mybir.dt.bfloat16
AF = mybir.ActivationFunctionType
ALU = mybir.AluOpType

B = 8
L = 2048
DM = 1024
DI = 2048
DS = 16
RK = 64
TC = 256
NCH = L // TC
NDB = DI // 128
POS = L // 8
G3 = 3 * DM
EPS = 1e-5
NCORE = 8
MEGA = NDB * TC
XRM = NDB * 260


class Plan:
    def __init__(self):
        self.c = {}

    def inc(self, name, amt):
        self.c[name] = self.c.get(name, 0) + amt
        return self.c[name]

    def val(self, name):
        return self.c.get(name, 0)


DEBUG = False


def build_nc():
    nc = bass.Bass("TRN2", target_bir_lowering=False, debug=False)

    def din(name, shape):
        return nc.dram_tensor(name, shape, F32, kind="ExternalInput").ap()

    xT = din("xT", [DM, L])
    xn = din("xn", [L, DM])
    winT = din("winT", [DM, 2 * DI])
    wxT = din("wxT", [DI, RK + 2 * DS])
    wdtT = din("wdtT", [RK, DI])
    woutT = din("woutT", [DI, DM])
    convw = din("convw", [128, 64])
    convb = din("convb", [128, 16])
    bdt = din("bdt", [128, 16])
    alog = din("alog", [1, DS])
    dvec = din("dvec", [128, 16])
    wgT = din("wgT", [DM, G3])
    ugT = din("ugT", [DM, G3])
    woT = din("woT", [DM, DM])
    uoT = din("uoT", [DM, DM])
    bg = din("bg", [128, 24])
    bo = din("bo", [1, DM])
    incv = din("incv", [128, 8])
    eye = din("eye", [128, 128])

    out = nc.dram_tensor("out", [B, POS, DM], F32, kind="ExternalOutput").ap()
    dbg = {}
    if DEBUG:
        for nm, shp in [("d_xr", [128, XRM]), ("d_xs", [128, MEGA]),
                        ("d_mdt", [128, MEGA]), ("d_xdbl", [96, TC]),
                        ("d_w0", [128, MEGA]), ("d_a0", [128, MEGA]),
                        ("d_h0", [128, MEGA]), ("d_yacc", [128, MEGA]),
                        ("d_ln1", [128, 2 * DM]), ("d_ag0", [8 * TC, DM]),
                        ("d_c0", [128, 2048]), ("d_hf0", [128, 2048]),
                        ("d_acol", [128, 16]), ("d_arw", [1, 16])]:
            dbg[nm] = nc.dram_tensor(nm, shp, F32, kind="ExternalOutput").ap()
    agin = nc.dram_tensor("agin", [L, DM], F32)
    agall = nc.dram_tensor("agall", [NCH, 8 * TC, DM], F32,
                           addr_space="Shared")

    es = ExitStack()

    AR_COLS = 47552
    AR = es.enter_context(nc.sbuf_tensor("AR", [128, AR_COLS], F32))
    _ptr = [0]

    def areg(n):
        o = _ptr[0]
        _ptr[0] += n
        assert _ptr[0] <= AR_COLS, f"arena overflow {_ptr[0]}"
        return o

    o_XR = areg(XRM)
    o_U = areg(MEGA)
    o_A1 = areg(MEGA)
    o_A2 = areg(MEGA)
    o_A4 = areg(MEGA)
    o_A5 = areg(MEGA)
    o_A8 = areg(MEGA)
    o_SZ = areg(MEGA // 2)
    o_B1 = areg(MEGA // 2)
    o_XTB = areg(2 * 1024)
    o_XN = areg(2 * DM)
    o_R = areg(2 * DM)
    o_WSL = areg(2 * 2048)
    o_WOS = areg(2 * 512)
    o_WXT = areg(768)
    o_WDT = areg(1024)
    o_XD = areg(256)
    o_XDB = areg(256)
    o_RP = areg(2 * TC)
    o_ST = areg(256)
    o_CL = areg(160)
    o_EYE = areg(128)
    o_MISC = areg(64)
    mamba_top = _ptr[0]

    _fp = [0]

    def freg(n):
        o = _fp[0]
        _fp[0] += n
        assert _fp[0] <= mamba_top, f"ffn overflow {_fp[0]} > {mamba_top}"
        return o

    f_WG = freg(12288)
    f_UG = freg(12288)
    f_WO = freg(4096)
    f_XTT = freg(8 * 1024)
    f_C = freg(2048)
    f_CB = freg(1024)
    f_F = freg(2048)
    f_I = freg(2048)
    f_G = freg(2048)
    f_X32 = f_F
    f_H = f_G
    f_UO = f_I
    f_UOX = f_C
    f_LAND = f_WO

    BGC = es.enter_context(nc.sbuf_tensor("BGC", [128, 24], F32))
    INC = es.enter_context(nc.sbuf_tensor("INC", [128, 8], F32))
    BOR = es.enter_context(nc.sbuf_tensor("BOR", [1, DM], F32))
    ONE = es.enter_context(nc.sbuf_tensor("ONE", [1, 128], F32))
    ALG = es.enter_context(nc.sbuf_tensor("ALG", [1, DS], F32))
    ARW = es.enter_context(nc.sbuf_tensor("ARW", [1, DS], F32))

    PB = [es.enter_context(nc.psum_tensor(f"PB{i}", [128, 512], F32))
          for i in range(8)]

    s_d = es.enter_context(nc.semaphore("s_d"))
    s_k = es.enter_context(nc.semaphore("s_k"))
    s_o = es.enter_context(nc.semaphore("s_o"))
    s_b = [es.enter_context(nc.semaphore("s_b0")),
           es.enter_context(nc.semaphore("s_b1"))]
    s_g = es.enter_context(nc.semaphore("s_g"))
    s_gx = [es.enter_context(nc.semaphore("s_gx0")),
            es.enter_context(nc.semaphore("s_gx1"))]
    s_gw = [es.enter_context(nc.semaphore("s_gw0")),
            es.enter_context(nc.semaphore("s_gw1"))]
    s_go = [es.enter_context(nc.semaphore("s_go0")),
            es.enter_context(nc.semaphore("s_go1"))]
    s_t = es.enter_context(nc.semaphore("s_t"))
    s_v = es.enter_context(nc.semaphore("s_v"))
    s_s = es.enter_context(nc.semaphore("s_s"))
    s_p = es.enter_context(nc.semaphore("s_p"))
    s_c = es.enter_context(nc.semaphore("s_c"))
    sem_map = {"s_v": s_v, "s_s": s_s, "s_d": s_d, "s_g": s_g,
               "s_t": s_t, "s_p": s_p, "s_c": s_c,
               "s_k": s_k, "s_o": s_o, "s_b0": s_b[0], "s_b1": s_b[1],
               "s_gx0": s_gx[0], "s_gx1": s_gx[1],
               "s_gw0": s_gw[0], "s_gw1": s_gw[1],
               "s_go0": s_go[0], "s_go1": s_go[1]}

    block = es.enter_context(nc.Block())
    P = Plan()
    prog = {e: [] for e in ("sync", "gpsimd", "vector", "scalar", "tensor")}

    def em(eng, fn):
        prog[eng].append(fn)

    def wait(eng, sem, v):
        if v > 0:
            em(eng, lambda e, sem=sem, v=v: e.wait_ge(sem, v))

    def A(o, n, p0=0, pn=128):
        return AR[p0:p0 + pn, o:o + n]

    def Ab(o, nbf, p0=0, pn=128):
        return AR[p0:p0 + pn, :].bitcast(BF16)[:, 2 * o:2 * o + nbf]

    def sdma(src_ap, dst_ap, semn="s_d"):
        sem = sem_map[semn]
        em("sync", lambda e, s=src_ap, d=dst_ap, m=sem:
           e.dma_start(out=d, in_=s).then_inc(m, 16))
        return P.inc(semn, 16)

    def gdma(src_ap, dst_ap, semn="s_g"):
        sem = sem_map[semn]
        em("gpsimd", lambda g, s=src_ap, d=dst_ap, m=sem:
           g.dma_start(out=d, in_=s).then_inc(m, 16))
        return P.inc(semn, 16)

    def emv(fn):
        wait("vector", s_v, P.val("s_v"))
        em("vector", fn)
        return P.inc("s_v", 1)

    def ems(fn):
        wait("scalar", s_s, P.val("s_s"))
        em("scalar", fn)
        return P.inc("s_s", 1)

    def emp(fn):
        wait("gpsimd", s_p, P.val("s_p"))
        em("gpsimd", fn)
        return P.inc("s_p", 1)

    def tmm_group(mms, bank_ap=None):
        for fn in mms[:-1]:
            em("tensor", fn)
        em("tensor", lambda t, f=mms[-1]: f(t).then_inc(s_t, 1))
        return P.inc("s_t", 1)

    def dump(nm, src_ap, waits=()):
        if not DEBUG or nm not in dbg:
            return
        for eng_sem, v in waits:
            wait("sync", sem_map[eng_sem], v)
        sdma(src_ap, dbg[nm], "s_o")
        dv = P.val("s_o")
        wait("sync", s_o, dv)
        for e2 in ("vector", "scalar", "gpsimd", "tensor"):
            wait(e2, s_o, dv)

    bank_free = [("s_v", 0)] * 8

    def t_wait_bank(bi):
        sem, v = bank_free[bi]
        wait("tensor", sem_map[sem], v)


    o_cw = o_CL
    o_cb = o_CL + 64
    o_bd = o_CL + 80
    o_dv = o_CL + 96
    o_ac = o_CL + 112
    o_nb = o_CL + 128

    sdma(convw, A(o_cw, 64), "s_k")
    sdma(convb, A(o_cb, 16), "s_k")
    sdma(bdt, A(o_bd, 16), "s_k")
    sdma(dvec, A(o_dv, 16), "s_k")
    sdma(alog, ALG[:, :], "s_k")
    sdma(eye, A(o_EYE, 128), "s_k")
    sdma(bg, BGC[:, :], "s_k")
    sdma(incv, INC[:, :], "s_k")
    d_cst = sdma(bo, BOR[:, :], "s_k")

    wait("vector", s_k, d_cst)
    emv(lambda v: v.tensor_scalar_mul(A(o_nb, 16), A(o_bd, 16), -1.0).then_inc(s_v, 1))
    v_ones = emv(lambda v: v.memset(ONE[:, :], 1.0).then_inc(s_v, 1))
    emv(lambda v: v.memset(A(o_ST, 256), 0.0).then_inc(s_v, 1))
    emv(lambda v: v.memset(A(o_RP, 128), 1.0).then_inc(s_v, 1))
    emv(lambda v: v.memset(A(o_MISC + 48, 1), EPS).then_inc(s_v, 1))
    emv(lambda v: v.memset(A(o_XR, XRM), 0.0).then_inc(s_v, 1))

    wait("scalar", s_k, d_cst)
    v_arow = ems(lambda s: s.activation(ARW[:, :], ALG[:, :], AF.Exp).then_inc(s_s, 1))
    wait("tensor", s_s, v_arow)
    wait("tensor", s_v, v_ones)
    t_ac = tmm_group([lambda t: t.matmul(PB[0][:, 0:DS], ONE[:, 0:128],
                                         ARW[:, :], start=True, stop=True)])
    wait("vector", s_t, t_ac)
    p_acol_v = emv(lambda v: v.tensor_copy(A(o_ac, 16), PB[0][:, 0:DS])
                   .then_inc(s_v, 1))
    bank_free[0] = ("s_v", p_acol_v)
    dump("d_acol", A(o_ac, 16), [("s_v", p_acol_v)])
    dump("d_arw", ARW[:, :], [("s_s", v_arow)])

    winT_r = winT.rearrange("(blk p) o -> p blk o", p=128)
    woutT_r = woutT.rearrange("(blk p) o -> p blk o", p=128)
    xT_r = xT.rearrange("(blk p) t -> p blk t", p=128)

    g_xtb = [0, 0]
    g_xtb[0] = gdma(xT_r[:, :, 0:TC],
                    Ab(o_XTB, 2048).rearrange("p (blk t) -> p blk t", t=TC),
                    "s_gx0")

    g_wx = gdma(wxT.rearrange("(blk p) o -> p blk o", p=128),
                Ab(o_WXT, 1536).rearrange("p (blk o) -> p blk o", o=96))
    g_wdt = gdma(wdtT, Ab(o_WDT, 2048, 0, 64))

    sz_done_map = {cc2: [0] * NDB for cc2 in range(NCH)}
    v_yz_by_chunk = {}

    def emit_xz_slice_load(c2, sl):
        buf = sl % 2
        tprev = t_wsl_done.get((c2, sl - 2), t_wsl_done.get((c2 - 1, sl + 6), 0))
        wait("gpsimd", s_t, tprev)
        g_sl = gdma(winT_r[:, :, sl * 512:(sl + 1) * 512],
                    Ab(o_WSL + buf * 2048, 4096)
                    .rearrange("p (blk o) -> p blk o", o=512),
                    "s_gw" + str(buf))
        g_wsl_val[(c2, sl)] = g_sl

    def emit_xz_group_mm(c2, dout):
        sl = dout // 4
        buf = sl % 2
        bi = dout % 2
        wait("tensor", s_gw[buf], g_wsl_val[(c2, sl)])
        wait("tensor", s_gx[c2 % 2], g_xtb[c2 % 2])
        t_wait_bank(bi)
        db = dout % 4
        mms = []
        for mb in range(8):
            lhsT = Ab(o_WSL + buf * 2048, 4096)[
                :, mb * 512 + db * 128: mb * 512 + (db + 1) * 128]
            rhs = Ab(o_XTB + (c2 % 2) * 1024, 2048)[:, mb * TC:(mb + 1) * TC]
            mms.append(lambda t, o=PB[bi][:, 0:TC], l=lhsT, r=rhs,
                       st=(mb == 0), sp=(mb == 7):
                       t.matmul(o, l, r, start=st, stop=sp))
        t_mm = tmm_group(mms)
        if dout % 4 == 3:
            t_wsl_done[(c2, sl)] = P.val("s_t")
        return t_mm, bi

    def emit_xr_evac(c2, dout, t_mm, bi):
        wait("vector", s_t, t_mm)
        v_ev = emv(lambda v, ps=PB[bi][:, 0:TC],
                   d=A(o_XR + dout * 260 + 3, TC):
                   v.tensor_copy(d, ps).then_inc(s_v, 1))
        bank_free[bi] = ("s_v", v_ev)

    def emit_z_evac(c2, dout, t_mm, bi):
        zb = dout - 16
        wait("scalar", s_t, t_mm)
        if zb == 0 and v_yz_by_chunk.get(c2 - 1):
            wait("scalar", s_v, v_yz_by_chunk[c2 - 1])
        sz_done_map[c2][zb] = ems(
            lambda s, ps=PB[bi][:, 0:TC],
            d=Ab(o_SZ, MEGA)[:, zb * TC:(zb + 1) * TC]:
            s.activation(d, ps, AF.Silu).then_inc(s_s, 1))
        bank_free[bi] = ("s_s", sz_done_map[c2][zb])

    g_wsl_val = {}

    v_ln1_done = 0
    v_yz_prev = 0
    t_wsl_done = {}
    t_wos_done = {}
    t_ylast = 0
    v_acc_done = {}
    v_scan_done = {}
    v_scan_half = {}
    p_w_hist = {}
    v_chat_hist = {}
    s_a_done = {}
    d_agin_prev = 0
    cc_round = [0] * NCH

    for c in range(NCH):
        tc0 = c * TC

        wait("sync", s_v, v_ln1_done)
        if d_agin_prev:
            wait("sync", s_d, d_agin_prev)
        d_xn = sdma(xn[tc0:tc0 + TC, :].rearrange("(tb p) m -> p tb m", p=128),
                    A(o_XN, 2 * DM).rearrange("p (tb m) -> p tb m", tb=2))

        s_sz_done = sz_done_map[c]
        if c == 0:
            for sl in range(4):
                emit_xz_slice_load(0, sl)
                for db in range(4):
                    t_mm, bi = emit_xz_group_mm(0, sl * 4 + db)
                    emit_xr_evac(0, sl * 4 + db, t_mm, bi)
            for sl in range(4, 8):
                emit_xz_slice_load(0, sl)
                for db in range(4):
                    t_mm, bi = emit_xz_group_mm(0, sl * 4 + db)
                    emit_z_evac(0, sl * 4 + db, t_mm, bi)

        if c == 0:
            dump("d_xr", A(o_XR, XRM), [("s_v", P.val("s_v"))])
        for db in range(NDB):
            slot = o_XR + db * 260
            xcd = A(o_A4 + db * TC, TC)
            emv(lambda v, d=xcd, s=A(slot, TC), w=A(o_cw + db * 4, 1),
                b=A(o_cb + db, 1):
                v.tensor_scalar(d, s, w, b, ALU.mult, ALU.add).then_inc(s_v, 1))
            for j in range(1, 4):
                emv(lambda v, d=xcd, s=A(slot + j, TC), w=A(o_cw + db * 4 + j, 1):
                    v.scalar_tensor_tensor(d, s, w, d, ALU.mult, ALU.add)
                    .then_inc(s_v, 1))
        xr_r0 = A(o_XR, XRM).rearrange("p (blk w) -> p blk w", w=260)
        emv(lambda v, s=xr_r0[:, :, 256:259], d=xr_r0[:, :, 0:3]:
            v.tensor_copy(d, s).then_inc(s_v, 1))
        v_conv = P.val("s_v")

        wait("scalar", s_v, v_conv)

        s_xs = ems(lambda s: s.activation(A(o_A2, MEGA), A(o_A4, MEGA), AF.Silu)
                   .then_inc(s_s, 1))
        if c == 0:
            dump("d_xs", A(o_A2, MEGA), [("s_s", s_xs)])
        wait("gpsimd", s_s, s_xs)
        if t_ylast:
            wait("gpsimd", s_t, t_ylast)
        p_xsb = emp(lambda g: g.tensor_copy(Ab(o_B1, MEGA), A(o_A2, MEGA))
                    .then_inc(s_p, 1))

        wait("tensor", s_p, p_xsb)
        if c == 0:
            wait("tensor", s_g, g_wdt)
        t_wait_bank(4)
        mms = []
        for dib in range(NDB):
            lhsT = Ab(o_WXT, 1536)[:, dib * 96:(dib + 1) * 96]
            rhs = Ab(o_B1, MEGA)[:, dib * TC:(dib + 1) * TC]
            mms.append(lambda t, l=lhsT, r=rhs, st=(dib == 0), sp=(dib == NDB - 1):
                       t.matmul(PB[4][0:96, 0:TC], l, r, start=st, stop=sp))
        t_wx = tmm_group(mms)

        wait("vector", s_t, t_wx)
        wait("vector", s_b[0], P.val("s_b0"))
        wait("vector", s_b[1], P.val("s_b1"))
        v_xd = emv(lambda v: v.tensor_copy(A(o_XD, TC, 0, 96), PB[4][0:96, 0:TC])
                   .then_inc(s_v, 1))
        bank_free[4] = ("s_v", v_xd)
        wait("scalar", s_v, v_xd)
        s_xdb = ems(lambda s: s.copy(Ab(o_XDB, 256, 0, 64), A(o_XD, TC, 0, 64))
                    .then_inc(s_s, 1))

        wait("tensor", s_s, s_xdb)
        if c == 0:
            wait("tensor", s_g, g_wdt)
        for db in range(NDB):
            bi = 4 + (db % 2)
            t_wait_bank(bi)
            mms = [lambda t, b=bi,
                   l=Ab(o_WDT, 2048, 0, 64)[:, db * 128:(db + 1) * 128],
                   r=Ab(o_XDB, 256, 0, 64):
                   t.matmul(PB[b][:, 0:TC], l, r, start=True, stop=True)]
            t_dt = tmm_group(mms)
            wait("scalar", s_t, t_dt)
            if db == 0 and v_acc_done.get((c - 1, DS - 1)):
                wait("scalar", s_v, v_acc_done[(c - 1, DS - 1)][1])
            s_sig = ems(lambda s, b=bi, d=A(o_A1 + db * TC, TC),
                        nb=A(o_nb + db, 1):
                        s.activation(d, PB[b][:, 0:TC], AF.Sigmoid,
                                     bias=nb, scale=-1.0).then_inc(s_s, 1))
            bank_free[bi] = ("s_s", s_sig)

        s_mdt = ems(lambda s: s.activation(A(o_A4, MEGA), A(o_A1, MEGA), AF.Ln)
                    .then_inc(s_s, 1))

        if c == 0:
            dump("d_mdt", A(o_A4, MEGA), [("s_s", s_mdt)])
            dump("d_xdbl", A(o_XD, TC, 0, 96), [("s_v", v_xd)])
        wait("vector", s_s, s_mdt)
        v_u = emv(lambda v: v.scalar_tensor_tensor(
            A(o_U, MEGA), A(o_A4, MEGA), -1.0, A(o_A2, MEGA),
            ALU.mult, ALU.mult).then_inc(s_v, 1))
        for db in range(NDB):
            emv(lambda v, d=A(o_A8 + db * TC, TC), s=A(o_A2 + db * TC, TC),
                w=A(o_dv + db, 1):
                v.tensor_scalar_mul(d, s, w).then_inc(s_v, 1))
        v_yacc0 = P.val("s_v")

        t_rep_hist = {}
        d_bc_hist = {}

        def stage_bc(k):
            bcb2 = o_WDT + (k % 2) * 512
            sdma(A(o_XD, TC, 64 + k, 1), A(bcb2, TC, 64, 1), "s_b" + str(k % 2))
            d_bc_hist[k] = sdma(A(o_XD, TC, 80 + k, 1), A(bcb2 + TC, TC, 64, 1),
                                "s_b" + str(k % 2))

        wait("sync", s_v, v_xd)
        stage_bc(0)
        HM = MEGA // 2
        if c + 1 < NCH:
            wait("gpsimd", s_t, t_wsl_done[(c, 7)])
            g_xtb[(c + 1) % 2] = gdma(
                xT_r[:, :, (c + 1) * TC:(c + 2) * TC],
                Ab(o_XTB + ((c + 1) % 2) * 1024, 2048)
                .rearrange("p (blk t) -> p blk t", t=TC),
                "s_gx" + str((c + 1) % 2))
        for k in range(DS):
            par = k % 2
            pbB, pbC = PB[2 * par + 2], PB[2 * par + 3] if False else PB[2 * par + 3]
            pbB = PB[2 * par + 2]
            pbC = PB[2 * par + 3]
            if k + 1 < DS:
                wait("sync", s_t, t_rep_hist.get(k - 1, 0))
                stage_bc(k + 1)
            bcb = o_WDT + par * 512
            wait("tensor", s_b[par], d_bc_hist[k])
            if k < 2:
                t_wait_bank(2 * par + 2)
                t_wait_bank(2 * par + 3)
            pwar = v_chat_hist.get((c, k - 2), v_chat_hist.get((c - 1, DS - 1), 0))
            if pwar:
                wait("tensor", s_v, pwar)
            t_rep = tmm_group([
                lambda t, r=A(bcb, TC, 64, 1), o1=A(o_RP, 128, 64, 1), pb=pbB:
                t.matmul(pb[:, 0:TC], o1, r, start=True, stop=True),
                lambda t, r=A(bcb + TC, TC, 64, 1), o1=A(o_RP, 128, 64, 1), pb=pbC:
                t.matmul(pb[:, 0:TC], o1, r, start=True, stop=True)])
            t_rep_hist[k] = t_rep

            s_ah = [0, 0]
            for hf2 in range(2):
                aw = v_scan_half.get((c, k - 1, hf2),
                                     v_scan_half.get((c - 1, DS - 1, hf2), 0))
                if aw:
                    wait("scalar", s_v, aw)
                wait("scalar", s_v, p_acol_v)
                s_ah[hf2] = ems(lambda s, sc=A(o_ac + k, 1),
                                d=A(o_A1 + hf2 * HM, HM),
                                src=A(o_A4 + hf2 * HM, HM):
                                s.activation(d, src, AF.Exp,
                                             scale=sc).then_inc(s_s, 1))
            s_a_done[(c, k)] = s_ah[1]

            bb = pbB[:, 0:TC].unsqueeze(1).broadcast_to([128, NDB, TC])
            uu = A(o_U, MEGA).rearrange("p (blk t) -> p blk t", t=TC)
            ww = A(o_A2, MEGA).rearrange("p (blk t) -> p blk t", t=TC)
            wait("vector", s_t, t_rep)
            emv(lambda v, w=ww, u=uu, b2=bb:
                v.tensor_tensor(w, u, b2, ALU.mult).then_inc(s_v, 1))

            for db in range(NDB):
                if db == 0:
                    wait("vector", s_s, s_ah[0])
                if db == NDB // 2:
                    wait("vector", s_s, s_ah[1])
                emv(lambda v, o=A(o_A5 + db * TC, TC),
                    a=A(o_A1 + db * TC, TC),
                    w=A(o_A2 + db * TC, TC), i=A(o_ST + k * 16 + db, 1):
                    v.tensor_tensor_scan(o, a, w, i, ALU.mult, ALU.add)
                    .then_inc(s_v, 1))
                if db == NDB // 2 - 1:
                    v_scan_half[(c, k, 0)] = P.val("s_v")
            v_scan_half[(c, k, 1)] = P.val("s_v")
            if c == 0 and k == 0:
                dump("d_a0", A(o_A1, MEGA), [("s_s", s_ah[1])])
                dump("d_w0", A(o_A2, MEGA), [("s_v", P.val("s_v"))])
                dump("d_h0", A(o_A5, MEGA), [("s_v", P.val("s_v"))])
            lastc = A(o_A5, MEGA).rearrange("p (blk t) -> p blk t", t=TC)[:, :, TC - 1:TC]
            v_scan_done[(c, k)] = emv(
                lambda v, lc=lastc, d=A(o_ST + k * 16, 16).unsqueeze(2):
                v.tensor_copy(d, lc).then_inc(s_v, 1))

            cb2 = pbC[:, 0:TC].unsqueeze(1).broadcast_to([128, NDB, TC])
            hh = A(o_A5, MEGA).rearrange("p (blk t) -> p blk t", t=TC)
            v_chat = emv(lambda v, o=hh, h=hh, c2=cb2:
                         v.tensor_tensor(o, h, c2, ALU.mult).then_inc(s_v, 1))
            v_chat_hist[(c, k)] = v_chat
            bank_free[2 * par + 2] = ("s_v", v_chat)
            bank_free[2 * par + 3] = ("s_v", v_chat)
            v_acc = emv(lambda v:
                        v.tensor_tensor(A(o_A8, MEGA), A(o_A8, MEGA),
                                        A(o_A5, MEGA), ALU.add).then_inc(s_v, 1))
            v_acc_done[(c, k)] = ("v", v_acc)

            if c + 1 < NCH:
                if k % 4 == 0:
                    emit_xz_slice_load(c + 1, k // 4)
                t_mmp, bip = emit_xz_group_mm(c + 1, k)
                emit_xr_evac(c + 1, k, t_mmp, bip)



        if c == 0:
            dump("d_yacc", A(o_A8, MEGA), [("s_v", P.val("s_v"))])
        wait("vector", s_s, max(s_sz_done))
        wait("vector", s_t, t_wx)
        v_yz = emv(lambda v: v.tensor_tensor(Ab(o_B1, MEGA), A(o_A8, MEGA),
                                             Ab(o_SZ, MEGA), ALU.mult)
                   .then_inc(s_v, 1))
        v_yz_prev = v_yz
        v_yz_by_chunk[c] = v_yz

        wait("tensor", s_v, v_yz)
        for bi in (4, 5, 6, 7):
            t_wait_bank(bi)
        t_y = {}
        for dib in range(NDB):
            tprev = t_wos_done.get((c, dib - 2),
                                   t_wos_done.get((c - 1, dib + 14), 0))
            wait("gpsimd", s_t, tprev)
            g_wo = gdma(woutT_r[:, dib, :], Ab(o_WOS + (dib % 2) * 512, 1024),
                        "s_go" + str(dib % 2))
            wait("tensor", s_go[dib % 2], g_wo)
            mms = []
            for tb in range(2):
                for hf in range(2):
                    bi = 4 + tb * 2 + hf
                    lhsT = Ab(o_B1, MEGA)[:, dib * TC + tb * 128:
                                          dib * TC + tb * 128 + 128]
                    rhs = Ab(o_WOS + (dib % 2) * 512, 1024)[:, hf * 512:(hf + 1) * 512]
                    mms.append(lambda t, b=bi, l=lhsT, r=rhs, st=(dib == 0),
                               sp=(dib == NDB - 1):
                               t.matmul(PB[b][:, :], l, r, start=st, stop=sp))
            t_wos_done[(c, dib)] = tmm_group(mms)
        t_ylast = P.val("s_t")

        rs_cols = o_MISC
        v_ev_by_bank = {}
        for tb in range(2):
            for hf in range(2):
                bi = 4 + tb * 2 + hf
                wait("vector", s_t, t_ylast)
                wait("vector", s_d, d_xn)
                v_ev = emv(lambda v, d=A(o_R + tb * DM + hf * 512, 512),
                           p=PB[bi][:, :], x=A(o_XN + tb * DM + hf * 512, 512),
                           rc=A(rs_cols + tb * 2 + hf, 1):
                           v.scalar_tensor_tensor(d, p, 1.0, x, ALU.mult,
                                                  ALU.add, accum_out=rc)
                           .then_inc(s_v, 1))
                bank_free[bi] = ("s_v", v_ev)

        o_stat = o_MISC + 8
        for tb in range(2):
            mcol = A(o_stat + tb * 4 + 0, 1)
            vcol = A(o_stat + tb * 4 + 1, 1)
            rcol = A(o_stat + tb * 4 + 2, 1)
            ncol = A(o_stat + tb * 4 + 3, 1)
            tcol = A(o_MISC + 16, 1)
            emv(lambda v, m=mcol, a=A(rs_cols + tb * 2, 1),
                b2=A(rs_cols + tb * 2 + 1, 1):
                v.tensor_tensor(m, a, b2, ALU.add).then_inc(s_v, 1))
            v_m = emv(lambda v, m=mcol: v.tensor_scalar_mul(m, m, 1.0 / DM)
                      .then_inc(s_v, 1))
            wait("scalar", s_v, v_m)
            s_sq = ems(lambda s, jk=A(o_A5 + tb * DM, DM),
                       r=A(o_R + tb * DM, DM), q=vcol:
                       s.activation(jk, r, AF.Square, scale=1.0 / (DM ** 0.5),
                                    accum_out=q).then_inc(s_s, 1))
            wait("vector", s_s, s_sq)
            emv(lambda v, t2=tcol, m=mcol:
                v.tensor_tensor(t2, m, m, ALU.mult).then_inc(s_v, 1))
            v_var = emv(lambda v, q=vcol, t2=tcol:
                        v.tensor_tensor(q, q, t2, ALU.subtract).then_inc(s_v, 1))
            wait("scalar", s_v, v_var)
            ems(lambda s, q=vcol: s.activation(q, q, AF.Ln, bias=A(o_MISC + 48, 1))
                .then_inc(s_s, 1))
            s_rsc = ems(lambda s, q=vcol, r=rcol:
                        s.activation(r, q, AF.Exp, scale=-0.5).then_inc(s_s, 1))
            wait("vector", s_s, s_rsc)
            emv(lambda v, n=ncol, m=mcol, r=rcol:
                v.tensor_scalar(n, m, r, -1.0, ALU.mult, ALU.mult).then_inc(s_v, 1))
            emv(lambda v, d=A(o_XN + tb * DM, DM), r=A(o_R + tb * DM, DM),
                rc=rcol, n=ncol:
                v.tensor_scalar(d, r, rc, n, ALU.mult, ALU.add).then_inc(s_v, 1))
        v_ln1_done = P.val("s_v")

        wait("sync", s_v, v_ln1_done)
        d_agin_prev = sdma(A(o_XN, 2 * DM).rearrange("p (tb m) -> p tb m", tb=2),
                           agin[tc0:tc0 + TC, :]
                           .rearrange("(tb p) m -> p tb m", p=128))
        wait("gpsimd", s_d, d_agin_prev)
        em("gpsimd", lambda g, c2=c: g.collective_compute(
            "AllGather", ALU.bypass,
            replica_groups=[list(range(NCORE))],
            ins=[agin[c2 * TC:(c2 + 1) * TC, :].opt()],
            outs=[agall[c2, :, :].opt()]).then_inc(s_c, 1))
        cc_round[c] = P.inc("s_c", 1)

        if c + 1 < NCH:
            for sl in range(4, 8):
                emit_xz_slice_load(c + 1, sl)
                for db in range(4):
                    t_mmz, biz = emit_xz_group_mm(c + 1, sl * 4 + db)
                    emit_z_evac(c + 1, sl * 4 + db, t_mmz, biz)

        if c == 0:
            dump("d_ln1", A(o_XN, 2 * DM), [("s_v", v_ln1_done)])
            if DEBUG:
                wait("sync", s_c, cc_round[0])
                sdma(agall[0, :, :], dbg["d_ag0"], "s_o")

    v_md = P.val("s_v")
    s_md = P.val("s_s")
    t_md = P.val("s_t")
    wait("gpsimd", s_v, v_md)
    wait("gpsimd", s_s, s_md)
    wait("gpsimd", s_t, t_md)

    g_wg = gdma(wgT.rearrange("(blk p) o -> p blk o", p=128),
                Ab(f_WG, 8 * G3).rearrange("p (blk o) -> p blk o", o=G3))
    g_ug = gdma(ugT.rearrange("(blk p) o -> p blk o", p=128),
                Ab(f_UG, 8 * G3).rearrange("p (blk o) -> p blk o", o=G3))

    wait("vector", s_c, cc_round[NCH - 1])
    wait("vector", s_s, s_md)
    wait("vector", s_t, t_md)
    for mb in range(8):
        emv(lambda v, d=A(f_C + mb * TC, TC), i=INC[:, mb:mb + 1]:
            v.tensor_scalar(d, A(o_ST, TC), 0.0, i, ALU.mult, ALU.add)
            .then_inc(s_v, 1))
    v_cup = emv(lambda v: v.tensor_copy(Ab(f_CB, 2048), A(f_C, 2048))
                .then_inc(s_v, 1))

    agall_r = agall.rearrange("r (j2 tb p) m -> r j2 p tb m", tb=2, p=128)
    pid_holder = {}

    def get_pid(e):
        if "v" not in pid_holder:
            pid_holder["v"] = e.partition_id()
        return pid_holder["v"]

    wait("sync", s_c, cc_round[NCH - 1])

    def load_ag(j, dst_off):
        def dof(e, j=j, dst_off=dst_off):
            pid = get_pid(e)
            e.dma_start(out=A(dst_off, 2 * DM).rearrange("p (tb m) -> p tb m",
                                                         tb=2),
                        in_=agall_r[pid, j, :, :, :]).then_inc(s_d, 16)
        em("sync", dof)
        return P.inc("s_d", 16)

    t_xp = {}
    s_xp = {}
    d_land = {}

    def emit_tr_step(j, i):
        tb, mb = i // 8, i % 8
        bi = 2 if i % 2 == 0 else 7
        t_wait_bank(bi)
        if i == 0:
            wait("tensor", s_d, d_land[j])
        src = A(f_LAND + tb * DM + mb * 128, 128)
        t_tr = tmm_group([lambda t, s2=src, b=bi:
                          t.transpose(PB[b][:, 0:128], s2, A(o_EYE, 128))])
        wait("scalar", s_t, t_tr)
        dstb = Ab(f_XTT + j * 1024, 2048)[
            :, mb * TC + tb * 128: mb * TC + tb * 128 + 128]
        s_cp = ems(lambda s, d=dstb, b=bi:
                   s.copy(d, PB[b][:, 0:128]).then_inc(s_s, 1))
        bank_free[bi] = ("s_s", s_cp)
        if i == 15:
            t_xp[j] = t_tr
            s_xp[j] = s_cp

    d_land[0] = load_ag(0, f_LAND)
    for i in range(16):
        emit_tr_step(0, i)

    v_cstep = {}
    for j in range(B):
        if j + 1 < B:
            wait("sync", s_t, t_xp[j])
            d_land[j + 1] = load_ag(j + 1, f_LAND)

        wait("tensor", s_g, g_ug)
        wait("tensor", s_s, s_xp[j])
        wait("tensor", s_v, v_cup if j == 0 else v_cstep[j - 1])
        s_gate_last = 0
        GB_BANKS = (0, 1, 3, 4)
        for gb in range(24):
            bi = GB_BANKS[gb % 4]
            t_wait_bank(bi)
            mms = []
            for mb in range(8):
                lhsT = Ab(f_WG, 8 * G3)[:, mb * G3 + gb * 128: mb * G3 + (gb + 1) * 128]
                rhs = Ab(f_XTT + j * 1024, 2048)[:, mb * TC:(mb + 1) * TC]
                mms.append(lambda t, b=bi, l=lhsT, r=rhs, st=(mb == 0):
                           t.matmul(PB[b][:, 0:TC], l, r, start=st, stop=False))
            for mb in range(8):
                lhsT = Ab(f_UG, 8 * G3)[:, mb * G3 + gb * 128: mb * G3 + (gb + 1) * 128]
                rhs = Ab(f_CB, 2048)[:, mb * TC:(mb + 1) * TC]
                mms.append(lambda t, b=bi, l=lhsT, r=rhs, sp=(mb == 7):
                           t.matmul(PB[b][:, 0:TC], l, r, start=False, stop=sp))
            t_g = tmm_group(mms)
            kind = gb // 8
            mb8 = gb % 8
            dst = [f_F, f_I, f_G][kind] + mb8 * TC
            func = AF.Tanh if kind == 2 else AF.Sigmoid
            wait("scalar", s_t, t_g)
            s_gate_last = ems(lambda s, b=bi, d=A(dst, TC),
                              bb2=BGC[:, gb:gb + 1], fn=func:
                              s.activation(d, PB[b][:, 0:TC], fn, bias=bb2)
                              .then_inc(s_s, 1))
            bank_free[bi] = ("s_s", s_gate_last)
            if j + 1 < B and 1 <= gb <= 16:
                emit_tr_step(j + 1, gb - 1)

        wait("vector", s_s, s_gate_last)
        for mb in range(8):
            fsl = A(f_F + mb * TC, TC)
            isl = A(f_I + mb * TC, TC)
            gsl = A(f_G + mb * TC, TC)
            csl = A(f_C + mb * TC, TC)
            emv(lambda v, f=fsl, c2=csl: v.tensor_tensor(f, f, c2, ALU.mult)
                .then_inc(s_v, 1))
            emv(lambda v, i=isl, g2=gsl: v.tensor_tensor(g2, i, g2, ALU.mult)
                .then_inc(s_v, 1))
            emv(lambda v, c2=csl, f=fsl, g2=gsl:
                v.tensor_tensor(c2, f, g2, ALU.add).then_inc(s_v, 1))
        v_cstep[j] = emv(lambda v: v.tensor_copy(Ab(f_CB, 2048), A(f_C, 2048))
                         .then_inc(s_v, 1))
        if j == 0:
            dump("d_c0", A(f_C, 2048), [("s_v", v_cstep[j])])

    wait("gpsimd", s_s, P.val("s_s"))
    wait("gpsimd", s_v, P.val("s_v"))
    wait("gpsimd", s_t, t_xp[B - 1])
    g_wo2 = gdma(woT.rearrange("(blk p) o -> p blk o", p=128),
                 Ab(f_WO, 8 * DM).rearrange("p (blk o) -> p blk o", o=DM),
                 "s_gw0")
    g_uo2 = gdma(uoT.rearrange("(blk p) o -> p blk o", p=128),
                 Ab(f_UO, 8 * DM).rearrange("p (blk o) -> p blk o", o=DM),
                 "s_gw1")

    v_cF = v_cstep[B - 1]
    wait("tensor", s_gw[1], g_uo2)
    wait("tensor", s_v, v_cF)
    s_hoist = 0
    for pb2 in range(2):
        for hf in range(2):
            bi = 5 + hf
            t_wait_bank(bi)
            mms = []
            for mb in range(8):
                lhsT = Ab(f_CB, 2048)[:, mb * TC + pb2 * 128: mb * TC + pb2 * 128 + 128]
                rhs = Ab(f_UO, 8 * DM)[:, mb * DM + hf * 512: mb * DM + (hf + 1) * 512]
                mms.append(lambda t, b=bi, l=lhsT, r=rhs, st=(mb == 0):
                           t.matmul(PB[b][:, :], l, r, start=st, stop=False))
            mms.append(lambda t, b=bi, h=hf:
                       t.matmul(PB[b][:, :], ONE[:, 0:128],
                                BOR[:, h * 512:(h + 1) * 512],
                                start=False, stop=True))
            t_uo = tmm_group(mms)
            wait("scalar", s_t, t_uo)
            s_hoist = ems(lambda s, b=bi,
                          d=A(f_UOX + pb2 * DM + hf * 512, 512):
                          s.copy(d, PB[b][:, :]).then_inc(s_s, 1))
            bank_free[bi] = ("s_s", s_hoist)

    v_ln2 = 0
    d_out_prev = 0
    v_first = True
    for j in range(B):
        if j == 0:
            wait("sync", s_v, v_cF)
        else:
            wait("sync", s_o, d_out_prev)
        d_x = load_ag(j, f_X32)

        s_tanh_last = 0
        for pb2 in range(2):
            for hf in range(2):
                bi = 5 + hf
                t_wait_bank(bi)
                if j == 0 and pb2 == 0 and hf == 0:
                    wait("tensor", s_gw[0], g_wo2)
                    wait("tensor", s_s, s_xp[B - 1])
                mms = []
                for mb in range(8):
                    lhsT = Ab(f_XTT + j * 1024, 2048)[
                        :, mb * TC + pb2 * 128: mb * TC + pb2 * 128 + 128]
                    rhs = Ab(f_WO, 8 * DM)[:, mb * DM + hf * 512: mb * DM + (hf + 1) * 512]
                    mms.append(lambda t, b=bi, l=lhsT, r=rhs, st=(mb == 0),
                               sp=(mb == 7):
                               t.matmul(PB[b][:, :], l, r, start=st, stop=sp))
                t_h = tmm_group(mms)
                wait("vector", s_t, t_h)
                if v_first:
                    wait("vector", s_s, s_hoist)
                    v_first = False
                v_o = emv(lambda v, b=bi,
                          d=A(f_H + pb2 * DM + hf * 512, 512),
                          u=A(f_UOX + pb2 * DM + hf * 512, 512):
                          v.scalar_tensor_tensor(d, PB[b][:, :], 1.0, u,
                                                 ALU.mult, ALU.add)
                          .then_inc(s_v, 1))
                bank_free[bi] = ("s_v", v_o)
                wait("scalar", s_v, v_o)
                s_tanh_last = ems(lambda s,
                                  d=A(f_H + pb2 * DM + hf * 512, 512):
                                  s.activation(d, d, AF.Tanh).then_inc(s_s, 1))

        if j == 0:
            dump("d_hf0", A(f_H, 2048), [("s_s", s_tanh_last)])
        wait("vector", s_s, s_tanh_last)
        wait("vector", s_d, d_x)
        rs2 = o_MISC + 24
        for tb in range(2):
            for hf in range(2):
                emv(lambda v, d=A(f_H + tb * DM + hf * 512, 512),
                    x=A(f_X32 + tb * DM + hf * 512, 512),
                    rc=A(rs2 + tb * 2 + hf, 1):
                    v.scalar_tensor_tensor(d, d, 1.0, x, ALU.mult, ALU.add,
                                           accum_out=rc).then_inc(s_v, 1))
        o_st2 = o_MISC + 32
        for tb in range(2):
            mcol = A(o_st2 + tb * 4 + 0, 1)
            vcol = A(o_st2 + tb * 4 + 1, 1)
            rcol = A(o_st2 + tb * 4 + 2, 1)
            ncol = A(o_st2 + tb * 4 + 3, 1)
            tcol = A(o_MISC + 40, 1)
            emv(lambda v, m=mcol, a=A(rs2 + tb * 2, 1), b2=A(rs2 + tb * 2 + 1, 1):
                v.tensor_tensor(m, a, b2, ALU.add).then_inc(s_v, 1))
            v_m = emv(lambda v, m=mcol: v.tensor_scalar_mul(m, m, 1.0 / DM)
                      .then_inc(s_v, 1))
            wait("scalar", s_v, v_m)
            s_sq = ems(lambda s, jk=A(f_F + tb * DM, DM), r=A(f_H + tb * DM, DM),
                       q=vcol:
                       s.activation(jk, r, AF.Square, scale=1.0 / (DM ** 0.5),
                                    accum_out=q).then_inc(s_s, 1))
            wait("vector", s_s, s_sq)
            emv(lambda v, t2=tcol, m=mcol: v.tensor_tensor(t2, m, m, ALU.mult)
                .then_inc(s_v, 1))
            v_var = emv(lambda v, q=vcol, t2=tcol:
                        v.tensor_tensor(q, q, t2, ALU.subtract).then_inc(s_v, 1))
            wait("scalar", s_v, v_var)
            ems(lambda s, q=vcol: s.activation(q, q, AF.Ln, bias=A(o_MISC + 48, 1))
                .then_inc(s_s, 1))
            s_rsc = ems(lambda s, q=vcol, r=rcol:
                        s.activation(r, q, AF.Exp, scale=-0.5).then_inc(s_s, 1))
            wait("vector", s_s, s_rsc)
            emv(lambda v, n=ncol, m=mcol, r=rcol:
                v.tensor_scalar(n, m, r, -1.0, ALU.mult, ALU.mult).then_inc(s_v, 1))
            emv(lambda v, d=A(f_F + tb * DM, DM), r=A(f_H + tb * DM, DM),
                rc=rcol, n=ncol:
                v.tensor_scalar(d, r, rc, n, ALU.mult, ALU.add).then_inc(s_v, 1))
        v_ln2 = P.val("s_v")

        wait("sync", s_v, v_ln2)
        d_out_prev = sdma(A(f_F, 2 * DM).rearrange("p (tb m) -> p tb m", tb=2),
                          out[j, :, :].rearrange("(tb p) m -> p tb m", p=128),
                          "s_o")

    wait("sync", s_d, P.val("s_d"))
    wait("sync", s_o, P.val("s_o"))

    @block.gpsimd
    def _(g):
        for fn in prog["gpsimd"]:
            fn(g)

    @block.sync
    def _(e):
        for fn in prog["sync"]:
            fn(e)

    @block.vector
    def _(v):
        for fn in prog["vector"]:
            fn(v)

    @block.scalar
    def _(s):
        for fn in prog["scalar"]:
            fn(s)

    @block.tensor
    def _(t):
        for fn in prog["tensor"]:
            fn(t)

    es.close()
    return nc


_NC_CACHE = {}


def kernel(L0_attn_x, L0_attn_W_in, L0_attn_conv_w, L0_attn_conv_b, L0_attn_A_log,
           L0_attn_D, L0_attn_W_x, L0_attn_W_dt, L0_attn_b_dt, L0_attn_W_out,
           L0_ffn_wf, L0_ffn_uf, L0_ffn_bf, L0_ffn_wi, L0_ffn_ui, L0_ffn_bi,
           L0_ffn_wc, L0_ffn_uc, L0_ffn_bc, L0_ffn_wo, L0_ffn_uo, L0_ffn_bo,
           L0_ffn_in_c):
    x = np.asarray(L0_attn_x, dtype=np.float32)

    def T(a):
        return np.ascontiguousarray(np.asarray(a, dtype=np.float32).T)

    def colv(a):
        v = np.asarray(a, dtype=np.float32).reshape(-1)
        return np.ascontiguousarray(v.reshape(-1, 128).T)

    shared = {
        "winT": T(L0_attn_W_in),
        "wxT": T(L0_attn_W_x),
        "wdtT": T(L0_attn_W_dt),
        "woutT": T(L0_attn_W_out),
        "convw": np.ascontiguousarray(
            np.asarray(L0_attn_conv_w, dtype=np.float32)
            .reshape(16, 128, 4).transpose(1, 0, 2).reshape(128, 64)),
        "convb": colv(L0_attn_conv_b),
        "bdt": colv(L0_attn_b_dt),
        "alog": np.ascontiguousarray(
            np.asarray(L0_attn_A_log, dtype=np.float32).reshape(1, -1)),
        "dvec": colv(L0_attn_D),
        "wgT": np.ascontiguousarray(np.concatenate(
            [np.asarray(L0_ffn_wf), np.asarray(L0_ffn_wi),
             np.asarray(L0_ffn_wc)], axis=0).T.astype(np.float32)),
        "ugT": np.ascontiguousarray(np.concatenate(
            [np.asarray(L0_ffn_uf), np.asarray(L0_ffn_ui),
             np.asarray(L0_ffn_uc)], axis=0).T.astype(np.float32)),
        "woT": T(L0_ffn_wo),
        "uoT": T(L0_ffn_uo),
        "bg": colv(np.concatenate(
            [np.asarray(L0_ffn_bf), np.asarray(L0_ffn_bi),
             np.asarray(L0_ffn_bc)], axis=0)),
        "bo": np.ascontiguousarray(
            np.asarray(L0_ffn_bo, dtype=np.float32).reshape(1, -1)),
        "incv": colv(L0_ffn_in_c),
        "eye": np.eye(128, dtype=np.float32),
    }

    in_maps = []
    for j in range(NCORE):
        m = dict(shared)
        m["xT"] = np.ascontiguousarray(x[j].T)
        m["xn"] = np.ascontiguousarray(x[j])
        in_maps.append(m)

    if "nc" not in _NC_CACHE:
        _NC_CACHE["nc"] = build_nc()
    nc = _NC_CACHE["nc"]

    res = run_bass_kernel_spmd(nc, in_maps, core_ids=list(range(NCORE)))

    full = np.empty((B, L, DM), dtype=np.float32)
    for i in range(NCORE):
        full[:, i * POS:(i + 1) * POS, :] = res.results[i]["out"]
    return full


# version 13
# speedup vs baseline: 1.3294x; 1.0121x over previous
import numpy as np
from contextlib import ExitStack

import concourse.bass as bass
import concourse.mybir as mybir
from concourse.bass_utils import run_bass_kernel_spmd
from concourse import library_config

F32 = mybir.dt.float32
BF16 = mybir.dt.bfloat16
AF = mybir.ActivationFunctionType
ALU = mybir.AluOpType

B = 8
L = 2048
DM = 1024
DI = 2048
DS = 16
RK = 64
TC = 256
NCH = L // TC
NDB = DI // 128
POS = L // 8
G3 = 3 * DM
EPS = 1e-5
NCORE = 8
MEGA = NDB * TC
XRM = NDB * 260


class Plan:
    def __init__(self):
        self.c = {}

    def inc(self, name, amt):
        self.c[name] = self.c.get(name, 0) + amt
        return self.c[name]

    def val(self, name):
        return self.c.get(name, 0)


DEBUG = False


def build_nc():
    nc = bass.Bass("TRN2", target_bir_lowering=False, debug=False)

    def din(name, shape):
        return nc.dram_tensor(name, shape, F32, kind="ExternalInput").ap()

    xT = din("xT", [DM, L])
    xn = din("xn", [L, DM])
    winT = din("winT", [DM, 2 * DI])
    wxT = din("wxT", [DI, RK + 2 * DS])
    wdtT = din("wdtT", [RK, DI])
    woutT = din("woutT", [DI, DM])
    convw = din("convw", [128, 64])
    convb = din("convb", [128, 16])
    bdt = din("bdt", [128, 16])
    alog = din("alog", [1, DS])
    dvec = din("dvec", [128, 16])
    wgT = din("wgT", [DM, G3])
    ugT = din("ugT", [DM, G3])
    woT = din("woT", [DM, DM])
    uoT = din("uoT", [DM, DM])
    bg = din("bg", [128, 24])
    bo = din("bo", [1, DM])
    incv = din("incv", [128, 8])
    eye = din("eye", [128, 128])

    out = nc.dram_tensor("out", [B, POS, DM], F32, kind="ExternalOutput").ap()
    dbg = {}
    if DEBUG:
        for nm, shp in [("d_xr", [128, XRM]), ("d_xs", [128, MEGA]),
                        ("d_mdt", [128, MEGA]), ("d_xdbl", [96, TC]),
                        ("d_w0", [128, MEGA]), ("d_a0", [128, MEGA]),
                        ("d_h0", [128, MEGA]), ("d_yacc", [128, MEGA]),
                        ("d_ln1", [128, 2 * DM]), ("d_ag0", [8 * TC, DM]),
                        ("d_c0", [128, 2048]), ("d_hf0", [128, 2048]),
                        ("d_acol", [128, 16]), ("d_arw", [1, 16])]:
            dbg[nm] = nc.dram_tensor(nm, shp, F32, kind="ExternalOutput").ap()
    agin = nc.dram_tensor("agin", [L, DM], F32)
    agall = nc.dram_tensor("agall", [NCH, 8 * TC, DM], F32,
                           addr_space="Shared")

    es = ExitStack()

    AR_COLS = 47552
    AR = es.enter_context(nc.sbuf_tensor("AR", [128, AR_COLS], F32))
    _ptr = [0]

    def areg(n):
        o = _ptr[0]
        _ptr[0] += n
        assert _ptr[0] <= AR_COLS, f"arena overflow {_ptr[0]}"
        return o

    o_XR = areg(XRM)
    o_U = areg(MEGA)
    o_A1 = areg(MEGA)
    o_A2 = areg(MEGA)
    o_A4 = areg(MEGA)
    o_A5 = areg(MEGA)
    o_A8 = areg(MEGA)
    o_SZ = areg(MEGA // 2)
    o_B1 = areg(MEGA // 2)
    o_XTB = areg(2 * 1024)
    o_XN = areg(2 * DM)
    o_R = areg(2 * DM)
    o_WSL = areg(2 * 2048)
    o_WOS = areg(2 * 512)
    o_WXT = areg(768)
    o_WDT = areg(1024)
    o_XD = areg(256)
    o_XDB = areg(256)
    o_RP = areg(2 * TC)
    o_ST = areg(256)
    o_CL = areg(160)
    o_EYE = areg(128)
    o_MISC = areg(64)
    mamba_top = _ptr[0]

    _fp = [0]

    def freg(n):
        o = _fp[0]
        _fp[0] += n
        assert _fp[0] <= mamba_top, f"ffn overflow {_fp[0]} > {mamba_top}"
        return o

    f_WG = freg(12288)
    f_UG = freg(12288)
    f_WO = freg(4096)
    f_XTT = freg(8 * 1024)
    f_C = freg(2048)
    f_CB = freg(2 * 1024)
    f_F = freg(2048)
    f_I = freg(2048)
    f_G = freg(2048)
    f_X32 = f_F
    f_H = f_G
    f_UO = f_I
    f_UOX = f_C
    f_LAND = f_WO

    BGC = es.enter_context(nc.sbuf_tensor("BGC", [128, 24], F32))
    INC = es.enter_context(nc.sbuf_tensor("INC", [128, 8], F32))
    BOR = es.enter_context(nc.sbuf_tensor("BOR", [1, DM], F32))
    ONE = es.enter_context(nc.sbuf_tensor("ONE", [1, 128], F32))
    ALG = es.enter_context(nc.sbuf_tensor("ALG", [1, DS], F32))
    ARW = es.enter_context(nc.sbuf_tensor("ARW", [1, DS], F32))

    PB = [es.enter_context(nc.psum_tensor(f"PB{i}", [128, 512], F32))
          for i in range(8)]

    s_d = es.enter_context(nc.semaphore("s_d"))
    s_k = es.enter_context(nc.semaphore("s_k"))
    s_o = es.enter_context(nc.semaphore("s_o"))
    s_b = [es.enter_context(nc.semaphore("s_b0")),
           es.enter_context(nc.semaphore("s_b1"))]
    s_g = es.enter_context(nc.semaphore("s_g"))
    s_gx = [es.enter_context(nc.semaphore("s_gx0")),
            es.enter_context(nc.semaphore("s_gx1"))]
    s_gw = [es.enter_context(nc.semaphore("s_gw0")),
            es.enter_context(nc.semaphore("s_gw1"))]
    s_go = [es.enter_context(nc.semaphore("s_go0")),
            es.enter_context(nc.semaphore("s_go1"))]
    s_t = es.enter_context(nc.semaphore("s_t"))
    s_v = es.enter_context(nc.semaphore("s_v"))
    s_s = es.enter_context(nc.semaphore("s_s"))
    s_p = es.enter_context(nc.semaphore("s_p"))
    s_c = es.enter_context(nc.semaphore("s_c"))
    sem_map = {"s_v": s_v, "s_s": s_s, "s_d": s_d, "s_g": s_g,
               "s_t": s_t, "s_p": s_p, "s_c": s_c,
               "s_k": s_k, "s_o": s_o, "s_b0": s_b[0], "s_b1": s_b[1],
               "s_gx0": s_gx[0], "s_gx1": s_gx[1],
               "s_gw0": s_gw[0], "s_gw1": s_gw[1],
               "s_go0": s_go[0], "s_go1": s_go[1]}

    block = es.enter_context(nc.Block())
    P = Plan()
    prog = {e: [] for e in ("sync", "gpsimd", "vector", "scalar", "tensor")}

    def em(eng, fn):
        prog[eng].append(fn)

    def wait(eng, sem, v):
        if v > 0:
            em(eng, lambda e, sem=sem, v=v: e.wait_ge(sem, v))

    def A(o, n, p0=0, pn=128):
        return AR[p0:p0 + pn, o:o + n]

    def Ab(o, nbf, p0=0, pn=128):
        return AR[p0:p0 + pn, :].bitcast(BF16)[:, 2 * o:2 * o + nbf]

    def sdma(src_ap, dst_ap, semn="s_d"):
        sem = sem_map[semn]
        em("sync", lambda e, s=src_ap, d=dst_ap, m=sem:
           e.dma_start(out=d, in_=s).then_inc(m, 16))
        return P.inc(semn, 16)

    def gdma(src_ap, dst_ap, semn="s_g"):
        sem = sem_map[semn]
        em("gpsimd", lambda g, s=src_ap, d=dst_ap, m=sem:
           g.dma_start(out=d, in_=s).then_inc(m, 16))
        return P.inc(semn, 16)

    def emv(fn):
        wait("vector", s_v, P.val("s_v"))
        em("vector", fn)
        return P.inc("s_v", 1)

    def ems(fn):
        wait("scalar", s_s, P.val("s_s"))
        em("scalar", fn)
        return P.inc("s_s", 1)

    def emp(fn):
        wait("gpsimd", s_p, P.val("s_p"))
        em("gpsimd", fn)
        return P.inc("s_p", 1)

    def tmm_group(mms, bank_ap=None):
        for fn in mms[:-1]:
            em("tensor", fn)
        em("tensor", lambda t, f=mms[-1]: f(t).then_inc(s_t, 1))
        return P.inc("s_t", 1)

    def dump(nm, src_ap, waits=()):
        if not DEBUG or nm not in dbg:
            return
        for eng_sem, v in waits:
            wait("sync", sem_map[eng_sem], v)
        sdma(src_ap, dbg[nm], "s_o")
        dv = P.val("s_o")
        wait("sync", s_o, dv)
        for e2 in ("vector", "scalar", "gpsimd", "tensor"):
            wait(e2, s_o, dv)

    bank_free = [("s_v", 0)] * 8

    def t_wait_bank(bi):
        sem, v = bank_free[bi]
        wait("tensor", sem_map[sem], v)


    o_cw = o_CL
    o_cb = o_CL + 64
    o_bd = o_CL + 80
    o_dv = o_CL + 96
    o_ac = o_CL + 112
    o_nb = o_CL + 128

    sdma(convw, A(o_cw, 64), "s_k")
    sdma(convb, A(o_cb, 16), "s_k")
    sdma(bdt, A(o_bd, 16), "s_k")
    sdma(dvec, A(o_dv, 16), "s_k")
    sdma(alog, ALG[:, :], "s_k")
    sdma(eye, A(o_EYE, 128), "s_k")
    sdma(bg, BGC[:, :], "s_k")
    sdma(incv, INC[:, :], "s_k")
    d_cst = sdma(bo, BOR[:, :], "s_k")

    wait("vector", s_k, d_cst)
    emv(lambda v: v.tensor_scalar_mul(A(o_nb, 16), A(o_bd, 16), -1.0).then_inc(s_v, 1))
    v_ones = emv(lambda v: v.memset(ONE[:, :], 1.0).then_inc(s_v, 1))
    emv(lambda v: v.memset(A(o_ST, 256), 0.0).then_inc(s_v, 1))
    emv(lambda v: v.memset(A(o_RP, 128), 1.0).then_inc(s_v, 1))
    emv(lambda v: v.memset(A(o_MISC + 48, 1), EPS).then_inc(s_v, 1))
    emv(lambda v: v.memset(A(o_XR, XRM), 0.0).then_inc(s_v, 1))

    wait("scalar", s_k, d_cst)
    v_arow = ems(lambda s: s.activation(ARW[:, :], ALG[:, :], AF.Exp).then_inc(s_s, 1))
    wait("tensor", s_s, v_arow)
    wait("tensor", s_v, v_ones)
    t_ac = tmm_group([lambda t: t.matmul(PB[0][:, 0:DS], ONE[:, 0:128],
                                         ARW[:, :], start=True, stop=True)])
    wait("vector", s_t, t_ac)
    p_acol_v = emv(lambda v: v.tensor_copy(A(o_ac, 16), PB[0][:, 0:DS])
                   .then_inc(s_v, 1))
    bank_free[0] = ("s_v", p_acol_v)
    dump("d_acol", A(o_ac, 16), [("s_v", p_acol_v)])
    dump("d_arw", ARW[:, :], [("s_s", v_arow)])

    winT_r = winT.rearrange("(blk p) o -> p blk o", p=128)
    woutT_r = woutT.rearrange("(blk p) o -> p blk o", p=128)
    xT_r = xT.rearrange("(blk p) t -> p blk t", p=128)

    g_xtb = [0, 0]
    g_xtb[0] = gdma(xT_r[:, :, 0:TC],
                    Ab(o_XTB, 2048).rearrange("p (blk t) -> p blk t", t=TC),
                    "s_gx0")

    g_wx = gdma(wxT.rearrange("(blk p) o -> p blk o", p=128),
                Ab(o_WXT, 1536).rearrange("p (blk o) -> p blk o", o=96))
    g_wdt = gdma(wdtT, Ab(o_WDT, 2048, 0, 64))

    sz_done_map = {cc2: [0] * NDB for cc2 in range(NCH)}
    v_yz_by_chunk = {}

    def emit_xz_slice_load(c2, sl):
        buf = sl % 2
        tprev = t_wsl_done.get((c2, sl - 2), t_wsl_done.get((c2 - 1, sl + 6), 0))
        wait("gpsimd", s_t, tprev)
        g_sl = gdma(winT_r[:, :, sl * 512:(sl + 1) * 512],
                    Ab(o_WSL + buf * 2048, 4096)
                    .rearrange("p (blk o) -> p blk o", o=512),
                    "s_gw" + str(buf))
        g_wsl_val[(c2, sl)] = g_sl

    def emit_xz_group_mm(c2, dout):
        sl = dout // 4
        buf = sl % 2
        bi = dout % 2
        wait("tensor", s_gw[buf], g_wsl_val[(c2, sl)])
        wait("tensor", s_gx[c2 % 2], g_xtb[c2 % 2])
        t_wait_bank(bi)
        db = dout % 4
        mms = []
        for mb in range(8):
            lhsT = Ab(o_WSL + buf * 2048, 4096)[
                :, mb * 512 + db * 128: mb * 512 + (db + 1) * 128]
            rhs = Ab(o_XTB + (c2 % 2) * 1024, 2048)[:, mb * TC:(mb + 1) * TC]
            mms.append(lambda t, o=PB[bi][:, 0:TC], l=lhsT, r=rhs,
                       st=(mb == 0), sp=(mb == 7):
                       t.matmul(o, l, r, start=st, stop=sp))
        t_mm = tmm_group(mms)
        if dout % 4 == 3:
            t_wsl_done[(c2, sl)] = P.val("s_t")
        return t_mm, bi

    def emit_xr_evac(c2, dout, t_mm, bi):
        wait("vector", s_t, t_mm)
        v_ev = emv(lambda v, ps=PB[bi][:, 0:TC],
                   d=A(o_XR + dout * 260 + 3, TC):
                   v.tensor_copy(d, ps).then_inc(s_v, 1))
        bank_free[bi] = ("s_v", v_ev)

    def emit_z_evac(c2, dout, t_mm, bi):
        zb = dout - 16
        wait("scalar", s_t, t_mm)
        if zb == 0 and v_yz_by_chunk.get(c2 - 1):
            wait("scalar", s_v, v_yz_by_chunk[c2 - 1])
        sz_done_map[c2][zb] = ems(
            lambda s, ps=PB[bi][:, 0:TC],
            d=Ab(o_SZ, MEGA)[:, zb * TC:(zb + 1) * TC]:
            s.activation(d, ps, AF.Silu).then_inc(s_s, 1))
        bank_free[bi] = ("s_s", sz_done_map[c2][zb])

    g_wsl_val = {}

    v_ln1_done = 0
    v_yz_prev = 0
    t_wsl_done = {}
    t_wos_done = {}
    t_ylast = 0
    v_acc_done = {}
    v_scan_done = {}
    v_scan_half = {}
    p_w_hist = {}
    v_chat_hist = {}
    s_a_done = {}
    d_agin_prev = 0
    cc_round = [0] * NCH

    for c in range(NCH):
        tc0 = c * TC

        wait("sync", s_v, v_ln1_done)
        if d_agin_prev:
            wait("sync", s_d, d_agin_prev)
        d_xn = sdma(xn[tc0:tc0 + TC, :].rearrange("(tb p) m -> p tb m", p=128),
                    A(o_XN, 2 * DM).rearrange("p (tb m) -> p tb m", tb=2))

        s_sz_done = sz_done_map[c]
        if c == 0:
            for sl in range(4):
                emit_xz_slice_load(0, sl)
                for db in range(4):
                    t_mm, bi = emit_xz_group_mm(0, sl * 4 + db)
                    emit_xr_evac(0, sl * 4 + db, t_mm, bi)
            for sl in range(4, 8):
                emit_xz_slice_load(0, sl)
                for db in range(4):
                    t_mm, bi = emit_xz_group_mm(0, sl * 4 + db)
                    emit_z_evac(0, sl * 4 + db, t_mm, bi)

        if c == 0:
            dump("d_xr", A(o_XR, XRM), [("s_v", P.val("s_v"))])
        for db in range(NDB):
            slot = o_XR + db * 260
            xcd = A(o_A4 + db * TC, TC)
            emv(lambda v, d=xcd, s=A(slot, TC), w=A(o_cw + db * 4, 1),
                b=A(o_cb + db, 1):
                v.tensor_scalar(d, s, w, b, ALU.mult, ALU.add).then_inc(s_v, 1))
            for j in range(1, 4):
                emv(lambda v, d=xcd, s=A(slot + j, TC), w=A(o_cw + db * 4 + j, 1):
                    v.scalar_tensor_tensor(d, s, w, d, ALU.mult, ALU.add)
                    .then_inc(s_v, 1))
        xr_r0 = A(o_XR, XRM).rearrange("p (blk w) -> p blk w", w=260)
        emv(lambda v, s=xr_r0[:, :, 256:259], d=xr_r0[:, :, 0:3]:
            v.tensor_copy(d, s).then_inc(s_v, 1))
        v_conv = P.val("s_v")

        wait("scalar", s_v, v_conv)

        s_xs = ems(lambda s: s.activation(A(o_A2, MEGA), A(o_A4, MEGA), AF.Silu)
                   .then_inc(s_s, 1))
        if c == 0:
            dump("d_xs", A(o_A2, MEGA), [("s_s", s_xs)])
        wait("gpsimd", s_s, s_xs)
        if t_ylast:
            wait("gpsimd", s_t, t_ylast)
        p_xsb = emp(lambda g: g.tensor_copy(Ab(o_B1, MEGA), A(o_A2, MEGA))
                    .then_inc(s_p, 1))

        wait("tensor", s_p, p_xsb)
        if c == 0:
            wait("tensor", s_g, g_wdt)
        t_wait_bank(4)
        mms = []
        for dib in range(NDB):
            lhsT = Ab(o_WXT, 1536)[:, dib * 96:(dib + 1) * 96]
            rhs = Ab(o_B1, MEGA)[:, dib * TC:(dib + 1) * TC]
            mms.append(lambda t, l=lhsT, r=rhs, st=(dib == 0), sp=(dib == NDB - 1):
                       t.matmul(PB[4][0:96, 0:TC], l, r, start=st, stop=sp))
        t_wx = tmm_group(mms)

        wait("vector", s_t, t_wx)
        wait("vector", s_b[0], P.val("s_b0"))
        wait("vector", s_b[1], P.val("s_b1"))
        v_xd = emv(lambda v: v.tensor_copy(A(o_XD, TC, 0, 96), PB[4][0:96, 0:TC])
                   .then_inc(s_v, 1))
        bank_free[4] = ("s_v", v_xd)
        wait("scalar", s_v, v_xd)
        s_xdb = ems(lambda s: s.copy(Ab(o_XDB, 256, 0, 64), A(o_XD, TC, 0, 64))
                    .then_inc(s_s, 1))

        wait("tensor", s_s, s_xdb)
        if c == 0:
            wait("tensor", s_g, g_wdt)
        for db in range(NDB):
            bi = 4 + (db % 2)
            t_wait_bank(bi)
            mms = [lambda t, b=bi,
                   l=Ab(o_WDT, 2048, 0, 64)[:, db * 128:(db + 1) * 128],
                   r=Ab(o_XDB, 256, 0, 64):
                   t.matmul(PB[b][:, 0:TC], l, r, start=True, stop=True)]
            t_dt = tmm_group(mms)
            wait("scalar", s_t, t_dt)
            if db == 0 and v_acc_done.get((c - 1, DS - 1)):
                wait("scalar", s_v, v_acc_done[(c - 1, DS - 1)][1])
            s_sig = ems(lambda s, b=bi, d=A(o_A1 + db * TC, TC),
                        nb=A(o_nb + db, 1):
                        s.activation(d, PB[b][:, 0:TC], AF.Sigmoid,
                                     bias=nb, scale=-1.0).then_inc(s_s, 1))
            bank_free[bi] = ("s_s", s_sig)

        s_mdt = ems(lambda s: s.activation(A(o_A4, MEGA), A(o_A1, MEGA), AF.Ln)
                    .then_inc(s_s, 1))

        if c == 0:
            dump("d_mdt", A(o_A4, MEGA), [("s_s", s_mdt)])
            dump("d_xdbl", A(o_XD, TC, 0, 96), [("s_v", v_xd)])
        wait("vector", s_s, s_mdt)
        v_u = emv(lambda v: v.scalar_tensor_tensor(
            A(o_U, MEGA), A(o_A4, MEGA), -1.0, A(o_A2, MEGA),
            ALU.mult, ALU.mult).then_inc(s_v, 1))
        for db in range(NDB):
            emv(lambda v, d=A(o_A8 + db * TC, TC), s=A(o_A2 + db * TC, TC),
                w=A(o_dv + db, 1):
                v.tensor_scalar_mul(d, s, w).then_inc(s_v, 1))
        v_yacc0 = P.val("s_v")

        t_rep_hist = {}
        d_bc_hist = {}

        def stage_bc(k):
            bcb2 = o_WDT + (k % 2) * 512
            sdma(A(o_XD, TC, 64 + k, 1), A(bcb2, TC, 64, 1), "s_b" + str(k % 2))
            d_bc_hist[k] = sdma(A(o_XD, TC, 80 + k, 1), A(bcb2 + TC, TC, 64, 1),
                                "s_b" + str(k % 2))

        wait("sync", s_v, v_xd)
        stage_bc(0)
        HM = MEGA // 2
        if c + 1 < NCH:
            wait("gpsimd", s_t, t_wsl_done[(c, 7)])
            g_xtb[(c + 1) % 2] = gdma(
                xT_r[:, :, (c + 1) * TC:(c + 2) * TC],
                Ab(o_XTB + ((c + 1) % 2) * 1024, 2048)
                .rearrange("p (blk t) -> p blk t", t=TC),
                "s_gx" + str((c + 1) % 2))
        for k in range(DS):
            par = k % 2
            pbB, pbC = PB[2 * par + 2], PB[2 * par + 3] if False else PB[2 * par + 3]
            pbB = PB[2 * par + 2]
            pbC = PB[2 * par + 3]
            if k + 1 < DS:
                wait("sync", s_t, t_rep_hist.get(k - 1, 0))
                stage_bc(k + 1)
            bcb = o_WDT + par * 512
            wait("tensor", s_b[par], d_bc_hist[k])
            if k < 2:
                t_wait_bank(2 * par + 2)
                t_wait_bank(2 * par + 3)
            pwar = v_chat_hist.get((c, k - 2), v_chat_hist.get((c - 1, DS - 1), 0))
            if pwar:
                wait("tensor", s_v, pwar)
            t_rep = tmm_group([
                lambda t, r=A(bcb, TC, 64, 1), o1=A(o_RP, 128, 64, 1), pb=pbB:
                t.matmul(pb[:, 0:TC], o1, r, start=True, stop=True),
                lambda t, r=A(bcb + TC, TC, 64, 1), o1=A(o_RP, 128, 64, 1), pb=pbC:
                t.matmul(pb[:, 0:TC], o1, r, start=True, stop=True)])
            t_rep_hist[k] = t_rep

            s_ah = [0, 0]
            for hf2 in range(2):
                aw = v_scan_half.get((c, k - 1, hf2),
                                     v_scan_half.get((c - 1, DS - 1, hf2), 0))
                if aw:
                    wait("scalar", s_v, aw)
                wait("scalar", s_v, p_acol_v)
                s_ah[hf2] = ems(lambda s, sc=A(o_ac + k, 1),
                                d=A(o_A1 + hf2 * HM, HM),
                                src=A(o_A4 + hf2 * HM, HM):
                                s.activation(d, src, AF.Exp,
                                             scale=sc).then_inc(s_s, 1))
            s_a_done[(c, k)] = s_ah[1]

            bb = pbB[:, 0:TC].unsqueeze(1).broadcast_to([128, NDB, TC])
            uu = A(o_U, MEGA).rearrange("p (blk t) -> p blk t", t=TC)
            ww = A(o_A2, MEGA).rearrange("p (blk t) -> p blk t", t=TC)
            wait("vector", s_t, t_rep)
            emv(lambda v, w=ww, u=uu, b2=bb:
                v.tensor_tensor(w, u, b2, ALU.mult).then_inc(s_v, 1))

            for db in range(NDB):
                if db == 0:
                    wait("vector", s_s, s_ah[0])
                if db == NDB // 2:
                    wait("vector", s_s, s_ah[1])
                emv(lambda v, o=A(o_A5 + db * TC, TC),
                    a=A(o_A1 + db * TC, TC),
                    w=A(o_A2 + db * TC, TC), i=A(o_ST + k * 16 + db, 1):
                    v.tensor_tensor_scan(o, a, w, i, ALU.mult, ALU.add)
                    .then_inc(s_v, 1))
                if db == NDB // 2 - 1:
                    v_scan_half[(c, k, 0)] = P.val("s_v")
            v_scan_half[(c, k, 1)] = P.val("s_v")
            if c == 0 and k == 0:
                dump("d_a0", A(o_A1, MEGA), [("s_s", s_ah[1])])
                dump("d_w0", A(o_A2, MEGA), [("s_v", P.val("s_v"))])
                dump("d_h0", A(o_A5, MEGA), [("s_v", P.val("s_v"))])
            lastc = A(o_A5, MEGA).rearrange("p (blk t) -> p blk t", t=TC)[:, :, TC - 1:TC]
            v_scan_done[(c, k)] = emv(
                lambda v, lc=lastc, d=A(o_ST + k * 16, 16).unsqueeze(2):
                v.tensor_copy(d, lc).then_inc(s_v, 1))

            cb2 = pbC[:, 0:TC].unsqueeze(1).broadcast_to([128, NDB, TC])
            hh = A(o_A5, MEGA).rearrange("p (blk t) -> p blk t", t=TC)
            v_chat = emv(lambda v, o=hh, h=hh, c2=cb2:
                         v.tensor_tensor(o, h, c2, ALU.mult).then_inc(s_v, 1))
            v_chat_hist[(c, k)] = v_chat
            bank_free[2 * par + 2] = ("s_v", v_chat)
            bank_free[2 * par + 3] = ("s_v", v_chat)
            v_acc = emv(lambda v:
                        v.tensor_tensor(A(o_A8, MEGA), A(o_A8, MEGA),
                                        A(o_A5, MEGA), ALU.add).then_inc(s_v, 1))
            v_acc_done[(c, k)] = ("v", v_acc)

            if c + 1 < NCH:
                if k % 4 == 0:
                    emit_xz_slice_load(c + 1, k // 4)
                t_mmp, bip = emit_xz_group_mm(c + 1, k)
                emit_xr_evac(c + 1, k, t_mmp, bip)



        if c == 0:
            dump("d_yacc", A(o_A8, MEGA), [("s_v", P.val("s_v"))])
        wait("vector", s_s, max(s_sz_done))
        wait("vector", s_t, t_wx)
        v_yz = emv(lambda v: v.tensor_tensor(Ab(o_B1, MEGA), A(o_A8, MEGA),
                                             Ab(o_SZ, MEGA), ALU.mult)
                   .then_inc(s_v, 1))
        v_yz_prev = v_yz
        v_yz_by_chunk[c] = v_yz

        wait("tensor", s_v, v_yz)
        for bi in (4, 5, 6, 7):
            t_wait_bank(bi)
        t_y = {}
        for dib in range(NDB):
            tprev = t_wos_done.get((c, dib - 2),
                                   t_wos_done.get((c - 1, dib + 14), 0))
            wait("gpsimd", s_t, tprev)
            g_wo = gdma(woutT_r[:, dib, :], Ab(o_WOS + (dib % 2) * 512, 1024),
                        "s_go" + str(dib % 2))
            wait("tensor", s_go[dib % 2], g_wo)
            mms = []
            for tb in range(2):
                for hf in range(2):
                    bi = 4 + tb * 2 + hf
                    lhsT = Ab(o_B1, MEGA)[:, dib * TC + tb * 128:
                                          dib * TC + tb * 128 + 128]
                    rhs = Ab(o_WOS + (dib % 2) * 512, 1024)[:, hf * 512:(hf + 1) * 512]
                    mms.append(lambda t, b=bi, l=lhsT, r=rhs, st=(dib == 0),
                               sp=(dib == NDB - 1):
                               t.matmul(PB[b][:, :], l, r, start=st, stop=sp))
            t_wos_done[(c, dib)] = tmm_group(mms)
        t_ylast = P.val("s_t")

        rs_cols = o_MISC
        v_ev_by_bank = {}
        for tb in range(2):
            for hf in range(2):
                bi = 4 + tb * 2 + hf
                wait("vector", s_t, t_ylast)
                wait("vector", s_d, d_xn)
                v_ev = emv(lambda v, d=A(o_R + tb * DM + hf * 512, 512),
                           p=PB[bi][:, :], x=A(o_XN + tb * DM + hf * 512, 512),
                           rc=A(rs_cols + tb * 2 + hf, 1):
                           v.scalar_tensor_tensor(d, p, 1.0, x, ALU.mult,
                                                  ALU.add, accum_out=rc)
                           .then_inc(s_v, 1))
                bank_free[bi] = ("s_v", v_ev)

        o_stat = o_MISC + 8
        for tb in range(2):
            mcol = A(o_stat + tb * 4 + 0, 1)
            vcol = A(o_stat + tb * 4 + 1, 1)
            rcol = A(o_stat + tb * 4 + 2, 1)
            ncol = A(o_stat + tb * 4 + 3, 1)
            tcol = A(o_MISC + 16, 1)
            emv(lambda v, m=mcol, a=A(rs_cols + tb * 2, 1),
                b2=A(rs_cols + tb * 2 + 1, 1):
                v.tensor_tensor(m, a, b2, ALU.add).then_inc(s_v, 1))
            v_m = emv(lambda v, m=mcol: v.tensor_scalar_mul(m, m, 1.0 / DM)
                      .then_inc(s_v, 1))
            wait("scalar", s_v, v_m)
            s_sq = ems(lambda s, jk=A(o_A5 + tb * DM, DM),
                       r=A(o_R + tb * DM, DM), q=vcol:
                       s.activation(jk, r, AF.Square, scale=1.0 / (DM ** 0.5),
                                    accum_out=q).then_inc(s_s, 1))
            wait("vector", s_s, s_sq)
            emv(lambda v, t2=tcol, m=mcol:
                v.tensor_tensor(t2, m, m, ALU.mult).then_inc(s_v, 1))
            v_var = emv(lambda v, q=vcol, t2=tcol:
                        v.tensor_tensor(q, q, t2, ALU.subtract).then_inc(s_v, 1))
            wait("scalar", s_v, v_var)
            ems(lambda s, q=vcol: s.activation(q, q, AF.Ln, bias=A(o_MISC + 48, 1))
                .then_inc(s_s, 1))
            s_rsc = ems(lambda s, q=vcol, r=rcol:
                        s.activation(r, q, AF.Exp, scale=-0.5).then_inc(s_s, 1))
            wait("vector", s_s, s_rsc)
            emv(lambda v, n=ncol, m=mcol, r=rcol:
                v.tensor_scalar(n, m, r, -1.0, ALU.mult, ALU.mult).then_inc(s_v, 1))
            emv(lambda v, d=A(o_XN + tb * DM, DM), r=A(o_R + tb * DM, DM),
                rc=rcol, n=ncol:
                v.tensor_scalar(d, r, rc, n, ALU.mult, ALU.add).then_inc(s_v, 1))
        v_ln1_done = P.val("s_v")

        wait("sync", s_v, v_ln1_done)
        d_agin_prev = sdma(A(o_XN, 2 * DM).rearrange("p (tb m) -> p tb m", tb=2),
                           agin[tc0:tc0 + TC, :]
                           .rearrange("(tb p) m -> p tb m", p=128))
        wait("gpsimd", s_d, d_agin_prev)
        em("gpsimd", lambda g, c2=c: g.collective_compute(
            "AllGather", ALU.bypass,
            replica_groups=[list(range(NCORE))],
            ins=[agin[c2 * TC:(c2 + 1) * TC, :].opt()],
            outs=[agall[c2, :, :].opt()]).then_inc(s_c, 1))
        cc_round[c] = P.inc("s_c", 1)

        if c + 1 < NCH:
            for sl in range(4, 8):
                emit_xz_slice_load(c + 1, sl)
                for db in range(4):
                    t_mmz, biz = emit_xz_group_mm(c + 1, sl * 4 + db)
                    emit_z_evac(c + 1, sl * 4 + db, t_mmz, biz)

        if c == 0:
            dump("d_ln1", A(o_XN, 2 * DM), [("s_v", v_ln1_done)])
            if DEBUG:
                wait("sync", s_c, cc_round[0])
                sdma(agall[0, :, :], dbg["d_ag0"], "s_o")

    v_md = P.val("s_v")
    s_md = P.val("s_s")
    t_md = P.val("s_t")
    wait("gpsimd", s_v, v_md)
    wait("gpsimd", s_s, s_md)
    wait("gpsimd", s_t, t_md)

    g_wg = gdma(wgT.rearrange("(blk p) o -> p blk o", p=128),
                Ab(f_WG, 8 * G3).rearrange("p (blk o) -> p blk o", o=G3))
    g_ug = gdma(ugT.rearrange("(blk p) o -> p blk o", p=128),
                Ab(f_UG, 8 * G3).rearrange("p (blk o) -> p blk o", o=G3))

    wait("vector", s_c, cc_round[NCH - 1])
    wait("vector", s_s, s_md)
    wait("vector", s_t, t_md)
    for mb in range(8):
        emv(lambda v, d=A(f_C + mb * TC, TC), i=INC[:, mb:mb + 1]:
            v.tensor_scalar(d, A(o_ST, TC), 0.0, i, ALU.mult, ALU.add)
            .then_inc(s_v, 1))
    v_cup = emv(lambda v: v.tensor_copy(Ab(f_CB, 2048), A(f_C, 2048))
                .then_inc(s_v, 1))

    agall_r = agall.rearrange("r (j2 tb p) m -> r j2 p tb m", tb=2, p=128)
    pid_holder = {}

    def get_pid(e):
        if "v" not in pid_holder:
            pid_holder["v"] = e.partition_id()
        return pid_holder["v"]

    wait("sync", s_c, cc_round[NCH - 1])

    def load_ag(j, dst_off):
        def dof(e, j=j, dst_off=dst_off):
            pid = get_pid(e)
            e.dma_start(out=A(dst_off, 2 * DM).rearrange("p (tb m) -> p tb m",
                                                         tb=2),
                        in_=agall_r[pid, j, :, :, :]).then_inc(s_d, 16)
        em("sync", dof)
        return P.inc("s_d", 16)

    t_xp = {}
    s_xp = {}
    d_land = {}

    def emit_tr_step(j, i):
        tb, mb = i // 8, i % 8
        bi = 2 if i % 2 == 0 else 7
        t_wait_bank(bi)
        if i == 0:
            wait("tensor", s_d, d_land[j])
        src = A(f_LAND + tb * DM + mb * 128, 128)
        t_tr = tmm_group([lambda t, s2=src, b=bi:
                          t.transpose(PB[b][:, 0:128], s2, A(o_EYE, 128))])
        wait("scalar", s_t, t_tr)
        dstb = Ab(f_XTT + j * 1024, 2048)[
            :, mb * TC + tb * 128: mb * TC + tb * 128 + 128]
        s_cp = ems(lambda s, d=dstb, b=bi:
                   s.copy(d, PB[b][:, 0:128]).then_inc(s_s, 1))
        bank_free[bi] = ("s_s", s_cp)
        if i == 15:
            t_xp[j] = t_tr
            s_xp[j] = s_cp

    d_land[0] = load_ag(0, f_LAND)
    for i in range(16):
        emit_tr_step(0, i)

    v_cstep = {}
    for j in range(B):
        if j + 1 < B:
            wait("sync", s_t, t_xp[j])
            d_land[j + 1] = load_ag(j + 1, f_LAND)

        wait("tensor", s_g, g_ug)
        wait("tensor", s_s, s_xp[j])
        wait("tensor", s_v, v_cup if j == 0 else v_cstep[j - 1])
        cb_r = f_CB + (j % 2) * 1024
        cb_w = f_CB + ((j + 1) % 2) * 1024
        GB_BANKS = (0, 1, 3, 4)
        seq = 0
        for mb8 in range(8):
            s_act3 = 0
            for kind in range(3):
                gb = kind * 8 + mb8
                bi = GB_BANKS[seq % 4]
                t_wait_bank(bi)
                mms = []
                for mb in range(8):
                    lhsT = Ab(f_WG, 8 * G3)[:, mb * G3 + gb * 128: mb * G3 + (gb + 1) * 128]
                    rhs = Ab(f_XTT + j * 1024, 2048)[:, mb * TC:(mb + 1) * TC]
                    mms.append(lambda t, b=bi, l=lhsT, r=rhs, st=(mb == 0):
                               t.matmul(PB[b][:, 0:TC], l, r, start=st, stop=False))
                for mb in range(8):
                    lhsT = Ab(f_UG, 8 * G3)[:, mb * G3 + gb * 128: mb * G3 + (gb + 1) * 128]
                    rhs = Ab(cb_r, 2048)[:, mb * TC:(mb + 1) * TC]
                    mms.append(lambda t, b=bi, l=lhsT, r=rhs, sp=(mb == 7):
                               t.matmul(PB[b][:, 0:TC], l, r, start=False, stop=sp))
                t_g = tmm_group(mms)
                dst = [f_F, f_I, f_G][kind] + mb8 * TC
                func = AF.Tanh if kind == 2 else AF.Sigmoid
                wait("scalar", s_t, t_g)
                s_act3 = ems(lambda s, b=bi, d=A(dst, TC),
                             bb2=BGC[:, gb:gb + 1], fn=func:
                             s.activation(d, PB[b][:, 0:TC], fn, bias=bb2)
                             .then_inc(s_s, 1))
                bank_free[bi] = ("s_s", s_act3)
                if j + 1 < B and 1 <= seq <= 16:
                    emit_tr_step(j + 1, seq - 1)
                seq += 1
            wait("vector", s_s, s_act3)
            fsl = A(f_F + mb8 * TC, TC)
            isl = A(f_I + mb8 * TC, TC)
            gsl = A(f_G + mb8 * TC, TC)
            csl = A(f_C + mb8 * TC, TC)
            emv(lambda v, f=fsl, c2=csl: v.tensor_tensor(f, f, c2, ALU.mult)
                .then_inc(s_v, 1))
            emv(lambda v, i=isl, g2=gsl: v.tensor_tensor(g2, i, g2, ALU.mult)
                .then_inc(s_v, 1))
            emv(lambda v, c2=csl, f=fsl, g2=gsl:
                v.tensor_tensor(c2, f, g2, ALU.add).then_inc(s_v, 1))
            v_cstep[j] = emv(
                lambda v, d=Ab(cb_w, 2048)[:, mb8 * TC:(mb8 + 1) * TC], s2=csl:
                v.tensor_copy(d, s2).then_inc(s_v, 1))
        if j == 0:
            dump("d_c0", A(f_C, 2048), [("s_v", v_cstep[j])])

    wait("gpsimd", s_s, P.val("s_s"))
    wait("gpsimd", s_v, P.val("s_v"))
    wait("gpsimd", s_t, t_xp[B - 1])
    g_wo2 = gdma(woT.rearrange("(blk p) o -> p blk o", p=128),
                 Ab(f_WO, 8 * DM).rearrange("p (blk o) -> p blk o", o=DM),
                 "s_gw0")
    g_uo2 = gdma(uoT.rearrange("(blk p) o -> p blk o", p=128),
                 Ab(f_UO, 8 * DM).rearrange("p (blk o) -> p blk o", o=DM),
                 "s_gw1")

    v_cF = v_cstep[B - 1]
    wait("tensor", s_gw[1], g_uo2)
    wait("tensor", s_v, v_cF)
    s_hoist = 0
    for pb2 in range(2):
        for hf in range(2):
            bi = 5 + hf
            t_wait_bank(bi)
            mms = []
            for mb in range(8):
                lhsT = Ab(f_CB, 2048)[:, mb * TC + pb2 * 128: mb * TC + pb2 * 128 + 128]
                rhs = Ab(f_UO, 8 * DM)[:, mb * DM + hf * 512: mb * DM + (hf + 1) * 512]
                mms.append(lambda t, b=bi, l=lhsT, r=rhs, st=(mb == 0):
                           t.matmul(PB[b][:, :], l, r, start=st, stop=False))
            mms.append(lambda t, b=bi, h=hf:
                       t.matmul(PB[b][:, :], ONE[:, 0:128],
                                BOR[:, h * 512:(h + 1) * 512],
                                start=False, stop=True))
            t_uo = tmm_group(mms)
            wait("scalar", s_t, t_uo)
            s_hoist = ems(lambda s, b=bi,
                          d=A(f_UOX + pb2 * DM + hf * 512, 512):
                          s.copy(d, PB[b][:, :]).then_inc(s_s, 1))
            bank_free[bi] = ("s_s", s_hoist)

    v_ln2 = 0
    d_out_prev = 0
    v_first = True
    for j in range(B):
        if j == 0:
            wait("sync", s_v, v_cF)
        else:
            wait("sync", s_o, d_out_prev)
        d_x = load_ag(j, f_X32)

        s_tanh_last = 0
        for pb2 in range(2):
            for hf in range(2):
                bi = 5 + hf
                t_wait_bank(bi)
                if j == 0 and pb2 == 0 and hf == 0:
                    wait("tensor", s_gw[0], g_wo2)
                    wait("tensor", s_s, s_xp[B - 1])
                mms = []
                for mb in range(8):
                    lhsT = Ab(f_XTT + j * 1024, 2048)[
                        :, mb * TC + pb2 * 128: mb * TC + pb2 * 128 + 128]
                    rhs = Ab(f_WO, 8 * DM)[:, mb * DM + hf * 512: mb * DM + (hf + 1) * 512]
                    mms.append(lambda t, b=bi, l=lhsT, r=rhs, st=(mb == 0),
                               sp=(mb == 7):
                               t.matmul(PB[b][:, :], l, r, start=st, stop=sp))
                t_h = tmm_group(mms)
                wait("vector", s_t, t_h)
                if v_first:
                    wait("vector", s_s, s_hoist)
                    v_first = False
                v_o = emv(lambda v, b=bi,
                          d=A(f_H + pb2 * DM + hf * 512, 512),
                          u=A(f_UOX + pb2 * DM + hf * 512, 512):
                          v.scalar_tensor_tensor(d, PB[b][:, :], 1.0, u,
                                                 ALU.mult, ALU.add)
                          .then_inc(s_v, 1))
                bank_free[bi] = ("s_v", v_o)
                wait("scalar", s_v, v_o)
                s_tanh_last = ems(lambda s,
                                  d=A(f_H + pb2 * DM + hf * 512, 512):
                                  s.activation(d, d, AF.Tanh).then_inc(s_s, 1))

        if j == 0:
            dump("d_hf0", A(f_H, 2048), [("s_s", s_tanh_last)])
        wait("vector", s_s, s_tanh_last)
        wait("vector", s_d, d_x)
        rs2 = o_MISC + 24
        for tb in range(2):
            for hf in range(2):
                emv(lambda v, d=A(f_H + tb * DM + hf * 512, 512),
                    x=A(f_X32 + tb * DM + hf * 512, 512),
                    rc=A(rs2 + tb * 2 + hf, 1):
                    v.scalar_tensor_tensor(d, d, 1.0, x, ALU.mult, ALU.add,
                                           accum_out=rc).then_inc(s_v, 1))
        o_st2 = o_MISC + 32
        for tb in range(2):
            mcol = A(o_st2 + tb * 4 + 0, 1)
            vcol = A(o_st2 + tb * 4 + 1, 1)
            rcol = A(o_st2 + tb * 4 + 2, 1)
            ncol = A(o_st2 + tb * 4 + 3, 1)
            tcol = A(o_MISC + 40, 1)
            emv(lambda v, m=mcol, a=A(rs2 + tb * 2, 1), b2=A(rs2 + tb * 2 + 1, 1):
                v.tensor_tensor(m, a, b2, ALU.add).then_inc(s_v, 1))
            v_m = emv(lambda v, m=mcol: v.tensor_scalar_mul(m, m, 1.0 / DM)
                      .then_inc(s_v, 1))
            wait("scalar", s_v, v_m)
            s_sq = ems(lambda s, jk=A(f_F + tb * DM, DM), r=A(f_H + tb * DM, DM),
                       q=vcol:
                       s.activation(jk, r, AF.Square, scale=1.0 / (DM ** 0.5),
                                    accum_out=q).then_inc(s_s, 1))
            wait("vector", s_s, s_sq)
            emv(lambda v, t2=tcol, m=mcol: v.tensor_tensor(t2, m, m, ALU.mult)
                .then_inc(s_v, 1))
            v_var = emv(lambda v, q=vcol, t2=tcol:
                        v.tensor_tensor(q, q, t2, ALU.subtract).then_inc(s_v, 1))
            wait("scalar", s_v, v_var)
            ems(lambda s, q=vcol: s.activation(q, q, AF.Ln, bias=A(o_MISC + 48, 1))
                .then_inc(s_s, 1))
            s_rsc = ems(lambda s, q=vcol, r=rcol:
                        s.activation(r, q, AF.Exp, scale=-0.5).then_inc(s_s, 1))
            wait("vector", s_s, s_rsc)
            emv(lambda v, n=ncol, m=mcol, r=rcol:
                v.tensor_scalar(n, m, r, -1.0, ALU.mult, ALU.mult).then_inc(s_v, 1))
            emv(lambda v, d=A(f_F + tb * DM, DM), r=A(f_H + tb * DM, DM),
                rc=rcol, n=ncol:
                v.tensor_scalar(d, r, rc, n, ALU.mult, ALU.add).then_inc(s_v, 1))
        v_ln2 = P.val("s_v")

        wait("sync", s_v, v_ln2)
        d_out_prev = sdma(A(f_F, 2 * DM).rearrange("p (tb m) -> p tb m", tb=2),
                          out[j, :, :].rearrange("(tb p) m -> p tb m", p=128),
                          "s_o")

    wait("sync", s_d, P.val("s_d"))
    wait("sync", s_o, P.val("s_o"))

    @block.gpsimd
    def _(g):
        for fn in prog["gpsimd"]:
            fn(g)

    @block.sync
    def _(e):
        for fn in prog["sync"]:
            fn(e)

    @block.vector
    def _(v):
        for fn in prog["vector"]:
            fn(v)

    @block.scalar
    def _(s):
        for fn in prog["scalar"]:
            fn(s)

    @block.tensor
    def _(t):
        for fn in prog["tensor"]:
            fn(t)

    es.close()
    return nc


_NC_CACHE = {}


def kernel(L0_attn_x, L0_attn_W_in, L0_attn_conv_w, L0_attn_conv_b, L0_attn_A_log,
           L0_attn_D, L0_attn_W_x, L0_attn_W_dt, L0_attn_b_dt, L0_attn_W_out,
           L0_ffn_wf, L0_ffn_uf, L0_ffn_bf, L0_ffn_wi, L0_ffn_ui, L0_ffn_bi,
           L0_ffn_wc, L0_ffn_uc, L0_ffn_bc, L0_ffn_wo, L0_ffn_uo, L0_ffn_bo,
           L0_ffn_in_c):
    x = np.asarray(L0_attn_x, dtype=np.float32)

    def T(a):
        return np.ascontiguousarray(np.asarray(a, dtype=np.float32).T)

    def colv(a):
        v = np.asarray(a, dtype=np.float32).reshape(-1)
        return np.ascontiguousarray(v.reshape(-1, 128).T)

    shared = {
        "winT": T(L0_attn_W_in),
        "wxT": T(L0_attn_W_x),
        "wdtT": T(L0_attn_W_dt),
        "woutT": T(L0_attn_W_out),
        "convw": np.ascontiguousarray(
            np.asarray(L0_attn_conv_w, dtype=np.float32)
            .reshape(16, 128, 4).transpose(1, 0, 2).reshape(128, 64)),
        "convb": colv(L0_attn_conv_b),
        "bdt": colv(L0_attn_b_dt),
        "alog": np.ascontiguousarray(
            np.asarray(L0_attn_A_log, dtype=np.float32).reshape(1, -1)),
        "dvec": colv(L0_attn_D),
        "wgT": np.ascontiguousarray(np.concatenate(
            [np.asarray(L0_ffn_wf), np.asarray(L0_ffn_wi),
             np.asarray(L0_ffn_wc)], axis=0).T.astype(np.float32)),
        "ugT": np.ascontiguousarray(np.concatenate(
            [np.asarray(L0_ffn_uf), np.asarray(L0_ffn_ui),
             np.asarray(L0_ffn_uc)], axis=0).T.astype(np.float32)),
        "woT": T(L0_ffn_wo),
        "uoT": T(L0_ffn_uo),
        "bg": colv(np.concatenate(
            [np.asarray(L0_ffn_bf), np.asarray(L0_ffn_bi),
             np.asarray(L0_ffn_bc)], axis=0)),
        "bo": np.ascontiguousarray(
            np.asarray(L0_ffn_bo, dtype=np.float32).reshape(1, -1)),
        "incv": colv(L0_ffn_in_c),
        "eye": np.eye(128, dtype=np.float32),
    }

    in_maps = []
    for j in range(NCORE):
        m = dict(shared)
        m["xT"] = np.ascontiguousarray(x[j].T)
        m["xn"] = np.ascontiguousarray(x[j])
        in_maps.append(m)

    if "nc" not in _NC_CACHE:
        _NC_CACHE["nc"] = build_nc()
    nc = _NC_CACHE["nc"]

    res = run_bass_kernel_spmd(nc, in_maps, core_ids=list(range(NCORE)))

    full = np.empty((B, L, DM), dtype=np.float32)
    for i in range(NCORE):
        full[:, i * POS:(i + 1) * POS, :] = res.results[i]["out"]
    return full


# version 14
# speedup vs baseline: 1.3332x; 1.0028x over previous
import numpy as np
from contextlib import ExitStack

import concourse.bass as bass
import concourse.mybir as mybir
from concourse.bass_utils import run_bass_kernel_spmd
from concourse import library_config

F32 = mybir.dt.float32
BF16 = mybir.dt.bfloat16
AF = mybir.ActivationFunctionType
ALU = mybir.AluOpType

B = 8
L = 2048
DM = 1024
DI = 2048
DS = 16
RK = 64
TC = 256
NCH = L // TC
NDB = DI // 128
POS = L // 8
G3 = 3 * DM
EPS = 1e-5
NCORE = 8
MEGA = NDB * TC
XRM = NDB * 260


class Plan:
    def __init__(self):
        self.c = {}

    def inc(self, name, amt):
        self.c[name] = self.c.get(name, 0) + amt
        return self.c[name]

    def val(self, name):
        return self.c.get(name, 0)


DEBUG = False


def build_nc():
    nc = bass.Bass("TRN2", target_bir_lowering=False, debug=False)

    def din(name, shape):
        return nc.dram_tensor(name, shape, F32, kind="ExternalInput").ap()

    xT = din("xT", [DM, L])
    xn = din("xn", [L, DM])
    winT = din("winT", [DM, 2 * DI])
    wxT = din("wxT", [DI, RK + 2 * DS])
    wdtT = din("wdtT", [RK, DI])
    woutT = din("woutT", [DI, DM])
    convw = din("convw", [128, 64])
    convb = din("convb", [128, 16])
    bdt = din("bdt", [128, 16])
    alog = din("alog", [1, DS])
    dvec = din("dvec", [128, 16])
    wgT = din("wgT", [DM, G3])
    ugT = din("ugT", [DM, G3])
    woT = din("woT", [DM, DM])
    uoT = din("uoT", [DM, DM])
    bg = din("bg", [128, 24])
    bo = din("bo", [1, DM])
    incv = din("incv", [128, 8])
    eye = din("eye", [128, 128])

    out = nc.dram_tensor("out", [B, POS, DM], F32, kind="ExternalOutput").ap()
    dbg = {}
    if DEBUG:
        for nm, shp in [("d_xr", [128, XRM]), ("d_xs", [128, MEGA]),
                        ("d_mdt", [128, MEGA]), ("d_xdbl", [96, TC]),
                        ("d_w0", [128, MEGA]), ("d_a0", [128, MEGA]),
                        ("d_h0", [128, MEGA]), ("d_yacc", [128, MEGA]),
                        ("d_ln1", [128, 2 * DM]), ("d_ag0", [8 * TC, DM]),
                        ("d_c0", [128, 2048]), ("d_hf0", [128, 2048]),
                        ("d_acol", [128, 16]), ("d_arw", [1, 16])]:
            dbg[nm] = nc.dram_tensor(nm, shp, F32, kind="ExternalOutput").ap()
    agin = nc.dram_tensor("agin", [L, DM], F32)
    agall = nc.dram_tensor("agall", [NCH, 8 * TC, DM], F32,
                           addr_space="Shared")

    es = ExitStack()

    AR_COLS = 47552
    AR = es.enter_context(nc.sbuf_tensor("AR", [128, AR_COLS], F32))
    _ptr = [0]

    def areg(n):
        o = _ptr[0]
        _ptr[0] += n
        assert _ptr[0] <= AR_COLS, f"arena overflow {_ptr[0]}"
        return o

    o_XR = areg(XRM)
    o_U = areg(MEGA)
    o_A1 = areg(MEGA)
    o_A2 = areg(MEGA)
    o_A4 = areg(MEGA)
    o_A5 = areg(MEGA)
    o_A8 = areg(MEGA)
    o_SZ = areg(MEGA // 2)
    o_B1 = areg(MEGA // 2)
    o_XTB = areg(2 * 1024)
    o_XN = areg(2 * DM)
    o_R = areg(2 * DM)
    o_WSL = areg(2 * 2048)
    o_WOS = areg(2 * 512)
    o_WXT = areg(768)
    o_WDT = areg(1024)
    o_XD = areg(256)
    o_XDB = areg(256)
    o_RP = areg(2 * TC)
    o_ST = areg(256)
    o_CL = areg(160)
    o_EYE = areg(128)
    o_MISC = areg(64)
    mamba_top = _ptr[0]

    _fp = [0]

    def freg(n):
        o = _fp[0]
        _fp[0] += n
        assert _fp[0] <= mamba_top, f"ffn overflow {_fp[0]} > {mamba_top}"
        return o

    f_WG = freg(12288)
    f_UG = freg(12288)
    f_WO = freg(4096)
    f_XTT = freg(8 * 1024)
    f_C = freg(2048)
    f_CB = freg(2 * 1024)
    f_F = freg(2048)
    f_I = freg(2048)
    f_G = freg(2048)
    f_X32 = f_F
    f_H = f_G
    f_UO = f_I
    f_UOX = f_C
    f_LAND = f_WO

    BGC = es.enter_context(nc.sbuf_tensor("BGC", [128, 24], F32))
    INC = es.enter_context(nc.sbuf_tensor("INC", [128, 8], F32))
    BOR = es.enter_context(nc.sbuf_tensor("BOR", [1, DM], F32))
    ONE = es.enter_context(nc.sbuf_tensor("ONE", [1, 128], F32))
    ALG = es.enter_context(nc.sbuf_tensor("ALG", [1, DS], F32))
    ARW = es.enter_context(nc.sbuf_tensor("ARW", [1, DS], F32))

    PB = [es.enter_context(nc.psum_tensor(f"PB{i}", [128, 512], F32))
          for i in range(8)]

    s_d = es.enter_context(nc.semaphore("s_d"))
    s_k = es.enter_context(nc.semaphore("s_k"))
    s_o = es.enter_context(nc.semaphore("s_o"))
    s_b = [es.enter_context(nc.semaphore("s_b0")),
           es.enter_context(nc.semaphore("s_b1"))]
    s_g = es.enter_context(nc.semaphore("s_g"))
    s_gx = [es.enter_context(nc.semaphore("s_gx0")),
            es.enter_context(nc.semaphore("s_gx1"))]
    s_gw = [es.enter_context(nc.semaphore("s_gw0")),
            es.enter_context(nc.semaphore("s_gw1"))]
    s_go = [es.enter_context(nc.semaphore("s_go0")),
            es.enter_context(nc.semaphore("s_go1"))]
    s_t = es.enter_context(nc.semaphore("s_t"))
    s_v = es.enter_context(nc.semaphore("s_v"))
    s_s = es.enter_context(nc.semaphore("s_s"))
    s_p = es.enter_context(nc.semaphore("s_p"))
    s_c = es.enter_context(nc.semaphore("s_c"))
    sem_map = {"s_v": s_v, "s_s": s_s, "s_d": s_d, "s_g": s_g,
               "s_t": s_t, "s_p": s_p, "s_c": s_c,
               "s_k": s_k, "s_o": s_o, "s_b0": s_b[0], "s_b1": s_b[1],
               "s_gx0": s_gx[0], "s_gx1": s_gx[1],
               "s_gw0": s_gw[0], "s_gw1": s_gw[1],
               "s_go0": s_go[0], "s_go1": s_go[1]}

    block = es.enter_context(nc.Block())
    P = Plan()
    prog = {e: [] for e in ("sync", "gpsimd", "vector", "scalar", "tensor")}

    def em(eng, fn):
        prog[eng].append(fn)

    def wait(eng, sem, v):
        if v > 0:
            em(eng, lambda e, sem=sem, v=v: e.wait_ge(sem, v))

    def A(o, n, p0=0, pn=128):
        return AR[p0:p0 + pn, o:o + n]

    def Ab(o, nbf, p0=0, pn=128):
        return AR[p0:p0 + pn, :].bitcast(BF16)[:, 2 * o:2 * o + nbf]

    def sdma(src_ap, dst_ap, semn="s_d"):
        sem = sem_map[semn]
        em("sync", lambda e, s=src_ap, d=dst_ap, m=sem:
           e.dma_start(out=d, in_=s).then_inc(m, 16))
        return P.inc(semn, 16)

    def gdma(src_ap, dst_ap, semn="s_g"):
        sem = sem_map[semn]
        em("gpsimd", lambda g, s=src_ap, d=dst_ap, m=sem:
           g.dma_start(out=d, in_=s).then_inc(m, 16))
        return P.inc(semn, 16)

    def emv(fn):
        wait("vector", s_v, P.val("s_v"))
        em("vector", fn)
        return P.inc("s_v", 1)

    def ems(fn):
        wait("scalar", s_s, P.val("s_s"))
        em("scalar", fn)
        return P.inc("s_s", 1)

    def emp(fn):
        wait("gpsimd", s_p, P.val("s_p"))
        em("gpsimd", fn)
        return P.inc("s_p", 1)

    def tmm_group(mms, bank_ap=None):
        for fn in mms[:-1]:
            em("tensor", fn)
        em("tensor", lambda t, f=mms[-1]: f(t).then_inc(s_t, 1))
        return P.inc("s_t", 1)

    def dump(nm, src_ap, waits=()):
        if not DEBUG or nm not in dbg:
            return
        for eng_sem, v in waits:
            wait("sync", sem_map[eng_sem], v)
        sdma(src_ap, dbg[nm], "s_o")
        dv = P.val("s_o")
        wait("sync", s_o, dv)
        for e2 in ("vector", "scalar", "gpsimd", "tensor"):
            wait(e2, s_o, dv)

    bank_free = [("s_v", 0)] * 8

    def t_wait_bank(bi):
        sem, v = bank_free[bi]
        wait("tensor", sem_map[sem], v)


    o_cw = o_CL
    o_cb = o_CL + 64
    o_bd = o_CL + 80
    o_dv = o_CL + 96
    o_ac = o_CL + 112
    o_nb = o_CL + 128

    sdma(convw, A(o_cw, 64), "s_k")
    sdma(convb, A(o_cb, 16), "s_k")
    sdma(bdt, A(o_bd, 16), "s_k")
    sdma(dvec, A(o_dv, 16), "s_k")
    sdma(alog, ALG[:, :], "s_k")
    sdma(eye, A(o_EYE, 128), "s_k")
    sdma(bg, BGC[:, :], "s_k")
    sdma(incv, INC[:, :], "s_k")
    d_cst = sdma(bo, BOR[:, :], "s_k")

    wait("vector", s_k, d_cst)
    emv(lambda v: v.tensor_scalar_mul(A(o_nb, 16), A(o_bd, 16), -1.0).then_inc(s_v, 1))
    v_ones = emv(lambda v: v.memset(ONE[:, :], 1.0).then_inc(s_v, 1))
    emv(lambda v: v.memset(A(o_ST, 256), 0.0).then_inc(s_v, 1))
    emv(lambda v: v.memset(A(o_RP, 128), 1.0).then_inc(s_v, 1))
    emv(lambda v: v.memset(A(o_MISC + 48, 1), EPS).then_inc(s_v, 1))
    emv(lambda v: v.memset(A(o_XR, XRM), 0.0).then_inc(s_v, 1))

    wait("scalar", s_k, d_cst)
    v_arow = ems(lambda s: s.activation(ARW[:, :], ALG[:, :], AF.Exp).then_inc(s_s, 1))
    wait("tensor", s_s, v_arow)
    wait("tensor", s_v, v_ones)
    t_ac = tmm_group([lambda t: t.matmul(PB[0][:, 0:DS], ONE[:, 0:128],
                                         ARW[:, :], start=True, stop=True)])
    wait("vector", s_t, t_ac)
    p_acol_v = emv(lambda v: v.tensor_copy(A(o_ac, 16), PB[0][:, 0:DS])
                   .then_inc(s_v, 1))
    bank_free[0] = ("s_v", p_acol_v)
    dump("d_acol", A(o_ac, 16), [("s_v", p_acol_v)])
    dump("d_arw", ARW[:, :], [("s_s", v_arow)])

    winT_r = winT.rearrange("(blk p) o -> p blk o", p=128)
    woutT_r = woutT.rearrange("(blk p) o -> p blk o", p=128)
    xT_r = xT.rearrange("(blk p) t -> p blk t", p=128)

    g_xtb = [0, 0]
    g_xtb[0] = gdma(xT_r[:, :, 0:TC],
                    Ab(o_XTB, 2048).rearrange("p (blk t) -> p blk t", t=TC),
                    "s_gx0")

    g_wx = gdma(wxT.rearrange("(blk p) o -> p blk o", p=128),
                Ab(o_WXT, 1536).rearrange("p (blk o) -> p blk o", o=96))
    g_wdt = gdma(wdtT, Ab(o_WDT, 2048, 0, 64))

    sz_done_map = {cc2: [0] * NDB for cc2 in range(NCH)}
    v_yz_by_chunk = {}

    def emit_xz_slice_load(c2, sl):
        buf = sl % 2
        tprev = t_wsl_done.get((c2, sl - 2), t_wsl_done.get((c2 - 1, sl + 6), 0))
        wait("gpsimd", s_t, tprev)
        g_sl = gdma(winT_r[:, :, sl * 512:(sl + 1) * 512],
                    Ab(o_WSL + buf * 2048, 4096)
                    .rearrange("p (blk o) -> p blk o", o=512),
                    "s_gw" + str(buf))
        g_wsl_val[(c2, sl)] = g_sl

    def emit_xz_group_mm(c2, dout):
        sl = dout // 4
        buf = sl % 2
        bi = dout % 2
        wait("tensor", s_gw[buf], g_wsl_val[(c2, sl)])
        wait("tensor", s_gx[c2 % 2], g_xtb[c2 % 2])
        t_wait_bank(bi)
        db = dout % 4
        mms = []
        for mb in range(8):
            lhsT = Ab(o_WSL + buf * 2048, 4096)[
                :, mb * 512 + db * 128: mb * 512 + (db + 1) * 128]
            rhs = Ab(o_XTB + (c2 % 2) * 1024, 2048)[:, mb * TC:(mb + 1) * TC]
            mms.append(lambda t, o=PB[bi][:, 0:TC], l=lhsT, r=rhs,
                       st=(mb == 0), sp=(mb == 7):
                       t.matmul(o, l, r, start=st, stop=sp))
        t_mm = tmm_group(mms)
        if dout % 4 == 3:
            t_wsl_done[(c2, sl)] = P.val("s_t")
        return t_mm, bi

    def emit_xr_evac(c2, dout, t_mm, bi):
        wait("vector", s_t, t_mm)
        v_ev = emv(lambda v, ps=PB[bi][:, 0:TC],
                   d=A(o_XR + dout * 260 + 3, TC):
                   v.tensor_copy(d, ps).then_inc(s_v, 1))
        bank_free[bi] = ("s_v", v_ev)

    def emit_z_evac(c2, dout, t_mm, bi):
        zb = dout - 16
        wait("scalar", s_t, t_mm)
        if zb == 0 and v_yz_by_chunk.get(c2 - 1):
            wait("scalar", s_v, v_yz_by_chunk[c2 - 1])
        sz_done_map[c2][zb] = ems(
            lambda s, ps=PB[bi][:, 0:TC],
            d=Ab(o_SZ, MEGA)[:, zb * TC:(zb + 1) * TC]:
            s.activation(d, ps, AF.Silu).then_inc(s_s, 1))
        bank_free[bi] = ("s_s", sz_done_map[c2][zb])

    g_wsl_val = {}

    v_ln1_done = 0
    v_yz_prev = 0
    t_wsl_done = {}
    t_wos_done = {}
    t_ylast = 0
    v_acc_done = {}
    v_scan_done = {}
    v_scan_half = {}
    p_w_hist = {}
    v_chat_hist = {}
    s_a_done = {}
    d_agin_prev = 0
    cc_round = [0] * NCH

    for c in range(NCH):
        tc0 = c * TC

        wait("sync", s_v, v_ln1_done)
        if d_agin_prev:
            wait("sync", s_d, d_agin_prev)
        d_xn = sdma(xn[tc0:tc0 + TC, :].rearrange("(tb p) m -> p tb m", p=128),
                    A(o_XN, 2 * DM).rearrange("p (tb m) -> p tb m", tb=2))

        s_sz_done = sz_done_map[c]
        if c == 0:
            for sl in range(4):
                emit_xz_slice_load(0, sl)
                for db in range(4):
                    t_mm, bi = emit_xz_group_mm(0, sl * 4 + db)
                    emit_xr_evac(0, sl * 4 + db, t_mm, bi)
            for sl in range(4, 8):
                emit_xz_slice_load(0, sl)
                for db in range(4):
                    t_mm, bi = emit_xz_group_mm(0, sl * 4 + db)
                    emit_z_evac(0, sl * 4 + db, t_mm, bi)

        if c == 0:
            dump("d_xr", A(o_XR, XRM), [("s_v", P.val("s_v"))])
        for db in range(NDB):
            slot = o_XR + db * 260
            xcd = A(o_A4 + db * TC, TC)
            emv(lambda v, d=xcd, s=A(slot, TC), w=A(o_cw + db * 4, 1),
                b=A(o_cb + db, 1):
                v.tensor_scalar(d, s, w, b, ALU.mult, ALU.add).then_inc(s_v, 1))
            for j in range(1, 4):
                emv(lambda v, d=xcd, s=A(slot + j, TC), w=A(o_cw + db * 4 + j, 1):
                    v.scalar_tensor_tensor(d, s, w, d, ALU.mult, ALU.add)
                    .then_inc(s_v, 1))
        xr_r0 = A(o_XR, XRM).rearrange("p (blk w) -> p blk w", w=260)
        emv(lambda v, s=xr_r0[:, :, 256:259], d=xr_r0[:, :, 0:3]:
            v.tensor_copy(d, s).then_inc(s_v, 1))
        v_conv = P.val("s_v")

        wait("scalar", s_v, v_conv)

        s_xs = ems(lambda s: s.activation(A(o_A2, MEGA), A(o_A4, MEGA), AF.Silu)
                   .then_inc(s_s, 1))
        if c == 0:
            dump("d_xs", A(o_A2, MEGA), [("s_s", s_xs)])
        wait("gpsimd", s_s, s_xs)
        if t_ylast:
            wait("gpsimd", s_t, t_ylast)
        p_xsb = emp(lambda g: g.tensor_copy(Ab(o_B1, MEGA), A(o_A2, MEGA))
                    .then_inc(s_p, 1))

        wait("tensor", s_p, p_xsb)
        if c == 0:
            wait("tensor", s_g, g_wdt)
        t_wait_bank(4)
        mms = []
        for dib in range(NDB):
            lhsT = Ab(o_WXT, 1536)[:, dib * 96:(dib + 1) * 96]
            rhs = Ab(o_B1, MEGA)[:, dib * TC:(dib + 1) * TC]
            mms.append(lambda t, l=lhsT, r=rhs, st=(dib == 0), sp=(dib == NDB - 1):
                       t.matmul(PB[4][0:96, 0:TC], l, r, start=st, stop=sp))
        t_wx = tmm_group(mms)

        wait("vector", s_t, t_wx)
        wait("vector", s_b[0], P.val("s_b0"))
        wait("vector", s_b[1], P.val("s_b1"))
        v_xd = emv(lambda v: v.tensor_copy(A(o_XD, TC, 0, 96), PB[4][0:96, 0:TC])
                   .then_inc(s_v, 1))
        bank_free[4] = ("s_v", v_xd)
        wait("scalar", s_v, v_xd)
        s_xdb = ems(lambda s: s.copy(Ab(o_XDB, 256, 0, 64), A(o_XD, TC, 0, 64))
                    .then_inc(s_s, 1))

        wait("tensor", s_s, s_xdb)
        if c == 0:
            wait("tensor", s_g, g_wdt)
        for db in range(NDB):
            bi = 4 + (db % 2)
            t_wait_bank(bi)
            mms = [lambda t, b=bi,
                   l=Ab(o_WDT, 2048, 0, 64)[:, db * 128:(db + 1) * 128],
                   r=Ab(o_XDB, 256, 0, 64):
                   t.matmul(PB[b][:, 0:TC], l, r, start=True, stop=True)]
            t_dt = tmm_group(mms)
            wait("scalar", s_t, t_dt)
            if db == 0 and v_acc_done.get((c - 1, DS - 1)):
                wait("scalar", s_v, v_acc_done[(c - 1, DS - 1)][1])
            s_sig = ems(lambda s, b=bi, d=A(o_A1 + db * TC, TC),
                        nb=A(o_nb + db, 1):
                        s.activation(d, PB[b][:, 0:TC], AF.Sigmoid,
                                     bias=nb, scale=-1.0).then_inc(s_s, 1))
            bank_free[bi] = ("s_s", s_sig)

        s_mdt = ems(lambda s: s.activation(A(o_A4, MEGA), A(o_A1, MEGA), AF.Ln)
                    .then_inc(s_s, 1))

        if c == 0:
            dump("d_mdt", A(o_A4, MEGA), [("s_s", s_mdt)])
            dump("d_xdbl", A(o_XD, TC, 0, 96), [("s_v", v_xd)])
        wait("vector", s_s, s_mdt)
        v_u = emv(lambda v: v.scalar_tensor_tensor(
            A(o_U, MEGA), A(o_A4, MEGA), -1.0, A(o_A2, MEGA),
            ALU.mult, ALU.mult).then_inc(s_v, 1))
        for db in range(NDB):
            emv(lambda v, d=A(o_A8 + db * TC, TC), s=A(o_A2 + db * TC, TC),
                w=A(o_dv + db, 1):
                v.tensor_scalar_mul(d, s, w).then_inc(s_v, 1))
        v_yacc0 = P.val("s_v")

        t_rep_hist = {}
        d_bc_hist = {}

        def stage_bc(k):
            bcb2 = o_WDT + (k % 2) * 512
            sdma(A(o_XD, TC, 64 + k, 1), A(bcb2, TC, 64, 1), "s_b" + str(k % 2))
            d_bc_hist[k] = sdma(A(o_XD, TC, 80 + k, 1), A(bcb2 + TC, TC, 64, 1),
                                "s_b" + str(k % 2))

        wait("sync", s_v, v_xd)
        stage_bc(0)
        HM = MEGA // 2
        if c + 1 < NCH:
            wait("gpsimd", s_t, t_wsl_done[(c, 7)])
            g_xtb[(c + 1) % 2] = gdma(
                xT_r[:, :, (c + 1) * TC:(c + 2) * TC],
                Ab(o_XTB + ((c + 1) % 2) * 1024, 2048)
                .rearrange("p (blk t) -> p blk t", t=TC),
                "s_gx" + str((c + 1) % 2))
        for k in range(DS):
            par = k % 2
            pbB, pbC = PB[2 * par + 2], PB[2 * par + 3] if False else PB[2 * par + 3]
            pbB = PB[2 * par + 2]
            pbC = PB[2 * par + 3]
            if k + 1 < DS:
                wait("sync", s_t, t_rep_hist.get(k - 1, 0))
                stage_bc(k + 1)
            bcb = o_WDT + par * 512
            wait("tensor", s_b[par], d_bc_hist[k])
            if k < 2:
                t_wait_bank(2 * par + 2)
                t_wait_bank(2 * par + 3)
            pwar = v_chat_hist.get((c, k - 2), v_chat_hist.get((c - 1, DS - 1), 0))
            if pwar:
                wait("tensor", s_v, pwar)
            t_rep = tmm_group([
                lambda t, r=A(bcb, TC, 64, 1), o1=A(o_RP, 128, 64, 1), pb=pbB:
                t.matmul(pb[:, 0:TC], o1, r, start=True, stop=True),
                lambda t, r=A(bcb + TC, TC, 64, 1), o1=A(o_RP, 128, 64, 1), pb=pbC:
                t.matmul(pb[:, 0:TC], o1, r, start=True, stop=True)])
            t_rep_hist[k] = t_rep

            s_ah = [0, 0]
            for hf2 in range(2):
                aw = v_scan_half.get((c, k - 1, hf2),
                                     v_scan_half.get((c - 1, DS - 1, hf2), 0))
                if aw:
                    wait("scalar", s_v, aw)
                wait("scalar", s_v, p_acol_v)
                s_ah[hf2] = ems(lambda s, sc=A(o_ac + k, 1),
                                d=A(o_A1 + hf2 * HM, HM),
                                src=A(o_A4 + hf2 * HM, HM):
                                s.activation(d, src, AF.Exp,
                                             scale=sc).then_inc(s_s, 1))
            s_a_done[(c, k)] = s_ah[1]

            bb = pbB[:, 0:TC].unsqueeze(1).broadcast_to([128, NDB, TC])
            uu = A(o_U, MEGA).rearrange("p (blk t) -> p blk t", t=TC)
            ww = A(o_A2, MEGA).rearrange("p (blk t) -> p blk t", t=TC)
            wait("vector", s_t, t_rep)
            emv(lambda v, w=ww, u=uu, b2=bb:
                v.tensor_tensor(w, u, b2, ALU.mult).then_inc(s_v, 1))

            for db in range(NDB):
                if db == 0:
                    wait("vector", s_s, s_ah[0])
                if db == NDB // 2:
                    wait("vector", s_s, s_ah[1])
                emv(lambda v, o=A(o_A5 + db * TC, TC),
                    a=A(o_A1 + db * TC, TC),
                    w=A(o_A2 + db * TC, TC), i=A(o_ST + k * 16 + db, 1):
                    v.tensor_tensor_scan(o, a, w, i, ALU.mult, ALU.add)
                    .then_inc(s_v, 1))
                if db == NDB // 2 - 1:
                    v_scan_half[(c, k, 0)] = P.val("s_v")
            v_scan_half[(c, k, 1)] = P.val("s_v")
            if c == 0 and k == 0:
                dump("d_a0", A(o_A1, MEGA), [("s_s", s_ah[1])])
                dump("d_w0", A(o_A2, MEGA), [("s_v", P.val("s_v"))])
                dump("d_h0", A(o_A5, MEGA), [("s_v", P.val("s_v"))])
            lastc = A(o_A5, MEGA).rearrange("p (blk t) -> p blk t", t=TC)[:, :, TC - 1:TC]
            v_scan_done[(c, k)] = emv(
                lambda v, lc=lastc, d=A(o_ST + k * 16, 16).unsqueeze(2):
                v.tensor_copy(d, lc).then_inc(s_v, 1))

            cb2 = pbC[:, 0:TC].unsqueeze(1).broadcast_to([128, NDB, TC])
            hh = A(o_A5, MEGA).rearrange("p (blk t) -> p blk t", t=TC)
            v_chat = emv(lambda v, o=hh, h=hh, c2=cb2:
                         v.tensor_tensor(o, h, c2, ALU.mult).then_inc(s_v, 1))
            v_chat_hist[(c, k)] = v_chat
            bank_free[2 * par + 2] = ("s_v", v_chat)
            bank_free[2 * par + 3] = ("s_v", v_chat)
            v_acc = emv(lambda v:
                        v.tensor_tensor(A(o_A8, MEGA), A(o_A8, MEGA),
                                        A(o_A5, MEGA), ALU.add).then_inc(s_v, 1))
            v_acc_done[(c, k)] = ("v", v_acc)

            if c + 1 < NCH:
                if k % 4 == 0:
                    emit_xz_slice_load(c + 1, k // 4)
                t_mmp, bip = emit_xz_group_mm(c + 1, k)
                emit_xr_evac(c + 1, k, t_mmp, bip)



        if c == 0:
            dump("d_yacc", A(o_A8, MEGA), [("s_v", P.val("s_v"))])
        wait("vector", s_s, max(s_sz_done))
        wait("vector", s_t, t_wx)
        v_yz = emv(lambda v: v.tensor_tensor(Ab(o_B1, MEGA), A(o_A8, MEGA),
                                             Ab(o_SZ, MEGA), ALU.mult)
                   .then_inc(s_v, 1))
        v_yz_prev = v_yz
        v_yz_by_chunk[c] = v_yz

        wait("tensor", s_v, v_yz)
        for bi in (4, 5, 6, 7):
            t_wait_bank(bi)
        t_y = {}
        for dib in range(NDB):
            tprev = t_wos_done.get((c, dib - 2),
                                   t_wos_done.get((c - 1, dib + 14), 0))
            wait("gpsimd", s_t, tprev)
            g_wo = gdma(woutT_r[:, dib, :], Ab(o_WOS + (dib % 2) * 512, 1024),
                        "s_go" + str(dib % 2))
            wait("tensor", s_go[dib % 2], g_wo)
            mms = []
            for tb in range(2):
                for hf in range(2):
                    bi = 4 + tb * 2 + hf
                    lhsT = Ab(o_B1, MEGA)[:, dib * TC + tb * 128:
                                          dib * TC + tb * 128 + 128]
                    rhs = Ab(o_WOS + (dib % 2) * 512, 1024)[:, hf * 512:(hf + 1) * 512]
                    mms.append(lambda t, b=bi, l=lhsT, r=rhs, st=(dib == 0),
                               sp=(dib == NDB - 1):
                               t.matmul(PB[b][:, :], l, r, start=st, stop=sp))
            t_wos_done[(c, dib)] = tmm_group(mms)
        t_ylast = P.val("s_t")

        rs_cols = o_MISC
        v_ev_by_bank = {}
        for tb in range(2):
            for hf in range(2):
                bi = 4 + tb * 2 + hf
                wait("vector", s_t, t_ylast)
                wait("vector", s_d, d_xn)
                v_ev = emv(lambda v, d=A(o_R + tb * DM + hf * 512, 512),
                           p=PB[bi][:, :], x=A(o_XN + tb * DM + hf * 512, 512),
                           rc=A(rs_cols + tb * 2 + hf, 1):
                           v.scalar_tensor_tensor(d, p, 1.0, x, ALU.mult,
                                                  ALU.add, accum_out=rc)
                           .then_inc(s_v, 1))
                bank_free[bi] = ("s_v", v_ev)

        o_stat = o_MISC + 8
        for tb in range(2):
            mcol = A(o_stat + tb * 4 + 0, 1)
            vcol = A(o_stat + tb * 4 + 1, 1)
            rcol = A(o_stat + tb * 4 + 2, 1)
            ncol = A(o_stat + tb * 4 + 3, 1)
            tcol = A(o_MISC + 16, 1)
            emv(lambda v, m=mcol, a=A(rs_cols + tb * 2, 1),
                b2=A(rs_cols + tb * 2 + 1, 1):
                v.tensor_tensor(m, a, b2, ALU.add).then_inc(s_v, 1))
            v_m = emv(lambda v, m=mcol: v.tensor_scalar_mul(m, m, 1.0 / DM)
                      .then_inc(s_v, 1))
            wait("scalar", s_v, v_m)
            s_sq = ems(lambda s, jk=A(o_A5 + tb * DM, DM),
                       r=A(o_R + tb * DM, DM), q=vcol:
                       s.activation(jk, r, AF.Square, scale=1.0 / (DM ** 0.5),
                                    accum_out=q).then_inc(s_s, 1))
            wait("vector", s_s, s_sq)
            emv(lambda v, t2=tcol, m=mcol:
                v.tensor_tensor(t2, m, m, ALU.mult).then_inc(s_v, 1))
            v_var = emv(lambda v, q=vcol, t2=tcol:
                        v.tensor_tensor(q, q, t2, ALU.subtract).then_inc(s_v, 1))
            wait("scalar", s_v, v_var)
            ems(lambda s, q=vcol: s.activation(q, q, AF.Ln, bias=A(o_MISC + 48, 1))
                .then_inc(s_s, 1))
            s_rsc = ems(lambda s, q=vcol, r=rcol:
                        s.activation(r, q, AF.Exp, scale=-0.5).then_inc(s_s, 1))
            wait("vector", s_s, s_rsc)
            emv(lambda v, n=ncol, m=mcol, r=rcol:
                v.tensor_scalar(n, m, r, -1.0, ALU.mult, ALU.mult).then_inc(s_v, 1))
            emv(lambda v, d=A(o_XN + tb * DM, DM), r=A(o_R + tb * DM, DM),
                rc=rcol, n=ncol:
                v.tensor_scalar(d, r, rc, n, ALU.mult, ALU.add).then_inc(s_v, 1))
        v_ln1_done = P.val("s_v")

        wait("sync", s_v, v_ln1_done)
        d_agin_prev = sdma(A(o_XN, 2 * DM).rearrange("p (tb m) -> p tb m", tb=2),
                           agin[tc0:tc0 + TC, :]
                           .rearrange("(tb p) m -> p tb m", p=128))
        wait("gpsimd", s_d, d_agin_prev)
        em("gpsimd", lambda g, c2=c: g.collective_compute(
            "AllGather", ALU.bypass,
            replica_groups=[list(range(NCORE))],
            ins=[agin[c2 * TC:(c2 + 1) * TC, :].opt()],
            outs=[agall[c2, :, :].opt()]).then_inc(s_c, 1))
        cc_round[c] = P.inc("s_c", 1)

        if c + 1 < NCH:
            for sl in range(4, 8):
                emit_xz_slice_load(c + 1, sl)
                for db in range(4):
                    t_mmz, biz = emit_xz_group_mm(c + 1, sl * 4 + db)
                    emit_z_evac(c + 1, sl * 4 + db, t_mmz, biz)

        if c == 0:
            dump("d_ln1", A(o_XN, 2 * DM), [("s_v", v_ln1_done)])
            if DEBUG:
                wait("sync", s_c, cc_round[0])
                sdma(agall[0, :, :], dbg["d_ag0"], "s_o")

    v_md = P.val("s_v")
    s_md = P.val("s_s")
    t_md = P.val("s_t")
    wait("gpsimd", s_v, v_md)
    wait("gpsimd", s_s, s_md)
    wait("gpsimd", s_t, t_md)

    g_wg = gdma(wgT.rearrange("(blk p) o -> p blk o", p=128),
                Ab(f_WG, 8 * G3).rearrange("p (blk o) -> p blk o", o=G3))
    g_ug = gdma(ugT.rearrange("(blk p) o -> p blk o", p=128),
                Ab(f_UG, 8 * G3).rearrange("p (blk o) -> p blk o", o=G3))

    wait("vector", s_c, cc_round[NCH - 1])
    wait("vector", s_s, s_md)
    wait("vector", s_t, t_md)
    for mb in range(8):
        emv(lambda v, d=A(f_C + mb * TC, TC), i=INC[:, mb:mb + 1]:
            v.tensor_scalar(d, A(o_ST, TC), 0.0, i, ALU.mult, ALU.add)
            .then_inc(s_v, 1))
    v_cup = emv(lambda v: v.tensor_copy(Ab(f_CB, 2048), A(f_C, 2048))
                .then_inc(s_v, 1))

    agall_r = agall.rearrange("r (j2 tb p) m -> r j2 p tb m", tb=2, p=128)
    pid_holder = {}

    def get_pid(e):
        if "v" not in pid_holder:
            pid_holder["v"] = e.partition_id()
        return pid_holder["v"]

    wait("sync", s_c, cc_round[NCH - 1])

    def load_ag(j, dst_off):
        def dof(e, j=j, dst_off=dst_off):
            pid = get_pid(e)
            e.dma_start(out=A(dst_off, 2 * DM).rearrange("p (tb m) -> p tb m",
                                                         tb=2),
                        in_=agall_r[pid, j, :, :, :]).then_inc(s_d, 16)
        em("sync", dof)
        return P.inc("s_d", 16)

    t_xp = {}
    s_xp = {}
    d_land = {}

    def emit_tr_step(j, i):
        tb, mb = i // 8, i % 8
        bi = 2 if i % 2 == 0 else 7
        t_wait_bank(bi)
        if i == 0:
            wait("tensor", s_d, d_land[j])
        src = A(f_LAND + tb * DM + mb * 128, 128)
        t_tr = tmm_group([lambda t, s2=src, b=bi:
                          t.transpose(PB[b][:, 0:128], s2, A(o_EYE, 128))])
        wait("scalar", s_t, t_tr)
        dstb = Ab(f_XTT + j * 1024, 2048)[
            :, mb * TC + tb * 128: mb * TC + tb * 128 + 128]
        s_cp = ems(lambda s, d=dstb, b=bi:
                   s.copy(d, PB[b][:, 0:128]).then_inc(s_s, 1))
        bank_free[bi] = ("s_s", s_cp)
        if i == 15:
            t_xp[j] = t_tr
            s_xp[j] = s_cp

    d_land[0] = load_ag(0, f_LAND)
    for i in range(16):
        emit_tr_step(0, i)

    v_cstep = {}
    for j in range(B):
        if j + 1 < B:
            wait("sync", s_t, t_xp[j])
            d_land[j + 1] = load_ag(j + 1, f_LAND)

        wait("tensor", s_g, g_ug)
        wait("tensor", s_s, s_xp[j])
        wait("tensor", s_v, v_cup if j == 0 else v_cstep[j - 1])
        cb_r = f_CB + (j % 2) * 1024
        cb_w = f_CB + ((j + 1) % 2) * 1024
        GB_BANKS = (0, 1, 3, 4)
        seq = 0
        for mb8 in range(8):
            s_act3 = 0
            for kind in range(3):
                gb = kind * 8 + mb8
                bi = GB_BANKS[seq % 4]
                t_wait_bank(bi)
                mms = []
                for mb in range(8):
                    lhsT = Ab(f_WG, 8 * G3)[:, mb * G3 + gb * 128: mb * G3 + (gb + 1) * 128]
                    rhs = Ab(f_XTT + j * 1024, 2048)[:, mb * TC:(mb + 1) * TC]
                    mms.append(lambda t, b=bi, l=lhsT, r=rhs, st=(mb == 0):
                               t.matmul(PB[b][:, 0:TC], l, r, start=st, stop=False))
                for mb in range(8):
                    lhsT = Ab(f_UG, 8 * G3)[:, mb * G3 + gb * 128: mb * G3 + (gb + 1) * 128]
                    rhs = Ab(cb_r, 2048)[:, mb * TC:(mb + 1) * TC]
                    mms.append(lambda t, b=bi, l=lhsT, r=rhs, sp=(mb == 7):
                               t.matmul(PB[b][:, 0:TC], l, r, start=False, stop=sp))
                t_g = tmm_group(mms)
                dst = [f_F, f_I, f_G][kind] + mb8 * TC
                func = AF.Tanh if kind == 2 else AF.Sigmoid
                wait("scalar", s_t, t_g)
                s_act3 = ems(lambda s, b=bi, d=A(dst, TC),
                             bb2=BGC[:, gb:gb + 1], fn=func:
                             s.activation(d, PB[b][:, 0:TC], fn, bias=bb2)
                             .then_inc(s_s, 1))
                bank_free[bi] = ("s_s", s_act3)
                if j + 1 < B and 1 <= seq <= 16:
                    emit_tr_step(j + 1, seq - 1)
                seq += 1
            wait("vector", s_s, s_act3)
            fsl = A(f_F + mb8 * TC, TC)
            isl = A(f_I + mb8 * TC, TC)
            gsl = A(f_G + mb8 * TC, TC)
            csl = A(f_C + mb8 * TC, TC)
            emv(lambda v, f=fsl, c2=csl: v.tensor_tensor(f, f, c2, ALU.mult)
                .then_inc(s_v, 1))
            emv(lambda v, i=isl, g2=gsl: v.tensor_tensor(g2, i, g2, ALU.mult)
                .then_inc(s_v, 1))
            emv(lambda v, c2=csl, f=fsl, g2=gsl:
                v.tensor_tensor(c2, f, g2, ALU.add).then_inc(s_v, 1))
            v_cstep[j] = emv(
                lambda v, d=Ab(cb_w, 2048)[:, mb8 * TC:(mb8 + 1) * TC], s2=csl:
                v.tensor_copy(d, s2).then_inc(s_v, 1))
        if j == 0:
            dump("d_c0", A(f_C, 2048), [("s_v", v_cstep[j])])

    wait("gpsimd", s_s, P.val("s_s"))
    wait("gpsimd", s_v, P.val("s_v"))
    wait("gpsimd", s_t, t_xp[B - 1])
    g_wo2 = gdma(woT.rearrange("(blk p) o -> p blk o", p=128),
                 Ab(f_WO, 8 * DM).rearrange("p (blk o) -> p blk o", o=DM),
                 "s_gw0")
    g_uo2 = gdma(uoT.rearrange("(blk p) o -> p blk o", p=128),
                 Ab(f_UO, 8 * DM).rearrange("p (blk o) -> p blk o", o=DM),
                 "s_gw1")

    v_cF = v_cstep[B - 1]
    wait("tensor", s_gw[1], g_uo2)
    wait("tensor", s_v, v_cF)
    s_hoist = 0
    for pb2 in range(2):
        for hf in range(2):
            bi = 5 + hf
            t_wait_bank(bi)
            mms = []
            for mb in range(8):
                lhsT = Ab(f_CB, 2048)[:, mb * TC + pb2 * 128: mb * TC + pb2 * 128 + 128]
                rhs = Ab(f_UO, 8 * DM)[:, mb * DM + hf * 512: mb * DM + (hf + 1) * 512]
                mms.append(lambda t, b=bi, l=lhsT, r=rhs, st=(mb == 0):
                           t.matmul(PB[b][:, :], l, r, start=st, stop=False))
            mms.append(lambda t, b=bi, h=hf:
                       t.matmul(PB[b][:, :], ONE[:, 0:128],
                                BOR[:, h * 512:(h + 1) * 512],
                                start=False, stop=True))
            t_uo = tmm_group(mms)
            wait("scalar", s_t, t_uo)
            s_hoist = ems(lambda s, b=bi,
                          d=A(f_UOX + pb2 * DM + hf * 512, 512):
                          s.copy(d, PB[b][:, :]).then_inc(s_s, 1))
            bank_free[bi] = ("s_s", s_hoist)

    v_ln2 = 0
    d_out_prev = 0
    v_first = True
    for j in range(B):
        if j == 0:
            wait("sync", s_v, v_cF)
        else:
            wait("sync", s_o, d_out_prev)
        d_x = load_ag(j, f_X32)

        s_tanh_last = 0
        for pb2 in range(2):
            for hf in range(2):
                bi = (5, 6, 2, 7)[pb2 * 2 + hf]
                t_wait_bank(bi)
                if j == 0 and pb2 == 0 and hf == 0:
                    wait("tensor", s_gw[0], g_wo2)
                    wait("tensor", s_s, s_xp[B - 1])
                mms = []
                for mb in range(8):
                    lhsT = Ab(f_XTT + j * 1024, 2048)[
                        :, mb * TC + pb2 * 128: mb * TC + pb2 * 128 + 128]
                    rhs = Ab(f_WO, 8 * DM)[:, mb * DM + hf * 512: mb * DM + (hf + 1) * 512]
                    mms.append(lambda t, b=bi, l=lhsT, r=rhs, st=(mb == 0),
                               sp=(mb == 7):
                               t.matmul(PB[b][:, :], l, r, start=st, stop=sp))
                t_h = tmm_group(mms)
                wait("vector", s_t, t_h)
                if v_first:
                    wait("vector", s_s, s_hoist)
                    v_first = False
                v_o = emv(lambda v, b=bi,
                          d=A(f_H + pb2 * DM + hf * 512, 512),
                          u=A(f_UOX + pb2 * DM + hf * 512, 512):
                          v.scalar_tensor_tensor(d, PB[b][:, :], 1.0, u,
                                                 ALU.mult, ALU.add)
                          .then_inc(s_v, 1))
                bank_free[bi] = ("s_v", v_o)
                wait("scalar", s_v, v_o)
                s_tanh_last = ems(lambda s,
                                  d=A(f_H + pb2 * DM + hf * 512, 512):
                                  s.activation(d, d, AF.Tanh).then_inc(s_s, 1))

        if j == 0:
            dump("d_hf0", A(f_H, 2048), [("s_s", s_tanh_last)])
        wait("vector", s_s, s_tanh_last)
        wait("vector", s_d, d_x)
        rs2 = o_MISC + 24
        for tb in range(2):
            for hf in range(2):
                emv(lambda v, d=A(f_H + tb * DM + hf * 512, 512),
                    x=A(f_X32 + tb * DM + hf * 512, 512),
                    rc=A(rs2 + tb * 2 + hf, 1):
                    v.scalar_tensor_tensor(d, d, 1.0, x, ALU.mult, ALU.add,
                                           accum_out=rc).then_inc(s_v, 1))
        o_st2 = o_MISC + 32
        for tb in range(2):
            mcol = A(o_st2 + tb * 4 + 0, 1)
            vcol = A(o_st2 + tb * 4 + 1, 1)
            rcol = A(o_st2 + tb * 4 + 2, 1)
            ncol = A(o_st2 + tb * 4 + 3, 1)
            tcol = A(o_MISC + 40, 1)
            emv(lambda v, m=mcol, a=A(rs2 + tb * 2, 1), b2=A(rs2 + tb * 2 + 1, 1):
                v.tensor_tensor(m, a, b2, ALU.add).then_inc(s_v, 1))
            v_m = emv(lambda v, m=mcol: v.tensor_scalar_mul(m, m, 1.0 / DM)
                      .then_inc(s_v, 1))
            wait("scalar", s_v, v_m)
            s_sq = ems(lambda s, jk=A(f_F + tb * DM, DM), r=A(f_H + tb * DM, DM),
                       q=vcol:
                       s.activation(jk, r, AF.Square, scale=1.0 / (DM ** 0.5),
                                    accum_out=q).then_inc(s_s, 1))
            wait("vector", s_s, s_sq)
            emv(lambda v, t2=tcol, m=mcol: v.tensor_tensor(t2, m, m, ALU.mult)
                .then_inc(s_v, 1))
            v_var = emv(lambda v, q=vcol, t2=tcol:
                        v.tensor_tensor(q, q, t2, ALU.subtract).then_inc(s_v, 1))
            wait("scalar", s_v, v_var)
            ems(lambda s, q=vcol: s.activation(q, q, AF.Ln, bias=A(o_MISC + 48, 1))
                .then_inc(s_s, 1))
            s_rsc = ems(lambda s, q=vcol, r=rcol:
                        s.activation(r, q, AF.Exp, scale=-0.5).then_inc(s_s, 1))
            wait("vector", s_s, s_rsc)
            emv(lambda v, n=ncol, m=mcol, r=rcol:
                v.tensor_scalar(n, m, r, -1.0, ALU.mult, ALU.mult).then_inc(s_v, 1))
            emv(lambda v, d=A(f_F + tb * DM, DM), r=A(f_H + tb * DM, DM),
                rc=rcol, n=ncol:
                v.tensor_scalar(d, r, rc, n, ALU.mult, ALU.add).then_inc(s_v, 1))
        v_ln2 = P.val("s_v")

        wait("sync", s_v, v_ln2)
        d_out_prev = sdma(A(f_F, 2 * DM).rearrange("p (tb m) -> p tb m", tb=2),
                          out[j, :, :].rearrange("(tb p) m -> p tb m", p=128),
                          "s_o")

    wait("sync", s_d, P.val("s_d"))
    wait("sync", s_o, P.val("s_o"))

    @block.gpsimd
    def _(g):
        for fn in prog["gpsimd"]:
            fn(g)

    @block.sync
    def _(e):
        for fn in prog["sync"]:
            fn(e)

    @block.vector
    def _(v):
        for fn in prog["vector"]:
            fn(v)

    @block.scalar
    def _(s):
        for fn in prog["scalar"]:
            fn(s)

    @block.tensor
    def _(t):
        for fn in prog["tensor"]:
            fn(t)

    es.close()
    return nc


_NC_CACHE = {}


def kernel(L0_attn_x, L0_attn_W_in, L0_attn_conv_w, L0_attn_conv_b, L0_attn_A_log,
           L0_attn_D, L0_attn_W_x, L0_attn_W_dt, L0_attn_b_dt, L0_attn_W_out,
           L0_ffn_wf, L0_ffn_uf, L0_ffn_bf, L0_ffn_wi, L0_ffn_ui, L0_ffn_bi,
           L0_ffn_wc, L0_ffn_uc, L0_ffn_bc, L0_ffn_wo, L0_ffn_uo, L0_ffn_bo,
           L0_ffn_in_c):
    x = np.asarray(L0_attn_x, dtype=np.float32)

    def T(a):
        return np.ascontiguousarray(np.asarray(a, dtype=np.float32).T)

    def colv(a):
        v = np.asarray(a, dtype=np.float32).reshape(-1)
        return np.ascontiguousarray(v.reshape(-1, 128).T)

    shared = {
        "winT": T(L0_attn_W_in),
        "wxT": T(L0_attn_W_x),
        "wdtT": T(L0_attn_W_dt),
        "woutT": T(L0_attn_W_out),
        "convw": np.ascontiguousarray(
            np.asarray(L0_attn_conv_w, dtype=np.float32)
            .reshape(16, 128, 4).transpose(1, 0, 2).reshape(128, 64)),
        "convb": colv(L0_attn_conv_b),
        "bdt": colv(L0_attn_b_dt),
        "alog": np.ascontiguousarray(
            np.asarray(L0_attn_A_log, dtype=np.float32).reshape(1, -1)),
        "dvec": colv(L0_attn_D),
        "wgT": np.ascontiguousarray(np.concatenate(
            [np.asarray(L0_ffn_wf), np.asarray(L0_ffn_wi),
             np.asarray(L0_ffn_wc)], axis=0).T.astype(np.float32)),
        "ugT": np.ascontiguousarray(np.concatenate(
            [np.asarray(L0_ffn_uf), np.asarray(L0_ffn_ui),
             np.asarray(L0_ffn_uc)], axis=0).T.astype(np.float32)),
        "woT": T(L0_ffn_wo),
        "uoT": T(L0_ffn_uo),
        "bg": colv(np.concatenate(
            [np.asarray(L0_ffn_bf), np.asarray(L0_ffn_bi),
             np.asarray(L0_ffn_bc)], axis=0)),
        "bo": np.ascontiguousarray(
            np.asarray(L0_ffn_bo, dtype=np.float32).reshape(1, -1)),
        "incv": colv(L0_ffn_in_c),
        "eye": np.eye(128, dtype=np.float32),
    }

    in_maps = []
    for j in range(NCORE):
        m = dict(shared)
        m["xT"] = np.ascontiguousarray(x[j].T)
        m["xn"] = np.ascontiguousarray(x[j])
        in_maps.append(m)

    if "nc" not in _NC_CACHE:
        _NC_CACHE["nc"] = build_nc()
    nc = _NC_CACHE["nc"]

    res = run_bass_kernel_spmd(nc, in_maps, core_ids=list(range(NCORE)))

    full = np.empty((B, L, DM), dtype=np.float32)
    for i in range(NCORE):
        full[:, i * POS:(i + 1) * POS, :] = res.results[i]["out"]
    return full
